# version 05
# baseline (speedup 1.0000x reference)
import numpy as np
import ml_dtypes
from contextlib import ExitStack
import concourse.bass as bass
import concourse.mybir as mybir
from concourse.bass_utils import run_bass_kernel_spmd

F32 = mybir.dt.float32
BF16 = mybir.dt.bfloat16
AF = mybir.ActivationFunctionType
OP = mybir.AluOpType

SEQ = 4096
D = 1024
NDC = 8
DEPTH = 6
NEG = -1e30
PH_A = 512
PH_F = 1024
KC = 22
NDMASEM = 8
ATTN_CUT = 99

PERM64 = np.concatenate([np.arange(0, 16), np.arange(32, 48),
                         np.arange(16, 32), np.arange(48, 64)])


class _Op:
    __slots__ = ("idx", "stream", "emit", "reads", "writes", "dma",
                 "deps", "sdeps", "waits", "ticket", "marked", "tag",
                 "needs_drain", "slot")


class Prog:
    STREAMS = ("te", "ve", "se", "gp", "sq")

    def __init__(self):
        self.ops = []
        self.bars = []
        self.preset = set()

    def add(self, stream, emit, reads=(), writes=(), dma=False, tag=""):
        o = _Op()
        o.idx = len(self.ops)
        o.stream = stream
        o.emit = emit
        o.reads = tuple(reads)
        writes = tuple(writes)
        if stream in ("ve", "se"):
            for k in o.reads:
                if isinstance(k, tuple) and len(k) == 2 and k[0] == "bank":
                    writes = writes + (("bankrd", k[1]),)
        o.writes = tuple(writes)
        o.dma = dma
        o.tag = tag
        o.sdeps = []
        o.needs_drain = False
        o.slot = 0
        self.ops.append(o)
        return o

    def barrier(self):
        self.bars.append(len(self.ops))

    def analyze(self):
        last_w = {k: None for k in self.preset}
        readers = {}
        for o in self.ops:
            deps = set()
            for k in o.reads:
                if k not in last_w:
                    raise RuntimeError(f"read-before-write {k} op#{o.idx} {o.tag}")
                w = last_w[k]
                if w is not None:
                    deps.add(w)
                rd = readers.setdefault(k, {})
                if o.dma:
                    rd.setdefault("_dma", []).append(o)
                else:
                    rd[o.stream] = o
            for k in o.writes:
                pseudo = isinstance(k, tuple) and len(k) == 2 and k[0] == "bankrd"
                w = last_w.get(k)
                if w is not None and not (pseudo and w.stream == o.stream):
                    deps.add(w)
                for sk, r in readers.get(k, {}).items():
                    if sk == "_dma":
                        deps.update(r)
                    else:
                        deps.add(r)
                last_w[k] = o
                readers[k] = {}
            deps.discard(o)
            o.deps = [d for d in deps if d.stream != o.stream or d.dma]
            o.sdeps = [d for d in deps if d.stream == o.stream and not d.dma
                       and o.stream in ("ve", "se", "gp")]
        for bidx in self.bars:
            lasts = {}
            for o in self.ops[:bidx]:
                lasts[o.stream] = o
            firsts = {}
            for o in self.ops[bidx:]:
                if o.stream not in firsts:
                    firsts[o.stream] = o
            for s, o in firsts.items():
                for s2, d in lasts.items():
                    if s2 != s:
                        o.deps.append(d)
        for o in self.ops:
            o.marked = o.dma
        for o in self.ops:
            for d in o.deps:
                d.marked = True
            for d in o.sdeps:
                d.marked = True
        cnt = {}
        for o in self.ops:
            o.ticket = None
            if o.marked:
                if o.dma:
                    key = (o.stream, True)
                    n = cnt.get(key, 0)
                    cnt[key] = n + 1
                    o.slot = n % NDMASEM
                    o.ticket = (n // NDMASEM + 1) * 16
                else:
                    key = (o.stream, False)
                    cnt[key] = cnt.get(key, 0) + 1
                    o.slot = 0
                    o.ticket = cnt[key]
        hw = {s: {} for s in self.STREAMS}
        for o in self.ops:
            need = {}
            for d in o.deps:
                sk = (d.stream, d.dma, d.slot)
                if d.ticket > need.get(sk, 0):
                    need[sk] = d.ticket
            for d in o.sdeps:
                sk = (d.stream, d.dma, d.slot)
                if d.ticket > need.get(sk, 0):
                    need[sk] = d.ticket
            o.waits = []
            for sk in sorted(need, key=str):
                c = need[sk]
                if hw[o.stream].get(sk, 0) >= c:
                    continue
                hw[o.stream][sk] = c
                o.waits.append((sk, c))
        for o in self.ops:
            o.needs_drain = False

    def emit_all(self, block, sems):
        per = {s: [o for o in self.ops if o.stream == s] for s in self.STREAMS}
        sections = {"te": block.tensor, "ve": block.vector, "se": block.scalar,
                    "gp": block.gpsimd, "sq": block.sync}

        def mk(stream):
            def body(eng):
                for o in per[stream]:
                    if o.needs_drain:
                        eng.drain()
                    for sk, c in o.waits:
                        eng.wait_ge(sems[sk], c)
                    if o.dma and o.marked and o.ticket > 16:
                        eng.wait_ge(sems[(o.stream, True, o.slot)], o.ticket - 16)
                    instr = o.emit(eng)
                    if o.marked:
                        assert instr is not None, f"op {o.tag} marked, emit->None"
                        instr.then_inc(sems[(o.stream, o.dma, o.slot)],
                                       16 if o.dma else 1)
            return body

        for s in self.STREAMS:
            if per[s]:
                sections[s](mk(s))

    def stats(self):
        by = {}
        for o in self.ops:
            by[o.stream] = by.get(o.stream, 0) + 1
        return (f"ops={len(self.ops)} marked={sum(o.marked for o in self.ops)} "
                f"waits={sum(len(o.waits) for o in self.ops)} by={by}")


def build_program(stage=99, dump=None):
    nc = bass.Bass("TRN2", target_bir_lowering=False)
    P = Prog()
    te = lambda emit, r, w, tag="": P.add("te", emit, r, w, tag=tag)
    ve = lambda emit, r, w, tag="": P.add("ve", emit, r, w, tag=tag)
    se = lambda emit, r, w, tag="": P.add("se", emit, r, w, tag=tag)
    gp = lambda emit, r, w, tag="": P.add("gp", emit, r, w, tag=tag)

    def gdma(dst, src, r, w, tag=""):
        P.add("gp", lambda g: g.dma_start(dst, src), r, w, dma=True, tag=tag)

    def gdma_nc(dst, src, r, w, tag=""):
        P.add("gp", lambda g: g.dma_start(dst, src, allow_slow_non_contiguous=True),
              r, w, dma=True, tag=tag)

    def sdma(dst, src, r, w, tag=""):
        P.add("sq", lambda s: s.dma_start(dst, src), r, w, dma=True, tag=tag)

    dram = lambda name, shape, dt, kind="Internal": nc.dram_tensor(
        name, shape, dt, kind=kind).ap()
    xT = dram("xT", [D, SEQ], F32, "ExternalInput")
    aWq = dram("aWq", [128, NDC, 512], F32, "ExternalInput")
    aWk = dram("aWk", [128, NDC, 512], F32, "ExternalInput")
    aWv = dram("aWv", [128, NDC, 512], F32, "ExternalInput")
    aWo = dram("aWo", [NDC, 64, 8, 128], F32, "ExternalInput")
    fWqk = dram("fWqk", [DEPTH, 16, 128, NDC, 128], F32, "ExternalInput")
    fWv = dram("fWv", [DEPTH, 128, NDC, 1024], F32, "ExternalInput")
    fWo = dram("fWo", [DEPTH, NDC, 128, NDC, 128], F32, "ExternalInput")
    fW1a = dram("fW1a", [DEPTH, 6, 128, NDC, 512], F32, "ExternalInput")
    fW1g = dram("fW1g", [DEPTH, 6, 128, NDC, 512], F32, "ExternalInput")
    fW2 = dram("fW2", [DEPTH, NDC, 128, KC, 128], F32, "ExternalInput")
    fqs = dram("fqs", [DEPTH, 64], F32, "ExternalInput")
    fks = dram("fks", [DEPTH, 64], F32, "ExternalInput")
    fg = dram("fg", [DEPTH, D], F32, "ExternalInput")
    fb = dram("fb", [DEPTH, D], F32, "ExternalInput")
    fb1a = dram("fb1a", [DEPTH, 128 * KC], F32, "ExternalInput")
    fb1g = dram("fb1g", [DEPTH, 128 * KC], F32, "ExternalInput")
    fb2 = dram("fb2", [DEPTH, D], F32, "ExternalInput")
    outT = dram("outT", [D, SEQ], F32, "ExternalOutput")

    zH = dram("zH", [D, SEQ], F32)
    x1H = dram("x1H", [D, SEQ], F32)
    x2H = dram("x2H", [D, SEQ], F32)
    bWqk = dram("bWqk", [DEPTH, 16, 128, NDC, 128], BF16)
    bWv = dram("bWv", [DEPTH, 128, NDC, 1024], BF16)
    bWo = dram("bWo", [DEPTH, NDC, 128, NDC, 128], BF16)
    bW1a = dram("bW1a", [DEPTH, 6, 128, NDC, 512], BF16)
    bW1g = dram("bW1g", [DEPTH, 6, 128, NDC, 512], BF16)
    bW2 = dram("bW2", [DEPTH, NDC, 128, KC, 128], BF16)
    baWq = dram("baWq", [128, NDC, 512], BF16)
    baWk = dram("baWk", [128, NDC, 512], BF16)
    baWv = dram("baWv", [128, NDC, 512], BF16)
    baWo = dram("baWo", [NDC, 64, 8, 128], BF16)

    dumpH = None
    if dump is not None:
        dumpH = dram("dumpH", [D, SEQ], F32, "ExternalOutput")

    NCH = SEQ // PH_A
    P.preset |= {("hbm", "xT", i) for i in range(NCH)}

    def hk(name, t0, n):
        return [("hbm", name, i) for i in range(t0 // PH_A, (t0 + n + PH_A - 1) // PH_A)]

    es = ExitStack()
    A = lambda cm: es.enter_context(cm)
    sb = lambda name, shape, dt=F32: A(nc.sbuf_tensor(name, shape, dt))

    t_csq = sb("t_csq", [128, SEQ], BF16)
    t_snq = sb("t_snq", [128, SEQ], BF16)
    t_csk = sb("t_csk", [128, SEQ], BF16)
    t_snk = sb("t_snk", [128, SEQ], BF16)
    onescol = sb("onescol", [128, 1], BF16)
    onesrow = sb("onesrow", [128, 128], BF16)
    ident = sb("ident", [128, 128], BF16)
    mA1 = sb("mA1", [128, 128], BF16)
    mA2 = sb("mA2", [128, 128], BF16)
    sel16 = sb("sel16", [128, NDC, 16], BF16)
    vneg12 = sb("vneg12", [128, 1], F32)
    vneg16 = sb("vneg16", [128, 1], F32)
    ceps = sb("ceps", [1, 1], F32)
    cepsl2 = sb("cepsl2", [1, 1], F32)
    c16row = sb("c16row", [1, 64], BF16)
    p_qs = sb("p_qs", [128, 1], F32)
    p_ks = sb("p_ks", [128, 1], F32)
    p_g = sb("p_g", [128, NDC], F32)
    p_b = sb("p_b", [128, NDC], F32)
    p_b1a = sb("p_b1a", [128, KC], F32)
    p_b1g = sb("p_b1g", [128, KC], F32)
    p_b2 = sb("p_b2", [128, NDC], F32)
    zf = sb("zf", [128, NDC, PH_F + 1], F32)
    xs_bf = sb("xs_bf", [128, NDC, PH_F + 1], BF16)
    xsq_bf = sb("xsq_bf", [128, NDC, 513], BF16)
    carryb = sb("carryb", [128, NDC, 1], F32)
    r_mu = sb("r_mu", [1, 512], F32)
    r_s2 = sb("r_s2", [1, 512], F32)
    r_var = sb("r_var", [1, 512], F32)
    r_rstd = sb("r_rstd", [1, 512], BF16)
    r_mrs = sb("r_mrs", [1, 512], BF16)

    bank = [A(nc.psum_tensor(f"pb{i}", [128, 512], F32)) for i in range(8)]
    BK = [("bank", i) for i in range(8)]

    sems = {}
    for s in Prog.STREAMS:
        sems[(s, False, 0)] = A(nc.semaphore(f"sem_{s}"))
    for s in ("sq", "gp"):
        for i in range(NDMASEM):
            sems[(s, True, i)] = A(nc.semaphore(f"sem_{s}d{i}"))

    for (dst, src, n) in [(bWqk, fWqk, "wqk"), (bWv, fWv, "wv"), (bWo, fWo, "wo"),
                          (bW1a, fW1a, "w1a"), (bW1g, fW1g, "w1g"),
                          (bW2, fW2, "w2"), (baWq, aWq, "awq"),
                          (baWk, aWk, "awk"), (baWv, aWv, "awv"),
                          (baWo, aWo, "awo")]:
        gdma(dst, src, (), [("wc", n)], tag=f"cast_{n}")

    gp(lambda g: g.memset(onescol.ap(), 1.0), (), ["onescol"], "ones")
    gp(lambda g: g.memset(onesrow.ap(), 1.0), (), ["onesrow"], "onesr")
    gp(lambda g: g.memset(vneg12.ap(), -12.0), (), ["vneg12"], "v12")
    gp(lambda g: g.memset(vneg16.ap(), -16.0), (), ["vneg16"], "v16")
    gp(lambda g: g.memset(ceps.ap(), 1e-5), (), ["ceps"], "eps")
    gp(lambda g: g.memset(c16row.ap(), 16.0), (), ["c16row"], "c16")
    gp(lambda g: g.memset(cepsl2.ap(), 1e-16), (), ["cepsl2"], "eps2")

    tb = ExitStack()
    T = lambda n, s, d=F32: tb.enter_context(nc.sbuf_tensor(n, s, d))
    itr, itc, tmsk = T("itr", [128, 128]), T("itc", [128, 1]), T("tmsk", [128, 128])
    w_i, w_c, w_if, w_lb = T("w_i", [32, 1]), T("w_c", [32, 1]), T("w_if", [32, 1]), T("w_lb", [32, 1])
    w_fr, w_x, w_k = T("w_fr", [32, SEQ]), T("w_x", [32, SEQ]), T("w_k", [32, SEQ])
    w_sin, w_cos = T("w_sin", [32, SEQ]), T("w_cos", [32, SEQ])
    w_o1, w_o2 = T("w_o1", [32, SEQ], BF16), T("w_o2", [32, SEQ], BF16)

    gp(lambda g: g.iota(itr.ap(), [[1, 128]], base=0, channel_multiplier=0,
                        allow_small_or_imprecise_dtypes=True), (), ["itr"], "itr")
    gp(lambda g: g.iota(itc.ap(), [[1, 1]], base=0, channel_multiplier=1,
                        allow_small_or_imprecise_dtypes=True), (), ["itc"], "itc")
    ve(lambda v: v.tensor_scalar(ident.ap(), itr.ap(), itc.ap(), None, OP.is_equal),
       ["itr", "itc"], ["ident"], "ident")
    ve(lambda v: v.tensor_scalar(tmsk.ap(), itr.ap(), itc.ap(), None, OP.is_gt),
       ["itr", "itc"], ["tmsk"], "m1a")
    ve(lambda v: v.tensor_scalar(mA1.ap(), tmsk.ap(), NEG, None, OP.mult),
       ["tmsk"], ["mA1"], "m1b")
    ve(lambda v: v.tensor_scalar(tmsk.ap(), itr.ap(), itc.ap(), None, OP.is_lt),
       ["itr", "itc"], ["tmsk"], "m2a")
    ve(lambda v: v.tensor_scalar(mA2.ap(), tmsk.ap(), NEG, None, OP.mult),
       ["tmsk"], ["mA2"], "m2b")
    hcol = tb.enter_context(nc.sbuf_tensor("hcol", [128, 1], F32))
    rge = tb.enter_context(nc.sbuf_tensor("rge", [128, 1], F32))
    ve(lambda v: v.tensor_scalar(rge.ap(), itc.ap(), 64.0, None, OP.is_ge),
       ["itc"], ["rge"], "rge")
    for dc in range(NDC):
        def mh(dc=dc):
            return lambda v: v.tensor_scalar(hcol.ap(), rge.ap(), 1.0, float(2 * dc),
                                             OP.mult, OP.add)
        ve(mh(), ["rge"], ["hcol"], f"hc{dc}")
        def ms(dc=dc):
            return lambda v: v.tensor_scalar(sel16[:, dc, :], itr[:, 0:16],
                                             hcol.ap(), None, OP.is_equal)
        ve(ms(), ["hcol", "itr"], ["sel16"], f"s16_{dc}")

    gp(lambda g: g.iota(w_i.ap(), [[1, 1]], base=0, channel_multiplier=1,
                        allow_small_or_imprecise_dtypes=True), (), ["w_i"], "w_i")
    gp(lambda g: g.iota(w_fr.ap(), [[1, SEQ]], base=0, channel_multiplier=0,
                        allow_small_or_imprecise_dtypes=True), (), ["w_fr"], "w_t")
    gp(lambda g: g.memset(w_c.ap(), 10000.0), (), ["w_c"], "c4")
    se(lambda s: s.activation(w_c.ap(), w_c.ap(), AF.Ln), ["w_c"], ["w_c"], "ln4")
    ve(lambda v: v.tensor_scalar(w_if.ap(), w_i.ap(), -1.0 / 32.0, None, OP.mult),
       ["w_i"], ["w_if"], "nx")
    se(lambda s: s.activation(w_if.ap(), w_if.ap(), AF.Exp, scale=w_c.ap()),
       ["w_if", "w_c"], ["w_if"], "invf")
    ve(lambda v: v.tensor_scalar(w_fr.ap(), w_fr.ap(), w_if.ap(), None, OP.mult),
       ["w_fr", "w_if"], ["w_fr"], "fr")

    TWO23 = 8388608.0
    INV2PI = float(np.float32(1.0 / (2 * np.pi)))
    c2pi = 2 * np.pi
    C1 = float(np.float32(int(c2pi * 2**10) / 2**10))
    C2 = float(np.float32(c2pi - C1))
    C3 = float(c2pi - C1 - np.float64(C2))
    PI2 = float(np.float32(np.pi / 2))
    PI2L = float(np.pi / 2 - np.float64(PI2))

    def sin_of(out, shift, tg):
        xin, xk = w_fr, "w_fr"
        if shift:
            ve(lambda v: v.tensor_scalar(w_x.ap(), w_fr.ap(), PI2, PI2L, OP.add, OP.add),
               ["w_fr"], ["w_x"], tg + "x")
            xin, xk = w_x, "w_x"
        ve(lambda v: v.tensor_scalar(w_k.ap(), xin.ap(), INV2PI, TWO23, OP.mult, OP.add),
           [xk], ["w_k"], tg + "k1")
        ve(lambda v: v.tensor_scalar(w_k.ap(), w_k.ap(), TWO23, None, OP.subtract),
           ["w_k"], ["w_k"], tg + "k")
        ve(lambda v: v.scalar_tensor_tensor(out.ap(), w_k.ap(), -C1, xin.ap(),
                                            OP.mult, OP.add),
           ["w_k", xk], [tg + "out"], tg + "r1")
        ve(lambda v: v.scalar_tensor_tensor(out.ap(), w_k.ap(), -C2, out.ap(),
                                            OP.mult, OP.add),
           ["w_k", tg + "out"], [tg + "out"], tg + "r2")
        ve(lambda v: v.scalar_tensor_tensor(out.ap(), w_k.ap(), -C3, out.ap(),
                                            OP.mult, OP.add),
           ["w_k", tg + "out"], [tg + "out"], tg + "r3")
        T2 = 3.1415925
        TWOPI = float(2 * np.pi)
        ve(lambda v: v.tensor_scalar(w_k.ap(), out.ap(), T2, -TWOPI,
                                     OP.is_gt, OP.mult),
           [tg + "out", "w_k"], ["w_k"], tg + "wg")
        ve(lambda v: v.tensor_tensor(out.ap(), out.ap(), w_k.ap(), OP.add),
           [tg + "out", "w_k"], [tg + "out"], tg + "wga")
        ve(lambda v: v.tensor_scalar(w_k.ap(), out.ap(), -T2, TWOPI,
                                     OP.is_lt, OP.mult),
           [tg + "out", "w_k"], ["w_k"], tg + "wl")
        ve(lambda v: v.tensor_tensor(out.ap(), out.ap(), w_k.ap(), OP.add),
           [tg + "out", "w_k"], [tg + "out"], tg + "wla")
        ve(lambda v: v.tensor_scalar(out.ap(), out.ap(), -T2, T2,
                                     OP.max, OP.min),
           [tg + "out"], [tg + "out"], tg + "cl")
        se(lambda s: s.activation(out.ap(), out.ap(), AF.Sin),
           [tg + "out"], [tg + "out"], tg + "sin")

    sin_of(w_sin, False, "ws")
    sin_of(w_cos, True, "wc")
    A256 = float(np.float32(25.6))
    A256L = float(25.6 - np.float64(A256))
    B896 = float(np.float32(89.6))
    B896L = float(89.6 - np.float64(B896))
    ve(lambda v: v.tensor_scalar(w_lb.ap(), w_i.ap(), 2.0, A256, OP.mult, OP.add),
       ["w_i"], ["w_lb"], "b1")
    ve(lambda v: v.tensor_scalar(w_lb.ap(), w_lb.ap(), A256L, None, OP.add),
       ["w_lb"], ["w_lb"], "b2")
    gp(lambda g: g.memset(w_c.ap(), B896), ["w_c"], ["w_c"], "d1")
    ve(lambda v: v.tensor_scalar(w_c.ap(), w_c.ap(), B896L, None, OP.add),
       ["w_c"], ["w_c"], "d2")
    ve(lambda v: v.reciprocal(w_c.ap(), w_c.ap()), ["w_c"], ["w_c"], "d3")
    ve(lambda v: v.tensor_scalar(w_lb.ap(), w_lb.ap(), w_c.ap(), None, OP.mult),
       ["w_lb", "w_c"], ["w_lb"], "base")
    se(lambda s: s.activation(w_lb.ap(), w_lb.ap(), AF.Ln), ["w_lb"], ["w_lb"], "lnb")
    gp(lambda g: g.iota(w_x.ap(), [[1, SEQ]], base=0, channel_multiplier=0,
                        allow_small_or_imprecise_dtypes=True), ["w_x"], ["w_x"], "t2")
    ve(lambda v: v.tensor_scalar(w_x.ap(), w_x.ap(), -2048.0, 1.0 / 128.0,
                                 OP.add, OP.mult),
       ["w_x"], ["w_x"], "pw")
    ve(lambda v: v.tensor_scalar(w_fr.ap(), w_x.ap(), w_lb.ap(), None, OP.mult),
       ["w_x", "w_lb", "w_fr"], ["w_fr"], "plnb")
    se(lambda s: s.activation(w_k.ap(), w_fr.ap(), AF.Exp, scale=-1.0),
       ["w_fr", "w_k"], ["w_k"], "isc")
    se(lambda s: s.activation(w_fr.ap(), w_fr.ap(), AF.Exp),
       ["w_fr"], ["w_fr"], "sc")

    def combine(outt, outk, srcA, srcAk, srcB, srcBk, scl, tg):
        ve(lambda v: v.scalar_tensor_tensor(outt.ap(), srcA.ap(), scl, srcB.ap(),
                                            OP.mult, OP.mult),
           [srcAk, srcBk], [outk], tg)

    def replicate_sn(tbl, tblk, kpos, kneg):
        for blk in range(2):
            r0 = blk * 64
            for (dst0, src, sk, s0) in ((0, w_o2, kneg, 0), (16, w_o1, kpos, 0),
                                        (32, w_o2, kneg, 16), (48, w_o1, kpos, 16)):
                def mk(dst0=dst0, src=src, s0=s0, r0=r0, tbl=tbl):
                    return lambda g: g.dma_start(tbl[r0 + dst0:r0 + dst0 + 16, :],
                                                 src[s0:s0 + 16, :])
                P.add("gp", mk(), [sk], [tblk], dma=True, tag=f"rep{tblk}{r0+dst0}")

    def replicate_cs(tbl, tblk, kcs):
        for blk in range(2):
            r0 = blk * 64
            for (dst0, s0) in ((0, 0), (16, 0), (32, 16), (48, 16)):
                def mk(dst0=dst0, s0=s0, r0=r0, tbl=tbl):
                    return lambda g: g.dma_start(tbl[r0 + dst0:r0 + dst0 + 16, :],
                                                 w_o1[s0:s0 + 16, :])
                P.add("gp", mk(), [kcs], [tblk], dma=True, tag=f"rep{tblk}{r0+dst0}")

    combine(w_o1, "w_o1", w_cos, "wcout", w_fr, "w_fr", 8.0, "csq32")
    replicate_cs(t_csq, "t_csq", "w_o1")
    combine(w_o1, "w_o1", w_sin, "wsout", w_fr, "w_fr", 8.0, "snqp")
    combine(w_o2, "w_o2", w_sin, "wsout", w_fr, "w_fr", -8.0, "snqn")
    replicate_sn(t_snq, "t_snq", "w_o1", "w_o2")
    combine(w_o1, "w_o1", w_cos, "wcout", w_k, "w_k", 1.0, "csk32")
    replicate_cs(t_csk, "t_csk", "w_o1")
    combine(w_o1, "w_o1", w_sin, "wsout", w_k, "w_k", 1.0, "snkp")
    combine(w_o2, "w_o2", w_sin, "wsout", w_k, "w_k", -1.0, "snkn")
    replicate_sn(t_snk, "t_snk", "w_o1", "w_o2")

    P.barrier()
    tb.close()

    def ln_stats_chunk(src_bf, sq_bf, ksrc, ksq, coff, coff_sq, n, shifted, tg):
        for st, (buf, kin, bk, cbase) in enumerate(
                ((src_bf, ksrc, 6, coff), (sq_bf, ksq, 7, coff_sq))):
            for dc in range(NDC):
                off = cbase + (0 if (shifted and dc >= 4) else 1)
                def mk(dc=dc, off=off, bk=bk, buf=buf, n=n):
                    return lambda t: t.matmul(bank[bk][0:1, 0:n], onescol.ap(),
                                              buf[:, dc, off:off + n],
                                              start=(dc == 0), stop=(dc == NDC - 1))
                te(mk(), [kin, "onescol"], [BK[bk]], f"{tg}st{st}dc{dc}")
        se(lambda s: s.activation(r_mu[0:1, 0:n], bank[6][0:1, 0:n], AF.Identity,
                                  scale=1.0 / 1024.0),
           [BK[6]], ["r_mu"], f"{tg}mu")
        se(lambda s: s.activation(r_s2[0:1, 0:n], bank[7][0:1, 0:n], AF.Identity,
                                  scale=1.0 / 1024.0),
           [BK[7]], ["r_s2"], f"{tg}s2")
        ve(lambda v: v.tensor_tensor(r_var[0:1, 0:n], r_mu[0:1, 0:n],
                                     r_mu[0:1, 0:n], OP.mult),
           ["r_mu"], ["r_var"], f"{tg}mu2")
        ve(lambda v: v.tensor_tensor(r_var[0:1, 0:n], r_s2[0:1, 0:n],
                                     r_var[0:1, 0:n], OP.subtract),
           ["r_s2", "r_var"], ["r_var"], f"{tg}var")
        se(lambda s: s.activation(r_var[0:1, 0:n], r_var[0:1, 0:n], AF.Ln,
                                  bias=ceps[0:1, :]),
           ["r_var", "ceps"], ["r_var"], f"{tg}ln")
        se(lambda s: s.activation(r_rstd[0:1, 0:n], r_var[0:1, 0:n], AF.Exp,
                                  scale=-0.5),
           ["r_var"], ["r_rstd"], f"{tg}rstd")
        ve(lambda v: v.tensor_tensor(r_mrs[0:1, 0:n], r_mu[0:1, 0:n],
                                     r_rstd[0:1, 0:n], OP.mult),
           ["r_mu", "r_rstd"], ["r_mrs"], f"{tg}mrs")

    def ln_bcast(n, tg):
        te(lambda t: t.matmul(bank[4][:, 0:n], onesrow[0:1, :], r_rstd[0:1, 0:n],
                              start=True, stop=True),
           ["onesrow", "r_rstd"], [BK[4]], f"{tg}bcA")
        te(lambda t: t.matmul(bank[5][:, 0:n], onesrow[0:1, :], r_mrs[0:1, 0:n],
                              start=True, stop=True),
           ["onesrow", "r_mrs"], [BK[5]], f"{tg}bcB")

    def ln_pass(xin, xin_name, zout, zout_name, tg):
        for p in range(SEQ // PH_F):
            t0 = p * PH_F
            sdma(zf[:, :, 1:PH_F + 1],
                 xin[:, t0:t0 + PH_F].rearrange("(a q) t -> q a t", q=128),
                 hk(xin_name, t0, PH_F), ["zf"], f"{tg}in{p}")
            se(lambda s: s.activation(xs_bf[:, :, 1:PH_F + 1], zf[:, :, 1:PH_F + 1],
                                      AF.Identity),
               ["zf"], ["xs_bf"], f"{tg}bf{p}")
            for c in range(PH_F // 512):
                co = c * 512
                def mksq(co=co):
                    return lambda s: s.activation(xsq_bf[:, :, 1:513],
                                                  zf[:, :, co + 1:co + 513], AF.Square)
                se(mksq(), ["zf"], ["xsq_bf"], f"{tg}sq{p}_{c}")
                ln_stats_chunk(xs_bf, xsq_bf, "xs_bf", "xsq_bf", co, 0, 512, False,
                               f"{tg}p{p}c{c}_")
                ln_bcast(512, f"{tg}p{p}c{c}_")
                bcA = bank[4][:, 0:512].unsqueeze(1).broadcast_to([128, NDC, 512])
                bcB = bank[5][:, 0:512].unsqueeze(1).broadcast_to([128, NDC, 512])
                def mk1(co=co, bcA=bcA):
                    return lambda v: v.tensor_tensor(zf[:, :, 1 + co:513 + co],
                                                     zf[:, :, 1 + co:513 + co],
                                                     bcA, OP.mult)
                ve(mk1(), ["zf", BK[4]], ["zf"], f"{tg}ap1_{p}_{c}")
                def mk2(co=co, bcB=bcB):
                    return lambda v: v.tensor_tensor(zf[:, :, 1 + co:513 + co],
                                                     zf[:, :, 1 + co:513 + co],
                                                     bcB, OP.subtract)
                ve(mk2(), ["zf", BK[5]], ["zf"], f"{tg}ap2_{p}_{c}")
            sdma(zout[:, t0:t0 + PH_F].rearrange("(a q) t -> q a t", q=128),
                 zf[:, :, 1:PH_F + 1], ["zf"], hk(zout_name, t0, PH_F), f"{tg}out{p}")

    if stage >= 1:
        ln_pass(xT, "xT", zH, "zH", "ln0")
    def attn_sublayer(li, tg):
        sc = ExitStack()
        S = lambda n, shp, d=BF16: sc.enter_context(nc.sbuf_tensor(tg + n, shp, d))
        wqk_sl = [S(f"wqk{i}", [128, NDC, 128]) for i in range(2)]
        wv = S("wv", [128, NDC, 1024])
        wo_sl = [S(f"wo{i}", [128, NDC, 128]) for i in range(2)]
        q_bf = S("q_bf", [128, NDC, 512])
        k_bf = S("k_bf", [128, NDC, 512])
        qr_bf = S("qr_bf", [128, NDC, 512])
        kr_bf = S("kr_bf", [128, NDC, 3, 256])
        v66 = S("v66", [128, 3, 2, 16, 65])
        qsh = S("qsh", [128, 512])
        qm = S("qm", [128, 512])
        st_sb = S("st_sb", [16, 512], F32)
        r_l2 = S("r_l2", [1, 16 * 512])
        eT = S("eT", [128, 2, 1536])
        outSb = S("outSb", [128, 512])
        rec_sb = S("rec_sb", [128, 512])
        outN = S("outN", [128, NDC, 512])

        gp(lambda g: g.memset(v66[:, :, :, :, 64:65], 1.0), (), [("v66", "k")],
           f"{tg}v66o")
        gdma(p_qs[0:64, 0:1], fqs[li, :], (), ["p_qs"], f"{tg}qs0")
        gdma(p_qs[64:128, 0:1], fqs[li, :], ["p_qs"], ["p_qs"], f"{tg}qs1")
        gdma(p_ks[0:64, 0:1], fks[li, :], (), ["p_ks"], f"{tg}ks0")
        gdma(p_ks[64:128, 0:1], fks[li, :], ["p_ks"], ["p_ks"], f"{tg}ks1")
        sdma(wv.ap(), bWv[li], [("wc", "wv")], ["wv"], f"{tg}wv")
        gp(lambda g: g.memset(zf[:, :, 0:1], 0.0), ["zf"], ["zf"], f"{tg}zc")

        NPH = SEQ // PH_A
        for p in range(NPH):
            t0 = p * PH_A
            sdma(zf[:, :, 1:PH_A + 1],
                 zH[:, t0:t0 + PH_A].rearrange("(a q) t -> q a t", q=128),
                 hk("zH", t0, PH_A) + ["zf"], ["zf"], f"{tg}in{p}")
            if p > 0:
                ve(lambda v: v.tensor_copy(zf[:, :, 0:1], carryb.ap()),
                   ["carryb", "zf"], ["zf"], f"{tg}rcar{p}")
            ve(lambda v: v.tensor_copy(carryb.ap(), zf[:, :, PH_A:PH_A + 1]),
               ["zf"], ["carryb"], f"{tg}ccar{p}")
            se(lambda s: s.activation(xs_bf[:, :, 0:PH_A + 1], zf[:, :, 0:PH_A + 1],
                                      AF.Identity),
               ["zf"], ["xs_bf"], f"{tg}bf{p}")

            for grp, (dst, pvec, kpv, tcs, tsn, tcsk, tsnk) in enumerate(
                    ((q_bf, p_qs, "p_qs", t_csq, t_snq, None, None),
                     (k_bf, p_ks, "p_ks", t_csk, t_snk, None, None))):
                cs_t = t_csq if grp == 0 else t_csk
                sn_t = t_snq if grp == 0 else t_snk
                kcs = "t_csq" if grp == 0 else "t_csk"
                ksn = "t_snq" if grp == 0 else "t_snk"
                kdst = "q_bf" if grp == 0 else "k_bf"
                for pt in range(NDC):
                    ft = grp * 8 + pt
                    par = pt % 2
                    bkq = par
                    def mkw(ft=ft, par=par):
                        return lambda s: s.dma_start(wqk_sl[par].ap(), bWqk[li, ft])
                    P.add("sq", mkw(), [("wc", "wqk")],
                          [("wqk_sl", par)], dma=True, tag=f"{tg}w{p}_{ft}")
                    for dc in range(NDC):
                        off = 0 if dc >= 4 else 1
                        def mkm(dc=dc, off=off, bkq=bkq, par=par):
                            return lambda t: t.matmul(
                                bank[bkq][:, 0:512], wqk_sl[par][:, dc, :],
                                xs_bf[:, dc, off:off + 512],
                                start=(dc == 0), stop=(dc == NDC - 1))
                        te(mkm(), [("wqk_sl", par), "xs_bf"], [BK[bkq]],
                           f"{tg}qk{p}_{ft}_{dc}")
                    def mke1(pt=pt, bkq=bkq):
                        return lambda s: s.activation(xsq_bf[:, pt, 0:512],
                                                      bank[bkq][:, 0:512], AF.Square)
                    se(mke1(), [BK[bkq]], ["xsq_bf"], f"{tg}sq{p}_{ft}")
                    def mke2(pt=pt, bkq=bkq, dst=dst, pvec=pvec):
                        return lambda s: s.activation(dst[:, pt, 0:512],
                                                      bank[bkq][:, 0:512], AF.Identity,
                                                      scale=pvec[:, 0:1])
                    se(mke2(), [BK[bkq], kpv], [kdst], f"{tg}ev{p}_{ft}")
                if ATTN_CUT < 2:
                    continue
                for dc in range(NDC):
                    def mks(dc=dc):
                        return lambda t: t.matmul(bank[6][0:16, 0:512],
                                                  sel16[:, dc, :], xsq_bf[:, dc, 0:512],
                                                  start=(dc == 0), stop=(dc == NDC - 1))
                    te(mks(), ["sel16", "xsq_bf"], [BK[6]], f"{tg}l2s{p}_{grp}_{dc}")
                ve(lambda v: v.tensor_scalar(st_sb.ap(), bank[6][0:16, 0:512],
                                             1e-16, None, OP.max),
                   [BK[6]], ["st_sb"], f"{tg}l2m{p}_{grp}")
                se(lambda s: s.activation(st_sb.ap(), st_sb.ap(), AF.Ln),
                   ["st_sb"], ["st_sb"], f"{tg}l2l{p}_{grp}")
                se(lambda s: s.activation(st_sb.ap(), st_sb.ap(), AF.Exp, scale=-0.5),
                   ["st_sb"], ["st_sb"], f"{tg}l2e{p}_{grp}")
                gdma(r_l2[0:1, :].rearrange("o (h t) -> o h t", h=16), st_sb.ap(),
                     ["st_sb"], ["r_l2"], f"{tg}l2r{p}_{grp}")
                for pt in range(NDC):
                    bkb = 4 + pt % 2
                    h0, h1 = 2 * pt, 2 * pt + 1
                    def mkb0(h0=h0, bkb=bkb):
                        return lambda t: t.matmul(bank[bkb][0:64, 0:512],
                                                  onesrow[0:1, 0:64],
                                                  r_l2[0:1, h0 * 512:(h0 + 1) * 512],
                                                  start=True, stop=True)
                    te(mkb0(), ["onesrow", "r_l2"], [BK[bkb]], f"{tg}bc{p}_{grp}_{h0}")
                    def mkb1(h1=h1, bkb=bkb):
                        return lambda t: t.matmul(bank[bkb][64:128, 0:512],
                                                  onesrow[0:1, 0:64],
                                                  r_l2[0:1, h1 * 512:(h1 + 1) * 512],
                                                  start=True, stop=True)
                    te(mkb1(), ["onesrow", "r_l2"], [BK[bkb]], f"{tg}bc{p}_{grp}_{h1}")
                    def mkn(pt=pt, bkb=bkb, dst=dst):
                        return lambda v: v.tensor_tensor(dst[:, pt, :], dst[:, pt, :],
                                                         bank[bkb][:, 0:512], OP.mult)
                    ve(mkn(), [kdst, BK[bkb]], [kdst], f"{tg}n{p}_{grp}_{pt}")
                    SHUF = [(i + 16) % 32 for i in range(32)]
                    def mksh(pt=pt, dst=dst):
                        return lambda v: v.stream_shuffle(qsh.ap(), dst[:, pt, :], SHUF)
                    ve(mksh(), [kdst], ["qsh"], f"{tg}sh{p}_{grp}_{pt}")
                    def mkqm(pt=pt, dst=dst, cs_t=cs_t, t0=t0):
                        return lambda v: v.tensor_tensor(qm.ap(), dst[:, pt, :],
                                                         cs_t[:, t0:t0 + 512], OP.mult)
                    ve(mkqm(), [kdst, kcs], ["qm"], f"{tg}m{p}_{grp}_{pt}")
                    def mkrt(pt=pt, sn_t=sn_t, t0=t0):
                        return lambda v: v.tensor_tensor(qsh.ap(), qsh.ap(),
                                                         sn_t[:, t0:t0 + 512], OP.mult)
                    ve(mkrt(), ["qsh", ksn], ["qsh"], f"{tg}r{p}_{grp}_{pt}")
                    if grp == 0:
                        def mka(pt=pt):
                            return lambda v: v.tensor_tensor(qr_bf[:, pt, :], qm.ap(),
                                                             qsh.ap(), OP.add)
                        ve(mka(), ["qm", "qsh"], ["qr_bf"], f"{tg}a{p}_{grp}_{pt}")
                    else:
                        for half in range(2):
                            seg = (2 * p + half) % 3
                            def mka(pt=pt, half=half, seg=seg):
                                return lambda v: v.tensor_tensor(
                                    kr_bf[:, pt, seg, :],
                                    qm[:, half * 256:(half + 1) * 256],
                                    qsh[:, half * 256:(half + 1) * 256], OP.add)
                            ve(mka(), ["qm", "qsh"], [("kr", seg)],
                               f"{tg}a{p}_{grp}_{pt}_{half}")

            for tt in range(4 if ATTN_CUT >= 3 else 0):
                seg = (2 * p + tt // 2) % 3
                tile = tt % 2
                toff = 0 if True else 0
                for half in range(2):
                    bkv = half
                    for dc in range(NDC):
                        off = 0 if dc >= 4 else 1
                        def mkv(tt=tt, dc=dc, off=off, bkv=bkv, half=half):
                            return lambda t: t.matmul(
                                bank[bkv][:, 0:512],
                                xs_bf[:, dc, off + tt * 128: off + tt * 128 + 128],
                                wv[:, dc, half * 512:(half + 1) * 512],
                                start=(dc == 0), stop=(dc == NDC - 1))
                        te(mkv(), ["xs_bf", "wv"], [BK[bkv]], f"{tg}v{p}_{tt}_{half}_{dc}")
                    def mkve(tt=tt, bkv=bkv, half=half, seg=seg, tile=tile):
                        src = bank[bkv][:, 0:512].rearrange("q (h e) -> q h e", e=64)
                        return lambda s: s.activation(
                            v66[:, seg, tile, half * 8:(half + 1) * 8, 0:64], src,
                            AF.Identity)
                    se(mkve(), [BK[bkv]], [("v66", seg)], f"{tg}ve{p}_{tt}_{half}")

            segP = (2 * p - 1) % 3
            segC0 = (2 * p) % 3
            segC1 = (2 * p + 1) % 3
            for j in range(8):
                for sub, h in ((0, 2 * j), (1, 2 * j + 1)):
                    rb = (h % 2) * 64
                    ptl = h // 2
                    pvb = 6 + sub
                    even = (sub == 0)
                    def mm(bk, cols, lhsT, rhs, st, sp, rd, tag):
                        def f(t, bk=bk, cols=cols, lhsT=lhsT, rhs=rhs, st=st, sp=sp):
                            return t.matmul(bank[bk][:, cols[0]:cols[1]],
                                            lhsT, rhs, start=st, stop=sp)
                        te(f, rd, [BK[bk]], tag)
                    krs = lambda seg, tl: kr_bf[rb:rb + 64, ptl, seg,
                                                tl * 128:tl * 128 + 128]
                    qrs = lambda c0, c1: qr_bf[rb:rb + 64, ptl, c0:c1]
                    if p > 0:
                        mm(0, (0, 128), krs(segP, 0), qrs(0, 128), True, False,
                           [("kr", segP), "qr_bf"], f"{tg}m0_{p}{h}")
                        mm(0, (0, 128), ident.ap(), mA1[:, 0:128], False, False,
                           ["ident", "mA1"], f"{tg}k0_{p}{h}")
                        mm(0, (128, 384), krs(segP, 1), qrs(0, 256), False, False,
                           [("kr", segP), "qr_bf"], f"{tg}m1_{p}{h}")
                        mm(0, (256, 384), ident.ap(), mA1[:, 0:128], False, False,
                           ["ident", "mA1"], f"{tg}k1_{p}{h}")
                    mm(0, (384, 512), krs(segC1, 1), qrs(384, 512), (p == 0), False,
                       [("kr", segC1), "qr_bf"], f"{tg}m5_{p}{h}")
                    mm(0, (384, 512), ident.ap(), mA2[:, 0:128], False, True,
                       ["ident", "mA2"], f"{tg}k5_{p}{h}")
                    mm(1, (0, 384), krs(segC0, 0), qrs(0, 384), True, False,
                       [("kr", segC0), "qr_bf"], f"{tg}m2_{p}{h}")
                    mm(1, (0, 128), ident.ap(), mA2[:, 0:128], False, False,
                       ["ident", "mA2"], f"{tg}k2a_{p}{h}")
                    mm(1, (256, 384), ident.ap(), mA1[:, 0:128], False, True,
                       ["ident", "mA1"], f"{tg}k2b_{p}{h}")
                    mm(2, (0, 384), krs(segC0, 1), qrs(128, 512), True, False,
                       [("kr", segC0), "qr_bf"], f"{tg}m3_{p}{h}")
                    mm(2, (0, 128), ident.ap(), mA2[:, 0:128], False, False,
                       ["ident", "mA2"], f"{tg}k3a_{p}{h}")
                    mm(2, (256, 384), ident.ap(), mA1[:, 0:128], False, True,
                       ["ident", "mA1"], f"{tg}k3b_{p}{h}")
                    mm(3, (0, 256), krs(segC1, 0), qrs(256, 512), True, False,
                       [("kr", segC1), "qr_bf"], f"{tg}m4_{p}{h}")
                    mm(3, (0, 128), ident.ap(), mA2[:, 0:128], False, True,
                       ["ident", "mA2"], f"{tg}k4_{p}{h}")
                    if p > 0:
                        def mkeA(sub=sub):
                            return lambda s: s.activation(eT[:, sub, 0:512],
                                                          bank[0][:, 0:512], AF.Exp,
                                                          bias=vneg12[:, 0:1])
                        se(mkeA(), [BK[0], "vneg12"], [("eT", sub)], f"{tg}eA_{p}{h}")
                    else:
                        def mkeA(sub=sub):
                            return lambda s: s.activation(eT[:, sub, 384:512],
                                                          bank[0][:, 384:512], AF.Exp,
                                                          bias=vneg12[:, 0:1])
                        se(mkeA(), [BK[0], "vneg12"], [("eT", sub)], f"{tg}eA_{p}{h}")
                    def mkeB(sub=sub):
                        return lambda s: s.activation(eT[:, sub, 512:896],
                                                      bank[1][:, 0:384], AF.Exp,
                                                      bias=vneg12[:, 0:1])
                    se(mkeB(), [BK[1], "vneg12"], [("eT", sub)], f"{tg}eB_{p}{h}")
                    def mkeC(sub=sub):
                        return lambda s: s.activation(eT[:, sub, 896:1280],
                                                      bank[2][:, 0:384], AF.Exp,
                                                      bias=vneg12[:, 0:1])
                    se(mkeC(), [BK[2], "vneg12"], [("eT", sub)], f"{tg}eC_{p}{h}")
                    def mkeD(sub=sub):
                        return lambda s: s.activation(eT[:, sub, 1280:1536],
                                                      bank[3][:, 0:256], AF.Exp,
                                                      bias=vneg12[:, 0:1])
                    se(mkeD(), [BK[3], "vneg12"], [("eT", sub)], f"{tg}eD_{p}{h}")
                    vb = (lambda sg, tl: v66[:, sg, tl, h, 0:65]) if even else \
                         (lambda sg, tl: v66[:, sg, tl, h, 0:64])
                    ob, oe = (0, 65) if even else (64, 128)
                    def pv(cols, lhsT, ecols, st, rd, tag, ob=ob, oe=oe, pvb=pvb,
                           sub=sub):
                        def f(t, cols=cols, lhsT=lhsT, ecols=ecols, st=st, ob=ob,
                              oe=oe, pvb=pvb, sub=sub):
                            return t.matmul(bank[pvb][ob:oe, cols[0]:cols[1]],
                                            lhsT, eT[:, sub, ecols[0]:ecols[1]],
                                            start=st, stop=False,
                                            skip_group_check=True)
                        te(f, rd + [("v66", "k"), ("eT", sub)], [BK[pvb]], tag)
                    pv((0, 384), vb(segC0, 0), (512, 896), True,
                       [("v66", segC0)], f"{tg}p2_{p}{h}")
                    if p > 0:
                        pv((0, 128), vb(segP, 0), (0, 128), False,
                           [("v66", segP)], f"{tg}p0_{p}{h}")
                        pv((0, 256), vb(segP, 1), (128, 384), False,
                           [("v66", segP)], f"{tg}p1_{p}{h}")
                    pv((128, 384), vb(segC0, 1), (896, 1152), False,
                       [("v66", segC0)], f"{tg}p3a_{p}{h}")
                    pv((384, 512), vb(segC0, 1), (1152, 1280), False,
                       [("v66", segC0)], f"{tg}p3b_{p}{h}")
                    pv((256, 384), vb(segC1, 0), (1280, 1408), False,
                       [("v66", segC1)], f"{tg}p4a_{p}{h}")
                    pv((384, 512), vb(segC1, 0), (1408, 1536), False,
                       [("v66", segC1)], f"{tg}p4b_{p}{h}")
                    pv((384, 512), vb(segC1, 1), (384, 512), False,
                       [("v66", segC1)], f"{tg}p5_{p}{h}")
                    if not even:
                        def rs(cols, ecols, st, tag, pvb=pvb, sub=sub):
                            def f(t, cols=cols, ecols=ecols, st=st, pvb=pvb, sub=sub):
                                return t.matmul(bank[pvb][32:33, cols[0]:cols[1]],
                                                onescol.ap(),
                                                eT[:, sub, ecols[0]:ecols[1]],
                                                start=st, stop=False,
                                                skip_group_check=True)
                            te(f, ["onescol", ("eT", sub)], [BK[pvb]], tag)
                        rs((0, 384), (512, 896), True, f"{tg}r2_{p}{h}")
                        if p > 0:
                            rs((0, 128), (0, 128), False, f"{tg}r0_{p}{h}")
                            rs((0, 256), (128, 384), False, f"{tg}r1_{p}{h}")
                        rs((128, 384), (896, 1152), False, f"{tg}r3a_{p}{h}")
                        rs((384, 512), (1152, 1280), False, f"{tg}r3b_{p}{h}")
                        rs((256, 384), (1280, 1408), False, f"{tg}r4a_{p}{h}")
                        rs((384, 512), (1408, 1536), False, f"{tg}r4b_{p}{h}")
                        rs((384, 512), (384, 512), False, f"{tg}r5_{p}{h}")
                def mkr1():
                    return lambda v: v.reciprocal(rec_sb[64:65, 0:512],
                                                  bank[6][64:65, 0:512])
                ve(mkr1(), [BK[6]], ["rec_sb"], f"{tg}rc0_{p}{j}")
                def mkr2():
                    return lambda v: v.reciprocal(rec_sb[32:33, 0:512],
                                                  bank[7][32:33, 0:512])
                ve(mkr2(), [BK[7]], ["rec_sb"], f"{tg}rc1_{p}{j}")
                def mkbc0():
                    return lambda t: t.matmul(bank[4][0:64, 0:512],
                                              onesrow[64:65, 0:64],
                                              rec_sb[64:65, 0:512],
                                              start=True, stop=True)
                te(mkbc0(), ["onesrow", "rec_sb"], [BK[4]], f"{tg}bc0_{p}{j}")
                def mkbc1():
                    return lambda t: t.matmul(bank[4][64:128, 0:512],
                                              onesrow[32:33, 0:64],
                                              rec_sb[32:33, 0:512],
                                              start=True, stop=True)
                te(mkbc1(), ["onesrow", "rec_sb"], [BK[4]], f"{tg}bc1_{p}{j}")
                def mkev0():
                    return lambda s: s.activation(outSb[0:64, 0:512],
                                                  bank[6][0:64, 0:512], AF.Identity)
                se(mkev0(), [BK[6]], ["outSb"], f"{tg}ev0_{p}{j}")
                def mkev1():
                    return lambda s: s.activation(outSb[64:128, 0:512],
                                                  bank[7][64:128, 0:512], AF.Identity)
                se(mkev1(), [BK[7]], ["outSb"], f"{tg}ev1_{p}{j}")
                def mkn0(j=j):
                    return lambda v: v.tensor_tensor(outN[0:64, j, 0:512],
                                                     outSb[0:64, 0:512],
                                                     bank[4][0:64, 0:512], OP.mult)
                ve(mkn0(), ["outSb", BK[4]], ["outN"], f"{tg}no0_{p}{j}")
                def mkn1(j=j):
                    return lambda v: v.tensor_tensor(outN[64:128, j, 0:512],
                                                     outSb[64:128, 0:512],
                                                     bank[4][64:128, 0:512], OP.mult)
                ve(mkn1(), ["outSb", BK[4]], ["outN"], f"{tg}no1_{p}{j}")

            for dt in range(NDC if ATTN_CUT >= 12 else 0):
                bky = dt % 2
                wpar = dt % 2
                def mkwd(dt=dt, wpar=wpar):
                    return lambda s: s.dma_start(wo_sl[wpar].ap(), bWo[li, dt])
                P.add("sq", mkwd(), [("wc", "wo")], [("wo_sl", wpar)], dma=True,
                      tag=f"{tg}wod{p}_{dt}")
                for ic in range(NDC):
                    def mkwo(dt=dt, ic=ic, bky=bky, wpar=wpar):
                        return lambda t: t.matmul(bank[bky][:, 0:512],
                                                  wo_sl[wpar][:, ic, :], outN[:, ic, :],
                                                  start=(ic == 0), stop=(ic == NDC - 1))
                    te(mkwo(), [("wo_sl", wpar), "outN"], [BK[bky]], f"{tg}wo{p}_{dt}_{ic}")
                def mkres(dt=dt, bky=bky):
                    return lambda v: v.tensor_tensor(zf[:, dt, 1:513], zf[:, dt, 1:513],
                                                     bank[bky][:, 0:512], OP.add)
                ve(mkres(), ["zf", BK[bky]], ["zf"], f"{tg}res{p}_{dt}")
            sdma(zH[:, t0:t0 + PH_A].rearrange("(a q) t -> q a t", q=128),
                 zf[:, :, 1:PH_A + 1], ["zf"], hk("zH", t0, PH_A), f"{tg}out{p}")

        sc.close()
        P.barrier()


    def ffn_sublayer(li, tg):
        PHF = 512
        sc = ExitStack()
        S = lambda n, shp, d=BF16: sc.enter_context(nc.sbuf_tensor(tg + n, shp, d))
        w1a_sl = [S(f"w1a{i}", [128, NDC, 512]) for i in range(2)]
        w1g_sl = [S(f"w1g{i}", [128, NDC, 512]) for i in range(2)]
        w2_sl = [S(f"w2s{i}", [128, KC, 128]) for i in range(2)]
        xn_bf = S("xn_bf", [128, NDC, PHF])
        h2T = S("h2T", [128, KC, PHF])
        ge = S("ge", [128, 512])
        a2 = S("a2", [128, 512])
        t1f = S("t1f", [128, NDC, 512], F32)

        gdma_nc(p_g.ap(), fg[li].rearrange("(a q) -> q a", q=128), (), ["p_g"], f"{tg}g")
        gdma_nc(p_b.ap(), fb[li].rearrange("(a q) -> q a", q=128), (), ["p_b"], f"{tg}b")
        gdma_nc(p_b2.ap(), fb2[li].rearrange("(a q) -> q a", q=128), (), ["p_b2"], f"{tg}b2")
        gdma_nc(p_b1a.ap(), fb1a[li].rearrange("(k q) -> q k", q=128), (), ["p_b1a"], f"{tg}b1a")
        gdma_nc(p_b1g.ap(), fb1g[li].rearrange("(k q) -> q k", q=128), (), ["p_b1g"], f"{tg}b1g")
        gp(lambda g: g.memset(zf[:, :, 0:1], 0.0), ["zf"], ["zf"], f"{tg}zc")

        for p in range(SEQ // PHF):
            t0 = p * PHF
            sdma(zf[:, :, 1:PHF + 1],
                 zH[:, t0:t0 + PHF].rearrange("(a q) t -> q a t", q=128),
                 hk("zH", t0, PHF) + ["zf"], ["zf"], f"{tg}in{p}")
            if p > 0:
                ve(lambda v: v.tensor_copy(zf[:, :, 0:1], carryb.ap()),
                   ["carryb", "zf"], ["zf"], f"{tg}rcar{p}")
            ve(lambda v: v.tensor_copy(carryb.ap(), zf[:, :, PHF:PHF + 1]),
               ["zf"], ["carryb"], f"{tg}ccar{p}")
            se(lambda s: s.activation(xs_bf[:, :, 0:PHF + 1], zf[:, :, 0:PHF + 1],
                                      AF.Identity),
               ["zf"], ["xs_bf"], f"{tg}bf{p}")
            se(lambda s: s.activation(xsq_bf[:, :, 0:513], zf[:, :, 0:513], AF.Square),
               ["zf"], ["xsq_bf"], f"{tg}sq{p}")
            ln_stats_chunk(xs_bf, xsq_bf, "xs_bf", "xsq_bf", 0, 0, 512, True,
                           f"{tg}p{p}_")
            ln_bcast(512, f"{tg}p{p}_")
            for half, (d0, off) in enumerate(((0, 1), (4, 0))):
                bcA = bank[4][:, 0:512].unsqueeze(1).broadcast_to([128, 4, 512])
                bcB = bank[5][:, 0:512].unsqueeze(1).broadcast_to([128, 4, 512])
                def mk1(d0=d0, off=off, bcA=bcA):
                    return lambda v: v.tensor_tensor(
                        t1f[:, d0:d0 + 4, :], zf[:, d0:d0 + 4, off:off + 512],
                        bcA, OP.mult)
                ve(mk1(), ["zf", BK[4]], ["t1f"], f"{tg}a1_{p}{half}")
                def mk2(d0=d0, bcB=bcB):
                    return lambda v: v.tensor_tensor(
                        t1f[:, d0:d0 + 4, :], t1f[:, d0:d0 + 4, :], bcB, OP.subtract)
                ve(mk2(), ["t1f", BK[5]], ["t1f"], f"{tg}a2_{p}{half}")
            for dc in range(NDC):
                def mk3(dc=dc):
                    return lambda s: s.activation(xn_bf[:, dc, :], t1f[:, dc, :],
                                                  AF.Identity,
                                                  scale=p_g[:, dc:dc + 1],
                                                  bias=p_b[:, dc:dc + 1])
                se(mk3(), ["t1f", "p_g", "p_b"], ["xn_bf"], f"{tg}a3_{p}{dc}")
            for ft in range(KC):
                gidx = ft // 4
                par = gidx % 2
                if ft % 4 == 0:
                    for (sl, bsrc, kw) in ((w1a_sl, bW1a, "w1asl"),
                                           (w1g_sl, bW1g, "w1gsl")):
                        def mkw(sl=sl, bsrc=bsrc, par=par, gidx=gidx):
                            return lambda s: s.dma_start(sl[par].ap(), bsrc[li, gidx])
                        P.add("sq", mkw(), [("wc", kw[:3])], [(kw, par)], dma=True,
                              tag=f"{tg}w1d{p}_{gidx}_{kw}")
                fo = (ft % 4) * 128
                bka, bkg = ft % 2, 2 + ft % 2
                for dc in range(NDC):
                    def mka(dc=dc, fo=fo, par=par, bka=bka):
                        return lambda t: t.matmul(
                            bank[bka][:, 0:512], w1a_sl[par][:, dc, fo:fo + 128],
                            xn_bf[:, dc, :], start=(dc == 0), stop=(dc == NDC - 1))
                    te(mka(), [("w1asl", par), "xn_bf"], [BK[bka]],
                       f"{tg}mA{p}_{ft}_{dc}")
                    def mkg(dc=dc, fo=fo, par=par, bkg=bkg):
                        return lambda t: t.matmul(
                            bank[bkg][:, 0:512], w1g_sl[par][:, dc, fo:fo + 128],
                            xn_bf[:, dc, :], start=(dc == 0), stop=(dc == NDC - 1))
                    te(mkg(), [("w1gsl", par), "xn_bf"], [BK[bkg]],
                       f"{tg}mG{p}_{ft}_{dc}")
                def mkge(ft=ft, bkg=bkg):
                    return lambda s: s.activation(ge.ap(), bank[bkg][:, 0:512],
                                                  AF.Gelu, bias=p_b1g[:, ft:ft + 1])
                se(mkge(), [BK[bkg], "p_b1g"], ["ge"], f"{tg}ge{p}_{ft}")
                def mka2(ft=ft, bka=bka):
                    return lambda s: s.activation(a2.ap(), bank[bka][:, 0:512],
                                                  AF.Identity, bias=p_b1a[:, ft:ft + 1])
                se(mka2(), [BK[bka], "p_b1a"], ["a2"], f"{tg}ae{p}_{ft}")
                def mkh2(ft=ft):
                    return lambda v: v.tensor_tensor(h2T[:, ft, :], a2.ap(), ge.ap(),
                                                     OP.mult)
                ve(mkh2(), ["a2", "ge"], ["h2T"], f"{tg}h2_{p}_{ft}")
            for dt in range(NDC):
                par = dt % 2
                def mkw2(dt=dt, par=par):
                    return lambda s: s.dma_start(w2_sl[par].ap(), bW2[li, dt])
                P.add("sq", mkw2(), [("wc", "w2")], [("w2sl", par)], dma=True,
                      tag=f"{tg}w2d{p}_{dt}")
                bky = 4 + dt % 2
                for kc in range(KC):
                    def mky(dt=dt, kc=kc, par=par, bky=bky):
                        return lambda t: t.matmul(
                            bank[bky][:, 0:512], w2_sl[par][:, kc, :], h2T[:, kc, :],
                            start=(kc == 0), stop=(kc == KC - 1))
                    te(mky(), [("w2sl", par), "h2T"], [BK[bky]], f"{tg}y{p}_{dt}_{kc}")
                def mkres(dt=dt, bky=bky):
                    return lambda v: v.scalar_tensor_tensor(
                        zf[:, dt, 1:513], bank[bky][:, 0:512], p_b2[:, dt:dt + 1],
                        zf[:, dt, 1:513], OP.add, OP.add)
                ve(mkres(), ["zf", BK[bky], "p_b2"], ["zf"], f"{tg}res{p}_{dt}")
            sdma(zH[:, t0:t0 + PHF].rearrange("(a q) t -> q a t", q=128),
                 zf[:, :, 1:PHF + 1], ["zf"], hk("zH", t0, PHF), f"{tg}out{p}")
        sc.close()
        P.barrier()

    def combine_pass(xin, xin_name, xout, xout_name, tg):
        sc = ExitStack()
        zc2 = sc.enter_context(nc.sbuf_tensor(tg + "zc2", [128, NDC, 512], F32))
        for p in range(SEQ // 512):
            t0 = p * 512
            sdma(zf[:, :, 0:512],
                 xin[:, t0:t0 + 512].rearrange("(a q) t -> q a t", q=128),
                 hk(xin_name, t0, 512) + ["zf"], ["zf"], f"{tg}xi{p}")
            sdma(zc2.ap(), zH[:, t0:t0 + 512].rearrange("(a q) t -> q a t", q=128),
                 hk("zH", t0, 512) + (["zc2"] if p > 0 else []), ["zc2"], f"{tg}zi{p}")
            ve(lambda v: v.scalar_tensor_tensor(zf[:, :, 0:512], zc2.ap(), 0.5,
                                                zf[:, :, 0:512], OP.mult, OP.add),
               ["zf", "zc2"], ["zf"], f"{tg}cmb{p}")
            sdma(xout[:, t0:t0 + 512].rearrange("(a q) t -> q a t", q=128),
                 zf[:, :, 0:512], ["zf"], hk(xout_name, t0, 512), f"{tg}xo{p}")
        sc.close()
        P.barrier()

    def global_attn(tg):
        sc = ExitStack()
        S = lambda n, shp, d=BF16: sc.enter_context(nc.sbuf_tensor(tg + n, shp, d))
        kT = S("kTg", [128, 4, SEQ])
        v66g = S("v66g", [128, 32, 8, 65])
        scA = ExitStack()
        SA = lambda n, shp, d=BF16: scA.enter_context(nc.sbuf_tensor(tg + n, shp, d))
        wk = SA("wkg", [128, NDC, 512])
        wv = SA("wvg", [128, NDC, 512])
        st8 = SA("st8", [8, 512], F32)
        r_l8 = SA("r_l8", [1, 8 * 512])
        gp(lambda g: g.memset(v66g[:, :, :, 64:65], 1.0), (), [("v66g", "k")], f"{tg}vo")
        sdma(wk.ap(), baWk[:], [("wc", "awk")], ["wkg"], f"{tg}wk")
        sdma(wv.ap(), baWv[:], [("wc", "awv")], ["wvg"], f"{tg}wv")
        for p in range(SEQ // 512):
            t0 = p * 512
            sdma(zf[:, :, 1:513],
                 zH[:, t0:t0 + 512].rearrange("(a q) t -> q a t", q=128),
                 hk("zH", t0, 512) + ["zf"], ["zf"], f"{tg}ain{p}")
            se(lambda s: s.activation(xs_bf[:, :, 1:513], zf[:, :, 1:513], AF.Identity),
               ["zf"], ["xs_bf"], f"{tg}abf{p}")
            for pt in range(4):
                bkq = pt % 2
                for dc in range(NDC):
                    def mkk(pt=pt, dc=dc, bkq=bkq):
                        return lambda t: t.matmul(bank[bkq][:, 0:512],
                                                  wk[:, dc, pt * 128:(pt + 1) * 128],
                                                  xs_bf[:, dc, 1:513],
                                                  start=(dc == 0), stop=(dc == NDC - 1))
                    te(mkk(), ["wkg", "xs_bf"], [BK[bkq]], f"{tg}k{p}_{pt}_{dc}")
                def mksq(pt=pt, bkq=bkq):
                    return lambda s: s.activation(xsq_bf[:, pt, 0:512],
                                                  bank[bkq][:, 0:512], AF.Square)
                se(mksq(), [BK[bkq]], ["xsq_bf"], f"{tg}ksq{p}_{pt}")
                def mkev(pt=pt, bkq=bkq, t0=t0):
                    return lambda s: s.activation(kT[:, pt, t0:t0 + 512],
                                                  bank[bkq][:, 0:512], AF.Identity)
                se(mkev(), [BK[bkq]], ["kTg"], f"{tg}kev{p}_{pt}")
            for dc in range(4):
                def mks(dc=dc):
                    return lambda t: t.matmul(bank[6][0:8, 0:512], sel16[:, dc, 0:8],
                                              xsq_bf[:, dc, 0:512],
                                              start=(dc == 0), stop=(dc == 3))
                te(mks(), ["sel16", "xsq_bf"], [BK[6]], f"{tg}l2s{p}_{dc}")
            ve(lambda v: v.tensor_scalar(st8.ap(), bank[6][0:8, 0:512], 1e-16, None,
                                         OP.max),
               [BK[6]], ["st8"], f"{tg}l2m{p}")
            se(lambda s: s.activation(st8.ap(), st8.ap(), AF.Ln), ["st8"], ["st8"],
               f"{tg}l2l{p}")
            se(lambda s: s.activation(st8.ap(), st8.ap(), AF.Exp, scale=-0.5),
               ["st8"], ["st8"], f"{tg}l2e{p}")
            gdma(r_l8[0:1, :].rearrange("o (h t) -> o h t", h=8), st8.ap(),
                 ["st8"], ["r_l8"], f"{tg}l2r{p}")
            for pt in range(4):
                bkb = 4 + pt % 2
                h0, h1 = 2 * pt, 2 * pt + 1
                def mkb0(h0=h0, bkb=bkb):
                    return lambda t: t.matmul(bank[bkb][0:64, 0:512],
                                              onesrow[0:1, 0:64],
                                              r_l8[0:1, h0 * 512:(h0 + 1) * 512],
                                              start=True, stop=True)
                te(mkb0(), ["onesrow", "r_l8"], [BK[bkb]], f"{tg}bc{p}_{h0}")
                def mkb1(h1=h1, bkb=bkb):
                    return lambda t: t.matmul(bank[bkb][64:128, 0:512],
                                              onesrow[0:1, 0:64],
                                              r_l8[0:1, h1 * 512:(h1 + 1) * 512],
                                              start=True, stop=True)
                te(mkb1(), ["onesrow", "r_l8"], [BK[bkb]], f"{tg}bc{p}_{h1}")
                def mkn(pt=pt, bkb=bkb, t0=t0):
                    return lambda v: v.tensor_tensor(kT[:, pt, t0:t0 + 512],
                                                     kT[:, pt, t0:t0 + 512],
                                                     bank[bkb][:, 0:512], OP.mult)
                ve(mkn(), ["kTg", BK[bkb]], ["kTg"], f"{tg}kn{p}_{pt}")
            for tt in range(4):
                bkv = tt % 2
                for dc in range(NDC):
                    def mkv(tt=tt, dc=dc, bkv=bkv):
                        return lambda t: t.matmul(
                            bank[bkv][:, 0:512],
                            xs_bf[:, dc, 1 + tt * 128:1 + tt * 128 + 128],
                            wv[:, dc, 0:512],
                            start=(dc == 0), stop=(dc == NDC - 1))
                    te(mkv(), ["xs_bf", "wvg"], [BK[bkv]], f"{tg}v{p}_{tt}_{dc}")
                def mkve(tt=tt, bkv=bkv, p=p):
                    src2 = bank[bkv][:, 0:512].rearrange("q (h e) -> q h e", e=64)
                    return lambda s: s.activation(
                        v66g[:, p * 4 + tt, 0:8, 0:64], src2, AF.Identity)
                se(mkve(), [BK[bkv]], [("v66g", "d")], f"{tg}ve{p}_{tt}")
        scA.close()
        P.barrier()
        scB = ExitStack()
        SB = lambda n, shp, d=BF16: scB.enter_context(nc.sbuf_tensor(tg + n, shp, d))
        wq = SB("wqg", [128, NDC, 512])
        wao_sl = [SB(f"waog{i}", [64, 8, 128]) for i in range(2)]
        qT = SB("qTg", [128, 4, 512])
        eT2 = SB("eT2", [128, 2, 512])
        outNg = SB("outNg", [64, 8, 512])
        outSbg = SB("outSbg", [64, 512])
        recg = SB("recg", [128, 512])
        rec2 = SB("rec2", [128, 512], F32)
        st8b = SB("st8b", [8, 512], F32)
        r_l8b = SB("r_l8b", [1, 8 * 512])
        sdma(wq.ap(), baWq[:], [("wc", "awq")], ["wqg"], f"{tg}wq")

        for p in range(SEQ // 512):
            t0 = p * 512
            sdma(zf[:, :, 1:513],
                 zH[:, t0:t0 + 512].rearrange("(a q) t -> q a t", q=128),
                 hk("zH", t0, 512) + ["zf"], ["zf"], f"{tg}bin{p}")
            se(lambda s: s.activation(xs_bf[:, :, 1:513], zf[:, :, 1:513], AF.Identity),
               ["zf"], ["xs_bf"], f"{tg}bbf{p}")
            for pt in range(4):
                bkq = pt % 2
                for dc in range(NDC):
                    def mkq(pt=pt, dc=dc, bkq=bkq):
                        return lambda t: t.matmul(bank[bkq][:, 0:512],
                                                  wq[:, dc, pt * 128:(pt + 1) * 128],
                                                  xs_bf[:, dc, 1:513],
                                                  start=(dc == 0), stop=(dc == NDC - 1))
                    te(mkq(), ["wqg", "xs_bf"], [BK[bkq]], f"{tg}q{p}_{pt}_{dc}")
                def mksq(pt=pt, bkq=bkq):
                    return lambda s: s.activation(xsq_bf[:, pt, 0:512],
                                                  bank[bkq][:, 0:512], AF.Square)
                se(mksq(), [BK[bkq]], ["xsq_bf"], f"{tg}qsq{p}_{pt}")
                def mkev(pt=pt, bkq=bkq):
                    return lambda s: s.activation(qT[:, pt, :], bank[bkq][:, 0:512],
                                                  AF.Identity)
                se(mkev(), [BK[bkq]], ["qTg"], f"{tg}qev{p}_{pt}")
            for dc in range(4):
                def mks(dc=dc):
                    return lambda t: t.matmul(bank[6][0:8, 0:512], sel16[:, dc, 0:8],
                                              xsq_bf[:, dc, 0:512],
                                              start=(dc == 0), stop=(dc == 3))
                te(mks(), ["sel16", "xsq_bf"], [BK[6]], f"{tg}ql2{p}_{dc}")
            ve(lambda v: v.tensor_scalar(st8b.ap(), bank[6][0:8, 0:512], 1e-16, None,
                                         OP.max),
               [BK[6]], ["st8b"], f"{tg}qlm{p}")
            se(lambda s: s.activation(st8b.ap(), st8b.ap(), AF.Ln), ["st8b"], ["st8b"],
               f"{tg}qll{p}")
            se(lambda s: s.activation(st8b.ap(), st8b.ap(), AF.Exp, scale=-0.5),
               ["st8b"], ["st8b"], f"{tg}qle{p}")
            gdma(r_l8b[0:1, :].rearrange("o (h t) -> o h t", h=8), st8b.ap(),
                 ["st8b"], ["r_l8b"], f"{tg}qlr{p}")
            for pt in range(4):
                bkb = 4 + pt % 2
                h0, h1 = 2 * pt, 2 * pt + 1
                def mkb0(h0=h0, bkb=bkb):
                    return lambda t: t.matmul(bank[bkb][0:64, 0:512], c16row.ap(),
                                              r_l8b[0:1, h0 * 512:(h0 + 1) * 512],
                                              start=True, stop=True)
                te(mkb0(), ["c16row", "r_l8b"], [BK[bkb]], f"{tg}qbc{p}_{h0}")
                def mkb1(h1=h1, bkb=bkb):
                    return lambda t: t.matmul(bank[bkb][64:128, 0:512], c16row.ap(),
                                              r_l8b[0:1, h1 * 512:(h1 + 1) * 512],
                                              start=True, stop=True)
                te(mkb1(), ["c16row", "r_l8b"], [BK[bkb]], f"{tg}qbc{p}_{h1}")
                def mkn(pt=pt, bkb=bkb):
                    return lambda v: v.tensor_tensor(qT[:, pt, :], qT[:, pt, :],
                                                     bank[bkb][:, 0:512], OP.mult)
                ve(mkn(), ["qTg", BK[bkb]], ["qTg"], f"{tg}qn{p}_{pt}")
            for h in range(8):
                rb = (h % 2) * 64
                ptl = h // 2
                pvb = 6 + (h % 2)
                first = [True]
                for kb in range(8):
                    sb_ = 2 + (kb % 2)
                    for kt in range(4):
                        kc0 = kb * 512 + kt * 128
                        def mksim(kc0=kc0, sb_=sb_, rb=rb, ptl=ptl):
                            return lambda t: t.matmul(
                                bank[sb_][0:128, 0:512],
                                kT[rb:rb + 64, ptl, kc0:kc0 + 128],
                                qT[rb:rb + 64, ptl, 0:512],
                                start=True, stop=True)
                        te(mksim(), ["kTg", "qTg"], [BK[sb_]], f"{tg}s{p}_{h}_{kb}_{kt}")
                        def mke(sb_=sb_, kt=kt):
                            return lambda s: s.activation(eT2[:, sb_ - 2, :],
                                                          bank[sb_][:, 0:512], AF.Exp,
                                                          bias=vneg16[:, 0:1])
                        se(mke(), [BK[sb_], "vneg16"], [("eT2", sb_ - 2)],
                           f"{tg}e{p}_{h}_{kb}_{kt}")
                        def mkpv(kb=kb, kt=kt, h=h, pvb=pvb, sb_=sb_, st=first[0]):
                            return lambda t: t.matmul(
                                bank[pvb][0:65, 0:512],
                                v66g[:, kb * 4 + kt, h, 0:65], eT2[:, sb_ - 2, :],
                                start=st, stop=False, skip_group_check=True)
                        te(mkpv(), [("v66g", "d"), ("v66g", "k"), ("eT2", sb_ - 2)],
                           [BK[pvb]], f"{tg}pv{p}_{h}_{kb}_{kt}")
                        first[0] = False
                def mkcl(pvb=pvb):
                    return lambda v: v.tensor_scalar(rec2[64:65, 0:512],
                                                     bank[pvb][64:65, 0:512],
                                                     1e-6, None, OP.max)
                ve(mkcl(), [BK[pvb]], ["rec2"], f"{tg}cl{p}_{h}")
                def mkrc(pvb=pvb):
                    return lambda v: v.reciprocal(recg[64:65, 0:512], rec2[64:65, 0:512])
                ve(mkrc(), ["rec2"], ["recg"], f"{tg}rc{p}_{h}")
                bkb = 4 + h % 2
                def mkbc(bkb=bkb):
                    return lambda t: t.matmul(bank[bkb][0:64, 0:512],
                                              onesrow[64:65, 0:64],
                                              recg[64:65, 0:512], start=True, stop=True)
                te(mkbc(), ["onesrow", "recg"], [BK[bkb]], f"{tg}bcv{p}_{h}")
                def mkevo(pvb=pvb):
                    return lambda s: s.activation(outSbg.ap(), bank[pvb][0:64, 0:512],
                                                  AF.Identity)
                se(mkevo(), [BK[pvb]], ["outSbg"], f"{tg}evo{p}_{h}")
                def mkno(h=h, bkb=bkb):
                    return lambda v: v.tensor_tensor(outNg[:, h, :], outSbg.ap(),
                                                     bank[bkb][0:64, 0:512], OP.mult)
                ve(mkno(), ["outSbg", BK[bkb]], ["outNg"], f"{tg}no{p}_{h}")
            sdma(zf[:, :, 513:1025], x1H[:, t0:t0 + 512].rearrange("(a q) t -> q a t", q=128),
                 hk("x1H", t0, 512) + ["zf"], ["zf"], f"{tg}xr{p}")
            for dt in range(NDC):
                bky = dt % 2
                wpar = dt % 2
                def mkwod(dt=dt, wpar=wpar):
                    return lambda s: s.dma_start(wao_sl[wpar].ap(), baWo[dt])
                P.add("sq", mkwod(), [("wc", "awo")], [("waog", wpar)], dma=True,
                      tag=f"{tg}wod{p}_{dt}")
                for hh in range(8):
                    def mkwo(hh=hh, bky=bky, wpar=wpar):
                        return lambda t: t.matmul(bank[bky][:, 0:512],
                                                  wao_sl[wpar][:, hh, :], outNg[:, hh, :],
                                                  start=(hh == 0), stop=(hh == 7))
                    te(mkwo(), [("waog", wpar), "outNg"], [BK[bky]],
                       f"{tg}wo{p}_{dt}_{hh}")
                def mkres(dt=dt, bky=bky):
                    return lambda v: v.tensor_tensor(zf[:, dt, 513:1025],
                                                     zf[:, dt, 513:1025],
                                                     bank[bky][:, 0:512], OP.add)
                ve(mkres(), ["zf", BK[bky]], ["zf"], f"{tg}res{p}_{dt}")
            sdma(x2H[:, t0:t0 + 512].rearrange("(a q) t -> q a t", q=128),
                 zf[:, :, 513:1025], ["zf"], hk("x2H", t0, 512), f"{tg}xw{p}")
        scB.close()
        sc.close()
        P.barrier()


    if stage >= 2:
        nlay = DEPTH if stage >= 4 else 1
        for li in range(nlay):
            attn_sublayer(li, f"A{li}_")
            if stage >= 3:
                ffn_sublayer(li, f"F{li}_")
    if stage >= 5:
        combine_pass(xT, "xT", x1H, "x1H", "c1_")
    if stage >= 6:
        ln_pass(x1H, "x1H", zH, "zH", "ln1")
        global_attn("G_")
    if stage >= 7:
        ln_pass(x2H, "x2H", zH, "zH", "ln2")
        for li in range(DEPTH):
            attn_sublayer(li, f"B{li}_")
            ffn_sublayer(li, f"C{li}_")
        combine_pass(x2H, "x2H", outT, "outT", "c2_")

    out_keys = []
    if dump in ("x1", "x2"):
        srcH = x1H if dump == "x1" else x2H
        for p in range(SEQ // PH_F):
            gdma(dumpH[:, p * PH_F:(p + 1) * PH_F], srcH[:, p * PH_F:(p + 1) * PH_F],
                 hk(dump + "H", p * PH_F, PH_F), [("dump", p)], f"dump{p}")
            out_keys.append(("dump", p))
    if dump == "z":
        for p in range(SEQ // PH_F):
            gdma(dumpH[:, p * PH_F:(p + 1) * PH_F], zH[:, p * PH_F:(p + 1) * PH_F],
                 hk("zH", p * PH_F, PH_F), [("dump", p)], f"dump{p}")
            out_keys.append(("dump", p))
    if stage >= 7:
        out_keys += [("hbm", "outT", i) for i in range(SEQ // PH_A)]
    else:
        for p in range(SEQ // PH_F):
            gdma(outT[:, p * PH_F:(p + 1) * PH_F], zH[:, p * PH_F:(p + 1) * PH_F],
                 hk("zH", p * PH_F, PH_F), [("out", p)], f"outcp{p}")
            out_keys.append(("out", p))
    P.add("sq", lambda s: None, out_keys, [("fin",)], tag="final_wait")

    P.analyze()
    print("PROG:", P.stats(), flush=True)
    with nc.Block() as block, nc.allow_low_precision(
            reason="bf16 rounding of normalizer rows; fp32 accumulation elsewhere"):
        P.emit_all(block, sems)
    es.close()
    return nc


def prep_inputs(inputs):
    x = np.asarray(inputs["L0_attn_x"], dtype=np.float32)
    Wqkv = np.asarray(inputs["L0_ffn_Wqkv"], dtype=np.float32)
    perm = np.concatenate([h * 64 + PERM64 for h in range(16)])
    fWqk = np.concatenate([Wqkv[:, :, :1024][:, :, perm],
                           Wqkv[:, :, 1024:2048][:, :, perm]], axis=2)
    fWqk = np.ascontiguousarray(
        fWqk.reshape(DEPTH, 8, 128, 16, 128).transpose(0, 3, 2, 1, 4))
    fWv = np.ascontiguousarray(
        Wqkv[:, :, 2048:].reshape(DEPTH, 8, 128, 1024).transpose(0, 2, 1, 3))
    Wo = np.asarray(inputs["L0_ffn_Wo"], dtype=np.float32)
    fWo = np.ascontiguousarray(
        Wo.reshape(DEPTH, 8, 128, 8, 128).transpose(0, 3, 2, 1, 4))
    W1 = np.asarray(inputs["L0_ffn_W1"], dtype=np.float32)
    b1 = np.asarray(inputs["L0_ffn_b1"], dtype=np.float32)
    W2 = np.asarray(inputs["L0_ffn_W2"], dtype=np.float32)
    padc = 128 * KC - 2730
    pad6 = 6 * 512 - 2730
    W1a = np.pad(W1[:, :, :2730], ((0, 0), (0, 0), (0, pad6)))
    W1g = np.pad(W1[:, :, 2730:], ((0, 0), (0, 0), (0, pad6)))
    fW1a = np.ascontiguousarray(
        W1a.reshape(DEPTH, 8, 128, 6, 512).transpose(0, 3, 2, 1, 4))
    fW1g = np.ascontiguousarray(
        W1g.reshape(DEPTH, 8, 128, 6, 512).transpose(0, 3, 2, 1, 4))
    W2p = np.pad(W2, ((0, 0), (0, padc), (0, 0)))
    fW2 = np.ascontiguousarray(
        W2p.reshape(DEPTH, KC, 128, 8, 128).transpose(0, 3, 2, 1, 4))
    aWq_ = np.asarray(inputs["L0_attn_Wq"], dtype=np.float32)
    aWkv_ = np.asarray(inputs["L0_attn_Wkv"], dtype=np.float32)
    aWo_ = np.asarray(inputs["L0_attn_Wo"], dtype=np.float32)
    aWq_t = np.ascontiguousarray(aWq_.reshape(8, 128, 512).transpose(1, 0, 2))
    aWk_t = np.ascontiguousarray(aWkv_[:, 0:512].reshape(8, 128, 512).transpose(1, 0, 2))
    aWv_t = np.ascontiguousarray(aWkv_[:, 512:1024].reshape(8, 128, 512).transpose(1, 0, 2))
    aWo_t = np.ascontiguousarray(aWo_.reshape(8, 64, 8, 128).transpose(2, 1, 0, 3))
    shared = {
        "aWq": aWq_t, "aWk": aWk_t, "aWv": aWv_t, "aWo": aWo_t,
        "fWqk": fWqk, "fWv": fWv, "fWo": fWo,
        "fW1a": fW1a, "fW1g": fW1g, "fW2": fW2,
        "fqs": np.ascontiguousarray(np.asarray(inputs["L0_ffn_q_scale"], np.float32)[:, PERM64]),
        "fks": np.ascontiguousarray(np.asarray(inputs["L0_ffn_k_scale"], np.float32)[:, PERM64]),
        "fg": np.ascontiguousarray(inputs["L0_ffn_ln_g"], dtype=np.float32),
        "fb": np.ascontiguousarray(inputs["L0_ffn_ln_b"], dtype=np.float32),
        "fb1a": np.ascontiguousarray(np.pad(b1[:, :2730], ((0, 0), (0, padc)))),
        "fb1g": np.ascontiguousarray(np.pad(b1[:, 2730:], ((0, 0), (0, padc)))),
        "fb2": np.ascontiguousarray(inputs["L0_ffn_b2"], dtype=np.float32),
    }
    in_maps = []
    for c in range(8):
        m = dict(shared)
        m["xT"] = np.ascontiguousarray(x[c].T)
        in_maps.append(m)
    return in_maps


def kernel(**inputs):
    in_maps = prep_inputs(inputs)
    nc = build_program(stage=99)
    res = run_bass_kernel_spmd(nc, in_maps, core_ids=list(range(8)))
    return np.stack([np.ascontiguousarray(r["outT"].T) for r in res.results])


# version 06 (final)
# speedup vs baseline: 1.0334x; 1.0334x over previous
import numpy as np
import ml_dtypes
from contextlib import ExitStack
import concourse.bass as bass
import concourse.mybir as mybir
from concourse.bass_utils import run_bass_kernel_spmd

F32 = mybir.dt.float32
BF16 = mybir.dt.bfloat16
AF = mybir.ActivationFunctionType
OP = mybir.AluOpType

SEQ = 4096
D = 1024
NDC = 8
DEPTH = 6
NEG = -1e30
PH_A = 512
PH_F = 1024
KC = 22
NDMASEM = 8
ATTN_CUT = 99

PERM64 = np.concatenate([np.arange(0, 16), np.arange(32, 48),
                         np.arange(16, 32), np.arange(48, 64)])


class _Op:
    __slots__ = ("idx", "stream", "emit", "reads", "writes", "dma",
                 "deps", "sdeps", "waits", "ticket", "marked", "tag",
                 "needs_drain", "slot")


class Prog:
    STREAMS = ("te", "ve", "se", "gp", "sq")

    def __init__(self):
        self.ops = []
        self.bars = []
        self.preset = set()

    def add(self, stream, emit, reads=(), writes=(), dma=False, tag=""):
        o = _Op()
        o.idx = len(self.ops)
        o.stream = stream
        o.emit = emit
        o.reads = tuple(reads)
        writes = tuple(writes)
        if stream in ("ve", "se"):
            for k in o.reads:
                if isinstance(k, tuple) and len(k) == 2 and k[0] == "bank":
                    writes = writes + (("bankrd", k[1]),)
        o.writes = tuple(writes)
        o.dma = dma
        o.tag = tag
        o.sdeps = []
        o.needs_drain = False
        o.slot = 0
        self.ops.append(o)
        return o

    def barrier(self):
        self.bars.append(len(self.ops))

    def analyze(self):
        last_w = {k: None for k in self.preset}
        readers = {}
        for o in self.ops:
            deps = set()
            for k in o.reads:
                if k not in last_w:
                    raise RuntimeError(f"read-before-write {k} op#{o.idx} {o.tag}")
                w = last_w[k]
                if w is not None:
                    deps.add(w)
                rd = readers.setdefault(k, {})
                if o.dma:
                    rd.setdefault("_dma", []).append(o)
                else:
                    rd[o.stream] = o
            for k in o.writes:
                pseudo = isinstance(k, tuple) and len(k) == 2 and k[0] == "bankrd"
                w = last_w.get(k)
                if w is not None and not (pseudo and w.stream == o.stream):
                    deps.add(w)
                for sk, r in readers.get(k, {}).items():
                    if sk == "_dma":
                        deps.update(r)
                    else:
                        deps.add(r)
                last_w[k] = o
                readers[k] = {}
            deps.discard(o)
            o.deps = [d for d in deps if d.stream != o.stream or d.dma]
            o.sdeps = [d for d in deps if d.stream == o.stream and not d.dma
                       and o.stream in ("ve", "se", "gp")]
        for bidx in self.bars:
            lasts = {}
            for o in self.ops[:bidx]:
                lasts[o.stream] = o
            firsts = {}
            for o in self.ops[bidx:]:
                if o.stream not in firsts:
                    firsts[o.stream] = o
            for s, o in firsts.items():
                for s2, d in lasts.items():
                    if s2 != s:
                        o.deps.append(d)
        for o in self.ops:
            o.marked = o.dma
        for o in self.ops:
            for d in o.deps:
                d.marked = True
            for d in o.sdeps:
                d.marked = True
        cnt = {}
        for o in self.ops:
            o.ticket = None
            if o.marked:
                if o.dma:
                    key = (o.stream, True)
                    n = cnt.get(key, 0)
                    cnt[key] = n + 1
                    o.slot = n % NDMASEM
                    o.ticket = (n // NDMASEM + 1) * 16
                else:
                    key = (o.stream, False)
                    cnt[key] = cnt.get(key, 0) + 1
                    o.slot = 0
                    o.ticket = cnt[key]
        hw = {s: {} for s in self.STREAMS}
        for o in self.ops:
            need = {}
            for d in o.deps:
                sk = (d.stream, d.dma, d.slot)
                if d.ticket > need.get(sk, 0):
                    need[sk] = d.ticket
            for d in o.sdeps:
                sk = (d.stream, d.dma, d.slot)
                if d.ticket > need.get(sk, 0):
                    need[sk] = d.ticket
            o.waits = []
            for sk in sorted(need, key=str):
                c = need[sk]
                if hw[o.stream].get(sk, 0) >= c:
                    continue
                hw[o.stream][sk] = c
                o.waits.append((sk, c))
        for o in self.ops:
            o.needs_drain = False

    def emit_all(self, block, sems):
        per = {s: [o for o in self.ops if o.stream == s] for s in self.STREAMS}
        sections = {"te": block.tensor, "ve": block.vector, "se": block.scalar,
                    "gp": block.gpsimd, "sq": block.sync}

        def mk(stream):
            def body(eng):
                for o in per[stream]:
                    if o.needs_drain:
                        eng.drain()
                    for sk, c in o.waits:
                        eng.wait_ge(sems[sk], c)
                    if o.dma and o.marked and o.ticket > 16:
                        eng.wait_ge(sems[(o.stream, True, o.slot)], o.ticket - 16)
                    instr = o.emit(eng)
                    if o.marked:
                        assert instr is not None, f"op {o.tag} marked, emit->None"
                        instr.then_inc(sems[(o.stream, o.dma, o.slot)],
                                       16 if o.dma else 1)
            return body

        for s in self.STREAMS:
            if per[s]:
                sections[s](mk(s))

    def stats(self):
        by = {}
        for o in self.ops:
            by[o.stream] = by.get(o.stream, 0) + 1
        return (f"ops={len(self.ops)} marked={sum(o.marked for o in self.ops)} "
                f"waits={sum(len(o.waits) for o in self.ops)} by={by}")


def build_program(stage=99, dump=None):
    nc = bass.Bass("TRN2", target_bir_lowering=False)
    P = Prog()
    te = lambda emit, r, w, tag="": P.add("te", emit, r, w, tag=tag)
    ve = lambda emit, r, w, tag="": P.add("ve", emit, r, w, tag=tag)
    se = lambda emit, r, w, tag="": P.add("se", emit, r, w, tag=tag)
    gp = lambda emit, r, w, tag="": P.add("gp", emit, r, w, tag=tag)

    def gdma(dst, src, r, w, tag=""):
        P.add("gp", lambda g: g.dma_start(dst, src), r, w, dma=True, tag=tag)

    def gdma_nc(dst, src, r, w, tag=""):
        P.add("gp", lambda g: g.dma_start(dst, src, allow_slow_non_contiguous=True),
              r, w, dma=True, tag=tag)

    def sdma(dst, src, r, w, tag=""):
        P.add("sq", lambda s: s.dma_start(dst, src), r, w, dma=True, tag=tag)

    dram = lambda name, shape, dt, kind="Internal": nc.dram_tensor(
        name, shape, dt, kind=kind).ap()
    xT = dram("xT", [D, SEQ], F32, "ExternalInput")
    aWq = dram("aWq", [128, NDC, 512], F32, "ExternalInput")
    aWk = dram("aWk", [128, NDC, 512], F32, "ExternalInput")
    aWv = dram("aWv", [128, NDC, 512], F32, "ExternalInput")
    aWo = dram("aWo", [NDC, 64, 8, 128], F32, "ExternalInput")
    fWqk = dram("fWqk", [DEPTH, 16, 128, NDC, 128], F32, "ExternalInput")
    fWv = dram("fWv", [DEPTH, 128, NDC, 1024], F32, "ExternalInput")
    fWo = dram("fWo", [DEPTH, NDC, 128, NDC, 128], F32, "ExternalInput")
    fW1a = dram("fW1a", [DEPTH, 6, 128, NDC, 512], F32, "ExternalInput")
    fW1g = dram("fW1g", [DEPTH, 6, 128, NDC, 512], F32, "ExternalInput")
    fW2 = dram("fW2", [DEPTH, NDC, 128, KC, 128], F32, "ExternalInput")
    fqs = dram("fqs", [DEPTH, 64], F32, "ExternalInput")
    fks = dram("fks", [DEPTH, 64], F32, "ExternalInput")
    fg = dram("fg", [DEPTH, D], F32, "ExternalInput")
    fb = dram("fb", [DEPTH, D], F32, "ExternalInput")
    fb1a = dram("fb1a", [DEPTH, 128 * KC], F32, "ExternalInput")
    fb1g = dram("fb1g", [DEPTH, 128 * KC], F32, "ExternalInput")
    fb2 = dram("fb2", [DEPTH, D], F32, "ExternalInput")
    outT = dram("outT", [D, SEQ], F32, "ExternalOutput")

    zH = dram("zH", [D, SEQ], F32)
    x1H = dram("x1H", [D, SEQ], F32)
    x2H = dram("x2H", [D, SEQ], F32)
    bWqk = dram("bWqk", [DEPTH, 16, 128, NDC, 128], BF16)
    bWv = dram("bWv", [DEPTH, 128, NDC, 1024], BF16)
    bWo = dram("bWo", [DEPTH, NDC, 128, NDC, 128], BF16)
    bW1a = dram("bW1a", [DEPTH, 6, 128, NDC, 512], BF16)
    bW1g = dram("bW1g", [DEPTH, 6, 128, NDC, 512], BF16)
    bW2 = dram("bW2", [DEPTH, NDC, 128, KC, 128], BF16)
    baWq = dram("baWq", [128, NDC, 512], BF16)
    baWk = dram("baWk", [128, NDC, 512], BF16)
    baWv = dram("baWv", [128, NDC, 512], BF16)
    baWo = dram("baWo", [NDC, 64, 8, 128], BF16)

    dumpH = None
    if dump is not None:
        dumpH = dram("dumpH", [D, SEQ], F32, "ExternalOutput")

    NCH = SEQ // PH_A
    P.preset |= {("hbm", "xT", i) for i in range(NCH)}

    def hk(name, t0, n):
        return [("hbm", name, i) for i in range(t0 // PH_A, (t0 + n + PH_A - 1) // PH_A)]

    es = ExitStack()
    A = lambda cm: es.enter_context(cm)
    sb = lambda name, shape, dt=F32: A(nc.sbuf_tensor(name, shape, dt))

    t_csq = sb("t_csq", [128, SEQ], BF16)
    t_snq = sb("t_snq", [128, SEQ], BF16)
    t_csk = sb("t_csk", [128, SEQ], BF16)
    t_snk = sb("t_snk", [128, SEQ], BF16)
    onescol = sb("onescol", [128, 1], BF16)
    onesrow = sb("onesrow", [128, 128], BF16)
    ident = sb("ident", [128, 128], BF16)
    mA1 = sb("mA1", [128, 128], BF16)
    mA2 = sb("mA2", [128, 128], BF16)
    sel16 = sb("sel16", [128, NDC, 16], BF16)
    vneg12 = sb("vneg12", [128, 1], F32)
    vneg16 = sb("vneg16", [128, 1], F32)
    ceps = sb("ceps", [1, 1], F32)
    cepsl2 = sb("cepsl2", [1, 1], F32)
    c16row = sb("c16row", [1, 64], BF16)
    p_qs = sb("p_qs", [128, 1], F32)
    p_ks = sb("p_ks", [128, 1], F32)
    p_g = sb("p_g", [128, NDC], F32)
    p_b = sb("p_b", [128, NDC], F32)
    p_b1a = sb("p_b1a", [128, KC], F32)
    p_b1g = sb("p_b1g", [128, KC], F32)
    p_b2 = sb("p_b2", [128, NDC], F32)
    zf = sb("zf", [128, NDC, PH_F + 1], F32)
    xs_bf = sb("xs_bf", [128, NDC, PH_F + 1], BF16)
    xsq_bf = sb("xsq_bf", [128, NDC, 513], BF16)
    carryb = sb("carryb", [128, NDC, 1], F32)
    r_mu = sb("r_mu", [1, 512], F32)
    r_s2 = sb("r_s2", [1, 512], F32)
    r_var = sb("r_var", [1, 512], F32)
    r_rstd = sb("r_rstd", [1, 512], BF16)
    r_mrs = sb("r_mrs", [1, 512], BF16)

    bank = [A(nc.psum_tensor(f"pb{i}", [128, 512], F32)) for i in range(8)]
    BK = [("bank", i) for i in range(8)]

    sems = {}
    for s in Prog.STREAMS:
        sems[(s, False, 0)] = A(nc.semaphore(f"sem_{s}"))
    for s in ("sq", "gp"):
        for i in range(NDMASEM):
            sems[(s, True, i)] = A(nc.semaphore(f"sem_{s}d{i}"))

    for (dst, src, n) in [(bWqk, fWqk, "wqk"), (bWv, fWv, "wv"), (bWo, fWo, "wo"),
                          (bW1a, fW1a, "w1a"), (bW1g, fW1g, "w1g"),
                          (bW2, fW2, "w2"), (baWq, aWq, "awq"),
                          (baWk, aWk, "awk"), (baWv, aWv, "awv"),
                          (baWo, aWo, "awo")]:
        gdma(dst, src, (), [("wc", n)], tag=f"cast_{n}")

    gp(lambda g: g.memset(onescol.ap(), 1.0), (), ["onescol"], "ones")
    gp(lambda g: g.memset(onesrow.ap(), 1.0), (), ["onesrow"], "onesr")
    gp(lambda g: g.memset(vneg12.ap(), -12.0), (), ["vneg12"], "v12")
    gp(lambda g: g.memset(vneg16.ap(), -16.0), (), ["vneg16"], "v16")
    gp(lambda g: g.memset(ceps.ap(), 1e-5), (), ["ceps"], "eps")
    gp(lambda g: g.memset(c16row.ap(), 16.0), (), ["c16row"], "c16")
    gp(lambda g: g.memset(cepsl2.ap(), 1e-16), (), ["cepsl2"], "eps2")

    tb = ExitStack()
    T = lambda n, s, d=F32: tb.enter_context(nc.sbuf_tensor(n, s, d))
    itr, itc, tmsk = T("itr", [128, 128]), T("itc", [128, 1]), T("tmsk", [128, 128])
    w_i, w_c, w_if, w_lb = T("w_i", [32, 1]), T("w_c", [32, 1]), T("w_if", [32, 1]), T("w_lb", [32, 1])
    w_fr, w_x, w_k = T("w_fr", [32, SEQ]), T("w_x", [32, SEQ]), T("w_k", [32, SEQ])
    w_sin, w_cos = T("w_sin", [32, SEQ]), T("w_cos", [32, SEQ])
    w_o1, w_o2 = T("w_o1", [32, SEQ], BF16), T("w_o2", [32, SEQ], BF16)

    gp(lambda g: g.iota(itr.ap(), [[1, 128]], base=0, channel_multiplier=0,
                        allow_small_or_imprecise_dtypes=True), (), ["itr"], "itr")
    gp(lambda g: g.iota(itc.ap(), [[1, 1]], base=0, channel_multiplier=1,
                        allow_small_or_imprecise_dtypes=True), (), ["itc"], "itc")
    ve(lambda v: v.tensor_scalar(ident.ap(), itr.ap(), itc.ap(), None, OP.is_equal),
       ["itr", "itc"], ["ident"], "ident")
    ve(lambda v: v.tensor_scalar(tmsk.ap(), itr.ap(), itc.ap(), None, OP.is_gt),
       ["itr", "itc"], ["tmsk"], "m1a")
    ve(lambda v: v.tensor_scalar(mA1.ap(), tmsk.ap(), NEG, None, OP.mult),
       ["tmsk"], ["mA1"], "m1b")
    ve(lambda v: v.tensor_scalar(tmsk.ap(), itr.ap(), itc.ap(), None, OP.is_lt),
       ["itr", "itc"], ["tmsk"], "m2a")
    ve(lambda v: v.tensor_scalar(mA2.ap(), tmsk.ap(), NEG, None, OP.mult),
       ["tmsk"], ["mA2"], "m2b")
    hcol = tb.enter_context(nc.sbuf_tensor("hcol", [128, 1], F32))
    rge = tb.enter_context(nc.sbuf_tensor("rge", [128, 1], F32))
    ve(lambda v: v.tensor_scalar(rge.ap(), itc.ap(), 64.0, None, OP.is_ge),
       ["itc"], ["rge"], "rge")
    for dc in range(NDC):
        def mh(dc=dc):
            return lambda v: v.tensor_scalar(hcol.ap(), rge.ap(), 1.0, float(2 * dc),
                                             OP.mult, OP.add)
        ve(mh(), ["rge"], ["hcol"], f"hc{dc}")
        def ms(dc=dc):
            return lambda v: v.tensor_scalar(sel16[:, dc, :], itr[:, 0:16],
                                             hcol.ap(), None, OP.is_equal)
        ve(ms(), ["hcol", "itr"], ["sel16"], f"s16_{dc}")

    gp(lambda g: g.iota(w_i.ap(), [[1, 1]], base=0, channel_multiplier=1,
                        allow_small_or_imprecise_dtypes=True), (), ["w_i"], "w_i")
    gp(lambda g: g.iota(w_fr.ap(), [[1, SEQ]], base=0, channel_multiplier=0,
                        allow_small_or_imprecise_dtypes=True), (), ["w_fr"], "w_t")
    gp(lambda g: g.memset(w_c.ap(), 10000.0), (), ["w_c"], "c4")
    se(lambda s: s.activation(w_c.ap(), w_c.ap(), AF.Ln), ["w_c"], ["w_c"], "ln4")
    ve(lambda v: v.tensor_scalar(w_if.ap(), w_i.ap(), -1.0 / 32.0, None, OP.mult),
       ["w_i"], ["w_if"], "nx")
    se(lambda s: s.activation(w_if.ap(), w_if.ap(), AF.Exp, scale=w_c.ap()),
       ["w_if", "w_c"], ["w_if"], "invf")
    ve(lambda v: v.tensor_scalar(w_fr.ap(), w_fr.ap(), w_if.ap(), None, OP.mult),
       ["w_fr", "w_if"], ["w_fr"], "fr")

    TWO23 = 8388608.0
    INV2PI = float(np.float32(1.0 / (2 * np.pi)))
    c2pi = 2 * np.pi
    C1 = float(np.float32(int(c2pi * 2**10) / 2**10))
    C2 = float(np.float32(c2pi - C1))
    C3 = float(c2pi - C1 - np.float64(C2))
    PI2 = float(np.float32(np.pi / 2))
    PI2L = float(np.pi / 2 - np.float64(PI2))

    def sin_of(out, shift, tg):
        xin, xk = w_fr, "w_fr"
        if shift:
            ve(lambda v: v.tensor_scalar(w_x.ap(), w_fr.ap(), PI2, PI2L, OP.add, OP.add),
               ["w_fr"], ["w_x"], tg + "x")
            xin, xk = w_x, "w_x"
        ve(lambda v: v.tensor_scalar(w_k.ap(), xin.ap(), INV2PI, TWO23, OP.mult, OP.add),
           [xk], ["w_k"], tg + "k1")
        ve(lambda v: v.tensor_scalar(w_k.ap(), w_k.ap(), TWO23, None, OP.subtract),
           ["w_k"], ["w_k"], tg + "k")
        ve(lambda v: v.scalar_tensor_tensor(out.ap(), w_k.ap(), -C1, xin.ap(),
                                            OP.mult, OP.add),
           ["w_k", xk], [tg + "out"], tg + "r1")
        ve(lambda v: v.scalar_tensor_tensor(out.ap(), w_k.ap(), -C2, out.ap(),
                                            OP.mult, OP.add),
           ["w_k", tg + "out"], [tg + "out"], tg + "r2")
        ve(lambda v: v.scalar_tensor_tensor(out.ap(), w_k.ap(), -C3, out.ap(),
                                            OP.mult, OP.add),
           ["w_k", tg + "out"], [tg + "out"], tg + "r3")
        T2 = 3.1415925
        TWOPI = float(2 * np.pi)
        ve(lambda v: v.tensor_scalar(w_k.ap(), out.ap(), T2, -TWOPI,
                                     OP.is_gt, OP.mult),
           [tg + "out", "w_k"], ["w_k"], tg + "wg")
        ve(lambda v: v.tensor_tensor(out.ap(), out.ap(), w_k.ap(), OP.add),
           [tg + "out", "w_k"], [tg + "out"], tg + "wga")
        ve(lambda v: v.tensor_scalar(w_k.ap(), out.ap(), -T2, TWOPI,
                                     OP.is_lt, OP.mult),
           [tg + "out", "w_k"], ["w_k"], tg + "wl")
        ve(lambda v: v.tensor_tensor(out.ap(), out.ap(), w_k.ap(), OP.add),
           [tg + "out", "w_k"], [tg + "out"], tg + "wla")
        ve(lambda v: v.tensor_scalar(out.ap(), out.ap(), -T2, T2,
                                     OP.max, OP.min),
           [tg + "out"], [tg + "out"], tg + "cl")
        se(lambda s: s.activation(out.ap(), out.ap(), AF.Sin),
           [tg + "out"], [tg + "out"], tg + "sin")

    sin_of(w_sin, False, "ws")
    sin_of(w_cos, True, "wc")
    A256 = float(np.float32(25.6))
    A256L = float(25.6 - np.float64(A256))
    B896 = float(np.float32(89.6))
    B896L = float(89.6 - np.float64(B896))
    ve(lambda v: v.tensor_scalar(w_lb.ap(), w_i.ap(), 2.0, A256, OP.mult, OP.add),
       ["w_i"], ["w_lb"], "b1")
    ve(lambda v: v.tensor_scalar(w_lb.ap(), w_lb.ap(), A256L, None, OP.add),
       ["w_lb"], ["w_lb"], "b2")
    gp(lambda g: g.memset(w_c.ap(), B896), ["w_c"], ["w_c"], "d1")
    ve(lambda v: v.tensor_scalar(w_c.ap(), w_c.ap(), B896L, None, OP.add),
       ["w_c"], ["w_c"], "d2")
    ve(lambda v: v.reciprocal(w_c.ap(), w_c.ap()), ["w_c"], ["w_c"], "d3")
    ve(lambda v: v.tensor_scalar(w_lb.ap(), w_lb.ap(), w_c.ap(), None, OP.mult),
       ["w_lb", "w_c"], ["w_lb"], "base")
    se(lambda s: s.activation(w_lb.ap(), w_lb.ap(), AF.Ln), ["w_lb"], ["w_lb"], "lnb")
    gp(lambda g: g.iota(w_x.ap(), [[1, SEQ]], base=0, channel_multiplier=0,
                        allow_small_or_imprecise_dtypes=True), ["w_x"], ["w_x"], "t2")
    ve(lambda v: v.tensor_scalar(w_x.ap(), w_x.ap(), -2048.0, 1.0 / 128.0,
                                 OP.add, OP.mult),
       ["w_x"], ["w_x"], "pw")
    ve(lambda v: v.tensor_scalar(w_fr.ap(), w_x.ap(), w_lb.ap(), None, OP.mult),
       ["w_x", "w_lb", "w_fr"], ["w_fr"], "plnb")
    se(lambda s: s.activation(w_k.ap(), w_fr.ap(), AF.Exp, scale=-1.0),
       ["w_fr", "w_k"], ["w_k"], "isc")
    se(lambda s: s.activation(w_fr.ap(), w_fr.ap(), AF.Exp),
       ["w_fr"], ["w_fr"], "sc")

    def combine(outt, outk, srcA, srcAk, srcB, srcBk, scl, tg):
        ve(lambda v: v.scalar_tensor_tensor(outt.ap(), srcA.ap(), scl, srcB.ap(),
                                            OP.mult, OP.mult),
           [srcAk, srcBk], [outk], tg)

    def replicate_sn(tbl, tblk, kpos, kneg):
        for blk in range(2):
            r0 = blk * 64
            for (dst0, src, sk, s0) in ((0, w_o2, kneg, 0), (16, w_o1, kpos, 0),
                                        (32, w_o2, kneg, 16), (48, w_o1, kpos, 16)):
                def mk(dst0=dst0, src=src, s0=s0, r0=r0, tbl=tbl):
                    return lambda g: g.dma_start(tbl[r0 + dst0:r0 + dst0 + 16, :],
                                                 src[s0:s0 + 16, :])
                P.add("gp", mk(), [sk], [tblk], dma=True, tag=f"rep{tblk}{r0+dst0}")

    def replicate_cs(tbl, tblk, kcs):
        for blk in range(2):
            r0 = blk * 64
            for (dst0, s0) in ((0, 0), (16, 0), (32, 16), (48, 16)):
                def mk(dst0=dst0, s0=s0, r0=r0, tbl=tbl):
                    return lambda g: g.dma_start(tbl[r0 + dst0:r0 + dst0 + 16, :],
                                                 w_o1[s0:s0 + 16, :])
                P.add("gp", mk(), [kcs], [tblk], dma=True, tag=f"rep{tblk}{r0+dst0}")

    combine(w_o1, "w_o1", w_cos, "wcout", w_fr, "w_fr", 8.0, "csq32")
    replicate_cs(t_csq, "t_csq", "w_o1")
    combine(w_o1, "w_o1", w_sin, "wsout", w_fr, "w_fr", 8.0, "snqp")
    combine(w_o2, "w_o2", w_sin, "wsout", w_fr, "w_fr", -8.0, "snqn")
    replicate_sn(t_snq, "t_snq", "w_o1", "w_o2")
    combine(w_o1, "w_o1", w_cos, "wcout", w_k, "w_k", 1.0, "csk32")
    replicate_cs(t_csk, "t_csk", "w_o1")
    combine(w_o1, "w_o1", w_sin, "wsout", w_k, "w_k", 1.0, "snkp")
    combine(w_o2, "w_o2", w_sin, "wsout", w_k, "w_k", -1.0, "snkn")
    replicate_sn(t_snk, "t_snk", "w_o1", "w_o2")

    P.barrier()
    tb.close()

    def ln_stats_chunk(src_bf, sq_bf, ksrc, ksq, coff, coff_sq, n, shifted, tg):
        for st, (buf, kin, bk, cbase) in enumerate(
                ((src_bf, ksrc, 6, coff), (sq_bf, ksq, 7, coff_sq))):
            for dc in range(NDC):
                off = cbase + (0 if (shifted and dc >= 4) else 1)
                def mk(dc=dc, off=off, bk=bk, buf=buf, n=n):
                    return lambda t: t.matmul(bank[bk][0:1, 0:n], onescol.ap(),
                                              buf[:, dc, off:off + n],
                                              start=(dc == 0), stop=(dc == NDC - 1))
                te(mk(), [kin, "onescol"], [BK[bk]], f"{tg}st{st}dc{dc}")
        se(lambda s: s.activation(r_mu[0:1, 0:n], bank[6][0:1, 0:n], AF.Identity,
                                  scale=1.0 / 1024.0),
           [BK[6]], ["r_mu"], f"{tg}mu")
        se(lambda s: s.activation(r_s2[0:1, 0:n], bank[7][0:1, 0:n], AF.Identity,
                                  scale=1.0 / 1024.0),
           [BK[7]], ["r_s2"], f"{tg}s2")
        ve(lambda v: v.tensor_tensor(r_var[0:1, 0:n], r_mu[0:1, 0:n],
                                     r_mu[0:1, 0:n], OP.mult),
           ["r_mu"], ["r_var"], f"{tg}mu2")
        ve(lambda v: v.tensor_tensor(r_var[0:1, 0:n], r_s2[0:1, 0:n],
                                     r_var[0:1, 0:n], OP.subtract),
           ["r_s2", "r_var"], ["r_var"], f"{tg}var")
        se(lambda s: s.activation(r_var[0:1, 0:n], r_var[0:1, 0:n], AF.Ln,
                                  bias=ceps[0:1, :]),
           ["r_var", "ceps"], ["r_var"], f"{tg}ln")
        se(lambda s: s.activation(r_rstd[0:1, 0:n], r_var[0:1, 0:n], AF.Exp,
                                  scale=-0.5),
           ["r_var"], ["r_rstd"], f"{tg}rstd")
        ve(lambda v: v.tensor_tensor(r_mrs[0:1, 0:n], r_mu[0:1, 0:n],
                                     r_rstd[0:1, 0:n], OP.mult),
           ["r_mu", "r_rstd"], ["r_mrs"], f"{tg}mrs")

    def ln_bcast(n, tg):
        te(lambda t: t.matmul(bank[4][:, 0:n], onesrow[0:1, :], r_rstd[0:1, 0:n],
                              start=True, stop=True),
           ["onesrow", "r_rstd"], [BK[4]], f"{tg}bcA")
        te(lambda t: t.matmul(bank[5][:, 0:n], onesrow[0:1, :], r_mrs[0:1, 0:n],
                              start=True, stop=True),
           ["onesrow", "r_mrs"], [BK[5]], f"{tg}bcB")

    def ln_pass(xin, xin_name, zout, zout_name, tg):
        for p in range(SEQ // PH_F):
            t0 = p * PH_F
            sdma(zf[:, :, 1:PH_F + 1],
                 xin[:, t0:t0 + PH_F].rearrange("(a q) t -> q a t", q=128),
                 hk(xin_name, t0, PH_F), ["zf"], f"{tg}in{p}")
            se(lambda s: s.activation(xs_bf[:, :, 1:PH_F + 1], zf[:, :, 1:PH_F + 1],
                                      AF.Identity),
               ["zf"], ["xs_bf"], f"{tg}bf{p}")
            for c in range(PH_F // 512):
                co = c * 512
                def mksq(co=co):
                    return lambda s: s.activation(xsq_bf[:, :, 1:513],
                                                  zf[:, :, co + 1:co + 513], AF.Square)
                se(mksq(), ["zf"], ["xsq_bf"], f"{tg}sq{p}_{c}")
                ln_stats_chunk(xs_bf, xsq_bf, "xs_bf", "xsq_bf", co, 0, 512, False,
                               f"{tg}p{p}c{c}_")
                ln_bcast(512, f"{tg}p{p}c{c}_")
                bcA = bank[4][:, 0:512].unsqueeze(1).broadcast_to([128, NDC, 512])
                bcB = bank[5][:, 0:512].unsqueeze(1).broadcast_to([128, NDC, 512])
                def mk1(co=co, bcA=bcA):
                    return lambda v: v.tensor_tensor(zf[:, :, 1 + co:513 + co],
                                                     zf[:, :, 1 + co:513 + co],
                                                     bcA, OP.mult)
                ve(mk1(), ["zf", BK[4]], ["zf"], f"{tg}ap1_{p}_{c}")
                def mk2(co=co, bcB=bcB):
                    return lambda v: v.tensor_tensor(zf[:, :, 1 + co:513 + co],
                                                     zf[:, :, 1 + co:513 + co],
                                                     bcB, OP.subtract)
                ve(mk2(), ["zf", BK[5]], ["zf"], f"{tg}ap2_{p}_{c}")
            sdma(zout[:, t0:t0 + PH_F].rearrange("(a q) t -> q a t", q=128),
                 zf[:, :, 1:PH_F + 1], ["zf"], hk(zout_name, t0, PH_F), f"{tg}out{p}")

    if stage >= 1:
        ln_pass(xT, "xT", zH, "zH", "ln0")
    def attn_sublayer(li, tg):
        sc = ExitStack()
        S = lambda n, shp, d=BF16: sc.enter_context(nc.sbuf_tensor(tg + n, shp, d))
        wqk_sl = [S(f"wqk{i}", [128, NDC, 128]) for i in range(2)]
        wv = S("wv", [128, NDC, 1024])
        wo_sl = [S(f"wo{i}", [128, NDC, 128]) for i in range(2)]
        q_bf = S("q_bf", [128, NDC, 512])
        k_bf = S("k_bf", [128, NDC, 512])
        qr_bf = S("qr_bf", [128, NDC, 512])
        kr_bf = S("kr_bf", [128, NDC, 3, 256])
        v66 = S("v66", [128, 3, 2, 16, 65])
        qsh = S("qsh", [128, 512])
        qm = S("qm", [128, 512])
        st_sb = S("st_sb", [16, 512], F32)
        r_l2 = S("r_l2", [65, 8 * 512])
        eT = S("eT", [128, 2, 1536])
        outSb = S("outSb", [128, 512])
        rec_sb = S("rec_sb", [128, 512])
        rec_f = S("rec_f", [128, 512], F32)
        outN = S("outN", [128, NDC, 512])

        gp(lambda g: g.memset(v66[:, :, :, :, 64:65], 1.0), (), [("v66", "k")],
           f"{tg}v66o")
        gdma(p_qs[0:64, 0:1], fqs[li, :], (), ["p_qs"], f"{tg}qs0")
        gdma(p_qs[64:128, 0:1], fqs[li, :], ["p_qs"], ["p_qs"], f"{tg}qs1")
        gdma(p_ks[0:64, 0:1], fks[li, :], (), ["p_ks"], f"{tg}ks0")
        gdma(p_ks[64:128, 0:1], fks[li, :], ["p_ks"], ["p_ks"], f"{tg}ks1")
        sdma(wv.ap(), bWv[li], [("wc", "wv")], ["wv"], f"{tg}wv")
        gp(lambda g: g.memset(zf[:, :, 0:1], 0.0), ["zf"], ["zf"], f"{tg}zc")

        NPH = SEQ // PH_A
        for p in range(NPH):
            t0 = p * PH_A
            sdma(zf[:, :, 1:PH_A + 1],
                 zH[:, t0:t0 + PH_A].rearrange("(a q) t -> q a t", q=128),
                 hk("zH", t0, PH_A) + ["zf"], ["zf"], f"{tg}in{p}")
            if p > 0:
                ve(lambda v: v.tensor_copy(zf[:, :, 0:1], carryb.ap()),
                   ["carryb", "zf"], ["zf"], f"{tg}rcar{p}")
            ve(lambda v: v.tensor_copy(carryb.ap(), zf[:, :, PH_A:PH_A + 1]),
               ["zf"], ["carryb"], f"{tg}ccar{p}")
            se(lambda s: s.activation(xs_bf[:, :, 0:PH_A + 1], zf[:, :, 0:PH_A + 1],
                                      AF.Identity),
               ["zf"], ["xs_bf"], f"{tg}bf{p}")

            for grp, (dst, pvec, kpv, tcs, tsn, tcsk, tsnk) in enumerate(
                    ((q_bf, p_qs, "p_qs", t_csq, t_snq, None, None),
                     (k_bf, p_ks, "p_ks", t_csk, t_snk, None, None))):
                cs_t = t_csq if grp == 0 else t_csk
                sn_t = t_snq if grp == 0 else t_snk
                kcs = "t_csq" if grp == 0 else "t_csk"
                ksn = "t_snq" if grp == 0 else "t_snk"
                kdst = "q_bf" if grp == 0 else "k_bf"
                for pt in range(NDC):
                    ft = grp * 8 + pt
                    par = pt % 2
                    bkq = par
                    def mkw(ft=ft, par=par):
                        return lambda s: s.dma_start(wqk_sl[par].ap(), bWqk[li, ft])
                    P.add("sq", mkw(), [("wc", "wqk")],
                          [("wqk_sl", par)], dma=True, tag=f"{tg}w{p}_{ft}")
                    for dc in range(NDC):
                        off = 0 if dc >= 4 else 1
                        def mkm(dc=dc, off=off, bkq=bkq, par=par):
                            return lambda t: t.matmul(
                                bank[bkq][:, 0:512], wqk_sl[par][:, dc, :],
                                xs_bf[:, dc, off:off + 512],
                                start=(dc == 0), stop=(dc == NDC - 1))
                        te(mkm(), [("wqk_sl", par), "xs_bf"], [BK[bkq]],
                           f"{tg}qk{p}_{ft}_{dc}")
                    def mke1(pt=pt, bkq=bkq):
                        return lambda s: s.activation(xsq_bf[:, pt, 0:512],
                                                      bank[bkq][:, 0:512], AF.Square)
                    se(mke1(), [BK[bkq]], ["xsq_bf"], f"{tg}sq{p}_{ft}")
                    def mke2(pt=pt, bkq=bkq, dst=dst, pvec=pvec):
                        return lambda s: s.activation(dst[:, pt, 0:512],
                                                      bank[bkq][:, 0:512], AF.Identity,
                                                      scale=pvec[:, 0:1])
                    se(mke2(), [BK[bkq], kpv], [kdst], f"{tg}ev{p}_{ft}")
                if ATTN_CUT < 2:
                    continue
                for dc in range(NDC):
                    def mks(dc=dc):
                        return lambda t: t.matmul(bank[6][0:16, 0:512],
                                                  sel16[:, dc, :], xsq_bf[:, dc, 0:512],
                                                  start=(dc == 0), stop=(dc == NDC - 1))
                    te(mks(), ["sel16", "xsq_bf"], [BK[6]], f"{tg}l2s{p}_{grp}_{dc}")
                ve(lambda v: v.tensor_scalar(st_sb.ap(), bank[6][0:16, 0:512],
                                             1e-16, None, OP.max),
                   [BK[6]], ["st_sb"], f"{tg}l2m{p}_{grp}")
                se(lambda s: s.activation(st_sb.ap(), st_sb.ap(), AF.Ln),
                   ["st_sb"], ["st_sb"], f"{tg}l2l{p}_{grp}")
                se(lambda s: s.activation(st_sb.ap(), st_sb.ap(), AF.Exp, scale=-0.5),
                   ["st_sb"], ["st_sb"], f"{tg}l2e{p}_{grp}")
                gdma(r_l2[0:1, :].rearrange("o (s t) -> o s t", s=8),
                     st_sb[0:8, :], ["st_sb"], ["r_l2"], f"{tg}l2r{p}_{grp}")
                gdma(r_l2[64:65, :].rearrange("o (s t) -> o s t", s=8),
                     st_sb[8:16, :], ["st_sb"], ["r_l2"], f"{tg}l2s{p}_{grp}")
                for pt in range(NDC):
                    bkb = 4 + pt % 2
                    h0, h1 = 2 * pt, 2 * pt + 1
                    def mkb0(h0=h0, bkb=bkb):
                        r, c = (h0 // 8) * 64, (h0 % 8) * 512
                        return lambda t: t.matmul(bank[bkb][0:64, 0:512],
                                                  onesrow[r:r + 1, 0:64],
                                                  r_l2[r:r + 1, c:c + 512],
                                                  start=True, stop=True)
                    te(mkb0(), ["onesrow", "r_l2"], [BK[bkb]], f"{tg}bc{p}_{grp}_{h0}")
                    def mkb1(h1=h1, bkb=bkb):
                        r, c = (h1 // 8) * 64, (h1 % 8) * 512
                        return lambda t: t.matmul(bank[bkb][64:128, 0:512],
                                                  onesrow[r:r + 1, 0:64],
                                                  r_l2[r:r + 1, c:c + 512],
                                                  start=True, stop=True)
                    te(mkb1(), ["onesrow", "r_l2"], [BK[bkb]], f"{tg}bc{p}_{grp}_{h1}")
                    def mkn(pt=pt, bkb=bkb, dst=dst):
                        return lambda v: v.tensor_tensor(dst[:, pt, :], dst[:, pt, :],
                                                         bank[bkb][:, 0:512], OP.mult)
                    ve(mkn(), [kdst, BK[bkb]], [kdst], f"{tg}n{p}_{grp}_{pt}")
                    SHUF = [(i + 16) % 32 for i in range(32)]
                    def mksh(pt=pt, dst=dst):
                        return lambda v: v.stream_shuffle(qsh.ap(), dst[:, pt, :], SHUF)
                    ve(mksh(), [kdst], ["qsh"], f"{tg}sh{p}_{grp}_{pt}")
                    def mkqm(pt=pt, dst=dst, cs_t=cs_t, t0=t0):
                        return lambda v: v.tensor_tensor(qm.ap(), dst[:, pt, :],
                                                         cs_t[:, t0:t0 + 512], OP.mult)
                    ve(mkqm(), [kdst, kcs], ["qm"], f"{tg}m{p}_{grp}_{pt}")
                    def mkrt(pt=pt, sn_t=sn_t, t0=t0):
                        return lambda v: v.tensor_tensor(qsh.ap(), qsh.ap(),
                                                         sn_t[:, t0:t0 + 512], OP.mult)
                    ve(mkrt(), ["qsh", ksn], ["qsh"], f"{tg}r{p}_{grp}_{pt}")
                    if grp == 0:
                        def mka(pt=pt):
                            return lambda v: v.tensor_tensor(qr_bf[:, pt, :], qm.ap(),
                                                             qsh.ap(), OP.add)
                        ve(mka(), ["qm", "qsh"], ["qr_bf"], f"{tg}a{p}_{grp}_{pt}")
                    else:
                        for half in range(2):
                            seg = (2 * p + half) % 3
                            def mka(pt=pt, half=half, seg=seg):
                                return lambda v: v.tensor_tensor(
                                    kr_bf[:, pt, seg, :],
                                    qm[:, half * 256:(half + 1) * 256],
                                    qsh[:, half * 256:(half + 1) * 256], OP.add)
                            ve(mka(), ["qm", "qsh"], [("kr", seg)],
                               f"{tg}a{p}_{grp}_{pt}_{half}")

            for tt in range(4 if ATTN_CUT >= 3 else 0):
                seg = (2 * p + tt // 2) % 3
                tile = tt % 2
                toff = 0 if True else 0
                for half in range(2):
                    bkv = half
                    for dc in range(NDC):
                        off = 0 if dc >= 4 else 1
                        def mkv(tt=tt, dc=dc, off=off, bkv=bkv, half=half):
                            return lambda t: t.matmul(
                                bank[bkv][:, 0:512],
                                xs_bf[:, dc, off + tt * 128: off + tt * 128 + 128],
                                wv[:, dc, half * 512:(half + 1) * 512],
                                start=(dc == 0), stop=(dc == NDC - 1))
                        te(mkv(), ["xs_bf", "wv"], [BK[bkv]], f"{tg}v{p}_{tt}_{half}_{dc}")
                    def mkve(tt=tt, bkv=bkv, half=half, seg=seg, tile=tile):
                        src = bank[bkv][:, 0:512].rearrange("q (h e) -> q h e", e=64)
                        return lambda s: s.activation(
                            v66[:, seg, tile, half * 8:(half + 1) * 8, 0:64], src,
                            AF.Identity)
                    se(mkve(), [BK[bkv]], [("v66", seg)], f"{tg}ve{p}_{tt}_{half}")

            segP = (2 * p - 1) % 3
            segC0 = (2 * p) % 3
            segC1 = (2 * p + 1) % 3
            for j in range(8):
                for sub, h in ((0, 2 * j), (1, 2 * j + 1)):
                    rb = (h % 2) * 64
                    ptl = h // 2
                    pvb = 6 + sub
                    even = (sub == 0)
                    def mm(bk, cols, lhsT, rhs, st, sp, rd, tag):
                        def f(t, bk=bk, cols=cols, lhsT=lhsT, rhs=rhs, st=st, sp=sp):
                            return t.matmul(bank[bk][:, cols[0]:cols[1]],
                                            lhsT, rhs, start=st, stop=sp)
                        te(f, rd, [BK[bk]], tag)
                    krs = lambda seg, tl: kr_bf[rb:rb + 64, ptl, seg,
                                                tl * 128:tl * 128 + 128]
                    qrs = lambda c0, c1: qr_bf[rb:rb + 64, ptl, c0:c1]
                    if p > 0:
                        mm(0, (0, 128), krs(segP, 0), qrs(0, 128), True, False,
                           [("kr", segP), "qr_bf"], f"{tg}m0_{p}{h}")
                        mm(0, (0, 128), ident.ap(), mA1[:, 0:128], False, False,
                           ["ident", "mA1"], f"{tg}k0_{p}{h}")
                        mm(0, (128, 384), krs(segP, 1), qrs(0, 256), False, False,
                           [("kr", segP), "qr_bf"], f"{tg}m1_{p}{h}")
                        mm(0, (256, 384), ident.ap(), mA1[:, 0:128], False, False,
                           ["ident", "mA1"], f"{tg}k1_{p}{h}")
                    mm(0, (384, 512), krs(segC1, 1), qrs(384, 512), (p == 0), False,
                       [("kr", segC1), "qr_bf"], f"{tg}m5_{p}{h}")
                    mm(0, (384, 512), ident.ap(), mA2[:, 0:128], False, True,
                       ["ident", "mA2"], f"{tg}k5_{p}{h}")
                    mm(1, (0, 384), krs(segC0, 0), qrs(0, 384), True, False,
                       [("kr", segC0), "qr_bf"], f"{tg}m2_{p}{h}")
                    mm(1, (0, 128), ident.ap(), mA2[:, 0:128], False, False,
                       ["ident", "mA2"], f"{tg}k2a_{p}{h}")
                    mm(1, (256, 384), ident.ap(), mA1[:, 0:128], False, True,
                       ["ident", "mA1"], f"{tg}k2b_{p}{h}")
                    mm(2, (0, 384), krs(segC0, 1), qrs(128, 512), True, False,
                       [("kr", segC0), "qr_bf"], f"{tg}m3_{p}{h}")
                    mm(2, (0, 128), ident.ap(), mA2[:, 0:128], False, False,
                       ["ident", "mA2"], f"{tg}k3a_{p}{h}")
                    mm(2, (256, 384), ident.ap(), mA1[:, 0:128], False, True,
                       ["ident", "mA1"], f"{tg}k3b_{p}{h}")
                    mm(3, (0, 256), krs(segC1, 0), qrs(256, 512), True, False,
                       [("kr", segC1), "qr_bf"], f"{tg}m4_{p}{h}")
                    mm(3, (0, 128), ident.ap(), mA2[:, 0:128], False, True,
                       ["ident", "mA2"], f"{tg}k4_{p}{h}")
                    if p > 0:
                        def mkeA(sub=sub):
                            return lambda s: s.activation(eT[:, sub, 0:512],
                                                          bank[0][:, 0:512], AF.Exp,
                                                          bias=vneg12[:, 0:1])
                        se(mkeA(), [BK[0], "vneg12"], [("eT", sub)], f"{tg}eA_{p}{h}")
                    else:
                        def mkeA(sub=sub):
                            return lambda s: s.activation(eT[:, sub, 384:512],
                                                          bank[0][:, 384:512], AF.Exp,
                                                          bias=vneg12[:, 0:1])
                        se(mkeA(), [BK[0], "vneg12"], [("eT", sub)], f"{tg}eA_{p}{h}")
                    def mkeB(sub=sub):
                        return lambda s: s.activation(eT[:, sub, 512:896],
                                                      bank[1][:, 0:384], AF.Exp,
                                                      bias=vneg12[:, 0:1])
                    se(mkeB(), [BK[1], "vneg12"], [("eT", sub)], f"{tg}eB_{p}{h}")
                    def mkeC(sub=sub):
                        return lambda s: s.activation(eT[:, sub, 896:1280],
                                                      bank[2][:, 0:384], AF.Exp,
                                                      bias=vneg12[:, 0:1])
                    se(mkeC(), [BK[2], "vneg12"], [("eT", sub)], f"{tg}eC_{p}{h}")
                    def mkeD(sub=sub):
                        return lambda s: s.activation(eT[:, sub, 1280:1536],
                                                      bank[3][:, 0:256], AF.Exp,
                                                      bias=vneg12[:, 0:1])
                    se(mkeD(), [BK[3], "vneg12"], [("eT", sub)], f"{tg}eD_{p}{h}")
                    vb = (lambda sg, tl: v66[:, sg, tl, h, 0:65]) if even else \
                         (lambda sg, tl: v66[:, sg, tl, h, 0:64])
                    ob, oe = (0, 65) if even else (64, 128)
                    def pv(cols, lhsT, ecols, st, rd, tag, ob=ob, oe=oe, pvb=pvb,
                           sub=sub):
                        def f(t, cols=cols, lhsT=lhsT, ecols=ecols, st=st, ob=ob,
                              oe=oe, pvb=pvb, sub=sub):
                            return t.matmul(bank[pvb][ob:oe, cols[0]:cols[1]],
                                            lhsT, eT[:, sub, ecols[0]:ecols[1]],
                                            start=st, stop=False,
                                            skip_group_check=True)
                        te(f, rd + [("v66", "k"), ("eT", sub)], [BK[pvb]], tag)
                    pv((0, 384), vb(segC0, 0), (512, 896), True,
                       [("v66", segC0)], f"{tg}p2_{p}{h}")
                    if p > 0:
                        pv((0, 128), vb(segP, 0), (0, 128), False,
                           [("v66", segP)], f"{tg}p0_{p}{h}")
                        pv((0, 256), vb(segP, 1), (128, 384), False,
                           [("v66", segP)], f"{tg}p1_{p}{h}")
                    pv((128, 384), vb(segC0, 1), (896, 1152), False,
                       [("v66", segC0)], f"{tg}p3a_{p}{h}")
                    pv((384, 512), vb(segC0, 1), (1152, 1280), False,
                       [("v66", segC0)], f"{tg}p3b_{p}{h}")
                    pv((256, 384), vb(segC1, 0), (1280, 1408), False,
                       [("v66", segC1)], f"{tg}p4a_{p}{h}")
                    pv((384, 512), vb(segC1, 0), (1408, 1536), False,
                       [("v66", segC1)], f"{tg}p4b_{p}{h}")
                    pv((384, 512), vb(segC1, 1), (384, 512), False,
                       [("v66", segC1)], f"{tg}p5_{p}{h}")
                    if not even:
                        def rs(cols, ecols, st, tag, pvb=pvb, sub=sub):
                            def f(t, cols=cols, ecols=ecols, st=st, pvb=pvb, sub=sub):
                                return t.matmul(bank[pvb][32:33, cols[0]:cols[1]],
                                                onescol.ap(),
                                                eT[:, sub, ecols[0]:ecols[1]],
                                                start=st, stop=False,
                                                skip_group_check=True)
                            te(f, ["onescol", ("eT", sub)], [BK[pvb]], tag)
                        rs((0, 384), (512, 896), True, f"{tg}r2_{p}{h}")
                        if p > 0:
                            rs((0, 128), (0, 128), False, f"{tg}r0_{p}{h}")
                            rs((0, 256), (128, 384), False, f"{tg}r1_{p}{h}")
                        rs((128, 384), (896, 1152), False, f"{tg}r3a_{p}{h}")
                        rs((384, 512), (1152, 1280), False, f"{tg}r3b_{p}{h}")
                        rs((256, 384), (1280, 1408), False, f"{tg}r4a_{p}{h}")
                        rs((384, 512), (1408, 1536), False, f"{tg}r4b_{p}{h}")
                        rs((384, 512), (384, 512), False, f"{tg}r5_{p}{h}")
                def mkr1a():
                    return lambda s: s.activation(rec_f[64:65, 0:512],
                                                  bank[6][64:65, 0:512], AF.Ln)
                se(mkr1a(), [BK[6]], ["rec_f"], f"{tg}rl0_{p}{j}")
                def mkr1b():
                    return lambda s: s.activation(rec_sb[64:65, 0:512],
                                                  rec_f[64:65, 0:512], AF.Exp,
                                                  scale=-1.0)
                se(mkr1b(), ["rec_f"], ["rec_sb"], f"{tg}rc0_{p}{j}")
                def mkr2a():
                    return lambda s: s.activation(rec_f[32:33, 0:512],
                                                  bank[7][32:33, 0:512], AF.Ln)
                se(mkr2a(), [BK[7]], ["rec_f"], f"{tg}rl1_{p}{j}")
                def mkr2b():
                    return lambda s: s.activation(rec_sb[32:33, 0:512],
                                                  rec_f[32:33, 0:512], AF.Exp,
                                                  scale=-1.0)
                se(mkr2b(), ["rec_f"], ["rec_sb"], f"{tg}rc1_{p}{j}")
                def mkbc0():
                    return lambda t: t.matmul(bank[4][0:64, 0:512],
                                              onesrow[64:65, 0:64],
                                              rec_sb[64:65, 0:512],
                                              start=True, stop=True)
                te(mkbc0(), ["onesrow", "rec_sb"], [BK[4]], f"{tg}bc0_{p}{j}")
                def mkbc1():
                    return lambda t: t.matmul(bank[4][64:128, 0:512],
                                              onesrow[32:33, 0:64],
                                              rec_sb[32:33, 0:512],
                                              start=True, stop=True)
                te(mkbc1(), ["onesrow", "rec_sb"], [BK[4]], f"{tg}bc1_{p}{j}")
                def mkev0():
                    return lambda s: s.activation(outSb[0:64, 0:512],
                                                  bank[6][0:64, 0:512], AF.Identity)
                se(mkev0(), [BK[6]], ["outSb"], f"{tg}ev0_{p}{j}")
                def mkev1():
                    return lambda s: s.activation(outSb[64:128, 0:512],
                                                  bank[7][64:128, 0:512], AF.Identity)
                se(mkev1(), [BK[7]], ["outSb"], f"{tg}ev1_{p}{j}")
                def mkn0(j=j):
                    return lambda v: v.tensor_tensor(outN[0:64, j, 0:512],
                                                     outSb[0:64, 0:512],
                                                     bank[4][0:64, 0:512], OP.mult)
                ve(mkn0(), ["outSb", BK[4]], ["outN"], f"{tg}no0_{p}{j}")
                def mkn1(j=j):
                    return lambda v: v.tensor_tensor(outN[64:128, j, 0:512],
                                                     outSb[64:128, 0:512],
                                                     bank[4][64:128, 0:512], OP.mult)
                ve(mkn1(), ["outSb", BK[4]], ["outN"], f"{tg}no1_{p}{j}")

            for dt in range(NDC if ATTN_CUT >= 12 else 0):
                bky = dt % 2
                wpar = dt % 2
                def mkwd(dt=dt, wpar=wpar):
                    return lambda s: s.dma_start(wo_sl[wpar].ap(), bWo[li, dt])
                P.add("sq", mkwd(), [("wc", "wo")], [("wo_sl", wpar)], dma=True,
                      tag=f"{tg}wod{p}_{dt}")
                for ic in range(NDC):
                    def mkwo(dt=dt, ic=ic, bky=bky, wpar=wpar):
                        return lambda t: t.matmul(bank[bky][:, 0:512],
                                                  wo_sl[wpar][:, ic, :], outN[:, ic, :],
                                                  start=(ic == 0), stop=(ic == NDC - 1))
                    te(mkwo(), [("wo_sl", wpar), "outN"], [BK[bky]], f"{tg}wo{p}_{dt}_{ic}")
                def mkres(dt=dt, bky=bky):
                    return lambda v: v.tensor_tensor(zf[:, dt, 1:513], zf[:, dt, 1:513],
                                                     bank[bky][:, 0:512], OP.add)
                ve(mkres(), ["zf", BK[bky]], ["zf"], f"{tg}res{p}_{dt}")
            sdma(zH[:, t0:t0 + PH_A].rearrange("(a q) t -> q a t", q=128),
                 zf[:, :, 1:PH_A + 1], ["zf"], hk("zH", t0, PH_A), f"{tg}out{p}")

        sc.close()
        P.barrier()


    def ffn_sublayer(li, tg):
        PHF = 512
        sc = ExitStack()
        S = lambda n, shp, d=BF16: sc.enter_context(nc.sbuf_tensor(tg + n, shp, d))
        w1a_sl = [S(f"w1a{i}", [128, NDC, 512]) for i in range(2)]
        w1g_sl = [S(f"w1g{i}", [128, NDC, 512]) for i in range(2)]
        w2_sl = [S(f"w2s{i}", [128, KC, 128]) for i in range(2)]
        xn_bf = S("xn_bf", [128, NDC, PHF])
        h2T = S("h2T", [128, KC, PHF])
        ge = S("ge", [128, 512])
        a2 = S("a2", [128, 512])
        t1f = S("t1f", [128, NDC, 512], F32)

        gdma_nc(p_g.ap(), fg[li].rearrange("(a q) -> q a", q=128), (), ["p_g"], f"{tg}g")
        gdma_nc(p_b.ap(), fb[li].rearrange("(a q) -> q a", q=128), (), ["p_b"], f"{tg}b")
        gdma_nc(p_b2.ap(), fb2[li].rearrange("(a q) -> q a", q=128), (), ["p_b2"], f"{tg}b2")
        gdma_nc(p_b1a.ap(), fb1a[li].rearrange("(k q) -> q k", q=128), (), ["p_b1a"], f"{tg}b1a")
        gdma_nc(p_b1g.ap(), fb1g[li].rearrange("(k q) -> q k", q=128), (), ["p_b1g"], f"{tg}b1g")
        gp(lambda g: g.memset(zf[:, :, 0:1], 0.0), ["zf"], ["zf"], f"{tg}zc")

        for p in range(SEQ // PHF):
            t0 = p * PHF
            sdma(zf[:, :, 1:PHF + 1],
                 zH[:, t0:t0 + PHF].rearrange("(a q) t -> q a t", q=128),
                 hk("zH", t0, PHF) + ["zf"], ["zf"], f"{tg}in{p}")
            if p > 0:
                ve(lambda v: v.tensor_copy(zf[:, :, 0:1], carryb.ap()),
                   ["carryb", "zf"], ["zf"], f"{tg}rcar{p}")
            ve(lambda v: v.tensor_copy(carryb.ap(), zf[:, :, PHF:PHF + 1]),
               ["zf"], ["carryb"], f"{tg}ccar{p}")
            se(lambda s: s.activation(xs_bf[:, :, 0:PHF + 1], zf[:, :, 0:PHF + 1],
                                      AF.Identity),
               ["zf"], ["xs_bf"], f"{tg}bf{p}")
            se(lambda s: s.activation(xsq_bf[:, :, 0:513], zf[:, :, 0:513], AF.Square),
               ["zf"], ["xsq_bf"], f"{tg}sq{p}")
            ln_stats_chunk(xs_bf, xsq_bf, "xs_bf", "xsq_bf", 0, 0, 512, True,
                           f"{tg}p{p}_")
            ln_bcast(512, f"{tg}p{p}_")
            for half, (d0, off) in enumerate(((0, 1), (4, 0))):
                bcA = bank[4][:, 0:512].unsqueeze(1).broadcast_to([128, 4, 512])
                bcB = bank[5][:, 0:512].unsqueeze(1).broadcast_to([128, 4, 512])
                def mk1(d0=d0, off=off, bcA=bcA):
                    return lambda v: v.tensor_tensor(
                        t1f[:, d0:d0 + 4, :], zf[:, d0:d0 + 4, off:off + 512],
                        bcA, OP.mult)
                ve(mk1(), ["zf", BK[4]], ["t1f"], f"{tg}a1_{p}{half}")
                def mk2(d0=d0, bcB=bcB):
                    return lambda v: v.tensor_tensor(
                        t1f[:, d0:d0 + 4, :], t1f[:, d0:d0 + 4, :], bcB, OP.subtract)
                ve(mk2(), ["t1f", BK[5]], ["t1f"], f"{tg}a2_{p}{half}")
            for dc in range(NDC):
                def mk3(dc=dc):
                    return lambda s: s.activation(xn_bf[:, dc, :], t1f[:, dc, :],
                                                  AF.Identity,
                                                  scale=p_g[:, dc:dc + 1],
                                                  bias=p_b[:, dc:dc + 1])
                se(mk3(), ["t1f", "p_g", "p_b"], ["xn_bf"], f"{tg}a3_{p}{dc}")
            for ft in range(KC):
                gidx = ft // 4
                par = gidx % 2
                if ft % 4 == 0:
                    for (sl, bsrc, kw) in ((w1a_sl, bW1a, "w1asl"),
                                           (w1g_sl, bW1g, "w1gsl")):
                        def mkw(sl=sl, bsrc=bsrc, par=par, gidx=gidx):
                            return lambda s: s.dma_start(sl[par].ap(), bsrc[li, gidx])
                        P.add("sq", mkw(), [("wc", kw[:3])], [(kw, par)], dma=True,
                              tag=f"{tg}w1d{p}_{gidx}_{kw}")
                fo = (ft % 4) * 128
                bka, bkg = ft % 2, 2 + ft % 2
                for dc in range(NDC):
                    def mka(dc=dc, fo=fo, par=par, bka=bka):
                        return lambda t: t.matmul(
                            bank[bka][:, 0:512], w1a_sl[par][:, dc, fo:fo + 128],
                            xn_bf[:, dc, :], start=(dc == 0), stop=(dc == NDC - 1))
                    te(mka(), [("w1asl", par), "xn_bf"], [BK[bka]],
                       f"{tg}mA{p}_{ft}_{dc}")
                    def mkg(dc=dc, fo=fo, par=par, bkg=bkg):
                        return lambda t: t.matmul(
                            bank[bkg][:, 0:512], w1g_sl[par][:, dc, fo:fo + 128],
                            xn_bf[:, dc, :], start=(dc == 0), stop=(dc == NDC - 1))
                    te(mkg(), [("w1gsl", par), "xn_bf"], [BK[bkg]],
                       f"{tg}mG{p}_{ft}_{dc}")
                def mkge(ft=ft, bkg=bkg):
                    return lambda s: s.activation(ge.ap(), bank[bkg][:, 0:512],
                                                  AF.Gelu, bias=p_b1g[:, ft:ft + 1])
                se(mkge(), [BK[bkg], "p_b1g"], ["ge"], f"{tg}ge{p}_{ft}")
                def mka2(ft=ft, bka=bka):
                    return lambda s: s.activation(a2.ap(), bank[bka][:, 0:512],
                                                  AF.Identity, bias=p_b1a[:, ft:ft + 1])
                se(mka2(), [BK[bka], "p_b1a"], ["a2"], f"{tg}ae{p}_{ft}")
                def mkh2(ft=ft):
                    return lambda v: v.tensor_tensor(h2T[:, ft, :], a2.ap(), ge.ap(),
                                                     OP.mult)
                ve(mkh2(), ["a2", "ge"], ["h2T"], f"{tg}h2_{p}_{ft}")
            for dt in range(NDC):
                par = dt % 2
                def mkw2(dt=dt, par=par):
                    return lambda s: s.dma_start(w2_sl[par].ap(), bW2[li, dt])
                P.add("sq", mkw2(), [("wc", "w2")], [("w2sl", par)], dma=True,
                      tag=f"{tg}w2d{p}_{dt}")
                bky = 4 + dt % 2
                for kc in range(KC):
                    def mky(dt=dt, kc=kc, par=par, bky=bky):
                        return lambda t: t.matmul(
                            bank[bky][:, 0:512], w2_sl[par][:, kc, :], h2T[:, kc, :],
                            start=(kc == 0), stop=(kc == KC - 1))
                    te(mky(), [("w2sl", par), "h2T"], [BK[bky]], f"{tg}y{p}_{dt}_{kc}")
                def mkres(dt=dt, bky=bky):
                    return lambda v: v.scalar_tensor_tensor(
                        zf[:, dt, 1:513], bank[bky][:, 0:512], p_b2[:, dt:dt + 1],
                        zf[:, dt, 1:513], OP.add, OP.add)
                ve(mkres(), ["zf", BK[bky], "p_b2"], ["zf"], f"{tg}res{p}_{dt}")
            sdma(zH[:, t0:t0 + PHF].rearrange("(a q) t -> q a t", q=128),
                 zf[:, :, 1:PHF + 1], ["zf"], hk("zH", t0, PHF), f"{tg}out{p}")
        sc.close()
        P.barrier()

    def combine_pass(xin, xin_name, xout, xout_name, tg):
        sc = ExitStack()
        zc2 = sc.enter_context(nc.sbuf_tensor(tg + "zc2", [128, NDC, 512], F32))
        for p in range(SEQ // 512):
            t0 = p * 512
            sdma(zf[:, :, 0:512],
                 xin[:, t0:t0 + 512].rearrange("(a q) t -> q a t", q=128),
                 hk(xin_name, t0, 512) + ["zf"], ["zf"], f"{tg}xi{p}")
            sdma(zc2.ap(), zH[:, t0:t0 + 512].rearrange("(a q) t -> q a t", q=128),
                 hk("zH", t0, 512) + (["zc2"] if p > 0 else []), ["zc2"], f"{tg}zi{p}")
            ve(lambda v: v.scalar_tensor_tensor(zf[:, :, 0:512], zc2.ap(), 0.5,
                                                zf[:, :, 0:512], OP.mult, OP.add),
               ["zf", "zc2"], ["zf"], f"{tg}cmb{p}")
            sdma(xout[:, t0:t0 + 512].rearrange("(a q) t -> q a t", q=128),
                 zf[:, :, 0:512], ["zf"], hk(xout_name, t0, 512), f"{tg}xo{p}")
        sc.close()
        P.barrier()

    def global_attn(tg):
        sc = ExitStack()
        S = lambda n, shp, d=BF16: sc.enter_context(nc.sbuf_tensor(tg + n, shp, d))
        kT = S("kTg", [128, 4, SEQ])
        v66g = S("v66g", [128, 32, 8, 65])
        scA = ExitStack()
        SA = lambda n, shp, d=BF16: scA.enter_context(nc.sbuf_tensor(tg + n, shp, d))
        wk = SA("wkg", [128, NDC, 512])
        wv = SA("wvg", [128, NDC, 512])
        st8 = SA("st8", [8, 512], F32)
        r_l8 = SA("r_l8", [1, 8 * 512])
        gp(lambda g: g.memset(v66g[:, :, :, 64:65], 1.0), (), [("v66g", "k")], f"{tg}vo")
        sdma(wk.ap(), baWk[:], [("wc", "awk")], ["wkg"], f"{tg}wk")
        sdma(wv.ap(), baWv[:], [("wc", "awv")], ["wvg"], f"{tg}wv")
        for p in range(SEQ // 512):
            t0 = p * 512
            sdma(zf[:, :, 1:513],
                 zH[:, t0:t0 + 512].rearrange("(a q) t -> q a t", q=128),
                 hk("zH", t0, 512) + ["zf"], ["zf"], f"{tg}ain{p}")
            se(lambda s: s.activation(xs_bf[:, :, 1:513], zf[:, :, 1:513], AF.Identity),
               ["zf"], ["xs_bf"], f"{tg}abf{p}")
            for pt in range(4):
                bkq = pt % 2
                for dc in range(NDC):
                    def mkk(pt=pt, dc=dc, bkq=bkq):
                        return lambda t: t.matmul(bank[bkq][:, 0:512],
                                                  wk[:, dc, pt * 128:(pt + 1) * 128],
                                                  xs_bf[:, dc, 1:513],
                                                  start=(dc == 0), stop=(dc == NDC - 1))
                    te(mkk(), ["wkg", "xs_bf"], [BK[bkq]], f"{tg}k{p}_{pt}_{dc}")
                def mksq(pt=pt, bkq=bkq):
                    return lambda s: s.activation(xsq_bf[:, pt, 0:512],
                                                  bank[bkq][:, 0:512], AF.Square)
                se(mksq(), [BK[bkq]], ["xsq_bf"], f"{tg}ksq{p}_{pt}")
                def mkev(pt=pt, bkq=bkq, t0=t0):
                    return lambda s: s.activation(kT[:, pt, t0:t0 + 512],
                                                  bank[bkq][:, 0:512], AF.Identity)
                se(mkev(), [BK[bkq]], ["kTg"], f"{tg}kev{p}_{pt}")
            for dc in range(4):
                def mks(dc=dc):
                    return lambda t: t.matmul(bank[6][0:8, 0:512], sel16[:, dc, 0:8],
                                              xsq_bf[:, dc, 0:512],
                                              start=(dc == 0), stop=(dc == 3))
                te(mks(), ["sel16", "xsq_bf"], [BK[6]], f"{tg}l2s{p}_{dc}")
            ve(lambda v: v.tensor_scalar(st8.ap(), bank[6][0:8, 0:512], 1e-16, None,
                                         OP.max),
               [BK[6]], ["st8"], f"{tg}l2m{p}")
            se(lambda s: s.activation(st8.ap(), st8.ap(), AF.Ln), ["st8"], ["st8"],
               f"{tg}l2l{p}")
            se(lambda s: s.activation(st8.ap(), st8.ap(), AF.Exp, scale=-0.5),
               ["st8"], ["st8"], f"{tg}l2e{p}")
            gdma(r_l8[0:1, :].rearrange("o (h t) -> o h t", h=8), st8.ap(),
                 ["st8"], ["r_l8"], f"{tg}l2r{p}")
            for pt in range(4):
                bkb = 4 + pt % 2
                h0, h1 = 2 * pt, 2 * pt + 1
                def mkb0(h0=h0, bkb=bkb):
                    return lambda t: t.matmul(bank[bkb][0:64, 0:512],
                                              onesrow[0:1, 0:64],
                                              r_l8[0:1, h0 * 512:(h0 + 1) * 512],
                                              start=True, stop=True)
                te(mkb0(), ["onesrow", "r_l8"], [BK[bkb]], f"{tg}bc{p}_{h0}")
                def mkb1(h1=h1, bkb=bkb):
                    return lambda t: t.matmul(bank[bkb][64:128, 0:512],
                                              onesrow[0:1, 0:64],
                                              r_l8[0:1, h1 * 512:(h1 + 1) * 512],
                                              start=True, stop=True)
                te(mkb1(), ["onesrow", "r_l8"], [BK[bkb]], f"{tg}bc{p}_{h1}")
                def mkn(pt=pt, bkb=bkb, t0=t0):
                    return lambda v: v.tensor_tensor(kT[:, pt, t0:t0 + 512],
                                                     kT[:, pt, t0:t0 + 512],
                                                     bank[bkb][:, 0:512], OP.mult)
                ve(mkn(), ["kTg", BK[bkb]], ["kTg"], f"{tg}kn{p}_{pt}")
            for tt in range(4):
                bkv = tt % 2
                for dc in range(NDC):
                    def mkv(tt=tt, dc=dc, bkv=bkv):
                        return lambda t: t.matmul(
                            bank[bkv][:, 0:512],
                            xs_bf[:, dc, 1 + tt * 128:1 + tt * 128 + 128],
                            wv[:, dc, 0:512],
                            start=(dc == 0), stop=(dc == NDC - 1))
                    te(mkv(), ["xs_bf", "wvg"], [BK[bkv]], f"{tg}v{p}_{tt}_{dc}")
                def mkve(tt=tt, bkv=bkv, p=p):
                    src2 = bank[bkv][:, 0:512].rearrange("q (h e) -> q h e", e=64)
                    return lambda s: s.activation(
                        v66g[:, p * 4 + tt, 0:8, 0:64], src2, AF.Identity)
                se(mkve(), [BK[bkv]], [("v66g", "d")], f"{tg}ve{p}_{tt}")
        scA.close()
        P.barrier()
        scB = ExitStack()
        SB = lambda n, shp, d=BF16: scB.enter_context(nc.sbuf_tensor(tg + n, shp, d))
        wq = SB("wqg", [128, NDC, 512])
        wao_sl = [SB(f"waog{i}", [64, 8, 128]) for i in range(2)]
        qT = SB("qTg", [128, 4, 512])
        eT2 = SB("eT2", [128, 2, 512])
        outNg = SB("outNg", [64, 8, 512])
        outSbg = SB("outSbg", [64, 512])
        recg = SB("recg", [128, 512])
        rec2 = SB("rec2", [128, 512], F32)
        st8b = SB("st8b", [8, 512], F32)
        r_l8b = SB("r_l8b", [1, 8 * 512])
        sdma(wq.ap(), baWq[:], [("wc", "awq")], ["wqg"], f"{tg}wq")

        for p in range(SEQ // 512):
            t0 = p * 512
            sdma(zf[:, :, 1:513],
                 zH[:, t0:t0 + 512].rearrange("(a q) t -> q a t", q=128),
                 hk("zH", t0, 512) + ["zf"], ["zf"], f"{tg}bin{p}")
            se(lambda s: s.activation(xs_bf[:, :, 1:513], zf[:, :, 1:513], AF.Identity),
               ["zf"], ["xs_bf"], f"{tg}bbf{p}")
            for pt in range(4):
                bkq = pt % 2
                for dc in range(NDC):
                    def mkq(pt=pt, dc=dc, bkq=bkq):
                        return lambda t: t.matmul(bank[bkq][:, 0:512],
                                                  wq[:, dc, pt * 128:(pt + 1) * 128],
                                                  xs_bf[:, dc, 1:513],
                                                  start=(dc == 0), stop=(dc == NDC - 1))
                    te(mkq(), ["wqg", "xs_bf"], [BK[bkq]], f"{tg}q{p}_{pt}_{dc}")
                def mksq(pt=pt, bkq=bkq):
                    return lambda s: s.activation(xsq_bf[:, pt, 0:512],
                                                  bank[bkq][:, 0:512], AF.Square)
                se(mksq(), [BK[bkq]], ["xsq_bf"], f"{tg}qsq{p}_{pt}")
                def mkev(pt=pt, bkq=bkq):
                    return lambda s: s.activation(qT[:, pt, :], bank[bkq][:, 0:512],
                                                  AF.Identity)
                se(mkev(), [BK[bkq]], ["qTg"], f"{tg}qev{p}_{pt}")
            for dc in range(4):
                def mks(dc=dc):
                    return lambda t: t.matmul(bank[6][0:8, 0:512], sel16[:, dc, 0:8],
                                              xsq_bf[:, dc, 0:512],
                                              start=(dc == 0), stop=(dc == 3))
                te(mks(), ["sel16", "xsq_bf"], [BK[6]], f"{tg}ql2{p}_{dc}")
            ve(lambda v: v.tensor_scalar(st8b.ap(), bank[6][0:8, 0:512], 1e-16, None,
                                         OP.max),
               [BK[6]], ["st8b"], f"{tg}qlm{p}")
            se(lambda s: s.activation(st8b.ap(), st8b.ap(), AF.Ln), ["st8b"], ["st8b"],
               f"{tg}qll{p}")
            se(lambda s: s.activation(st8b.ap(), st8b.ap(), AF.Exp, scale=-0.5),
               ["st8b"], ["st8b"], f"{tg}qle{p}")
            gdma(r_l8b[0:1, :].rearrange("o (h t) -> o h t", h=8), st8b.ap(),
                 ["st8b"], ["r_l8b"], f"{tg}qlr{p}")
            for pt in range(4):
                bkb = 4 + pt % 2
                h0, h1 = 2 * pt, 2 * pt + 1
                def mkb0(h0=h0, bkb=bkb):
                    return lambda t: t.matmul(bank[bkb][0:64, 0:512], c16row.ap(),
                                              r_l8b[0:1, h0 * 512:(h0 + 1) * 512],
                                              start=True, stop=True)
                te(mkb0(), ["c16row", "r_l8b"], [BK[bkb]], f"{tg}qbc{p}_{h0}")
                def mkb1(h1=h1, bkb=bkb):
                    return lambda t: t.matmul(bank[bkb][64:128, 0:512], c16row.ap(),
                                              r_l8b[0:1, h1 * 512:(h1 + 1) * 512],
                                              start=True, stop=True)
                te(mkb1(), ["c16row", "r_l8b"], [BK[bkb]], f"{tg}qbc{p}_{h1}")
                def mkn(pt=pt, bkb=bkb):
                    return lambda v: v.tensor_tensor(qT[:, pt, :], qT[:, pt, :],
                                                     bank[bkb][:, 0:512], OP.mult)
                ve(mkn(), ["qTg", BK[bkb]], ["qTg"], f"{tg}qn{p}_{pt}")
            for h in range(8):
                rb = (h % 2) * 64
                ptl = h // 2
                pvb = 6 + (h % 2)
                first = [True]
                for kb in range(8):
                    sb_ = 2 + (kb % 2)
                    for kt in range(4):
                        kc0 = kb * 512 + kt * 128
                        def mksim(kc0=kc0, sb_=sb_, rb=rb, ptl=ptl):
                            return lambda t: t.matmul(
                                bank[sb_][0:128, 0:512],
                                kT[rb:rb + 64, ptl, kc0:kc0 + 128],
                                qT[rb:rb + 64, ptl, 0:512],
                                start=True, stop=True)
                        te(mksim(), ["kTg", "qTg"], [BK[sb_]], f"{tg}s{p}_{h}_{kb}_{kt}")
                        def mke(sb_=sb_, kt=kt):
                            return lambda s: s.activation(eT2[:, sb_ - 2, :],
                                                          bank[sb_][:, 0:512], AF.Exp,
                                                          bias=vneg16[:, 0:1])
                        se(mke(), [BK[sb_], "vneg16"], [("eT2", sb_ - 2)],
                           f"{tg}e{p}_{h}_{kb}_{kt}")
                        def mkpv(kb=kb, kt=kt, h=h, pvb=pvb, sb_=sb_, st=first[0]):
                            return lambda t: t.matmul(
                                bank[pvb][0:65, 0:512],
                                v66g[:, kb * 4 + kt, h, 0:65], eT2[:, sb_ - 2, :],
                                start=st, stop=False, skip_group_check=True)
                        te(mkpv(), [("v66g", "d"), ("v66g", "k"), ("eT2", sb_ - 2)],
                           [BK[pvb]], f"{tg}pv{p}_{h}_{kb}_{kt}")
                        first[0] = False
                def mkcl(pvb=pvb):
                    return lambda v: v.tensor_scalar(rec2[64:65, 0:512],
                                                     bank[pvb][64:65, 0:512],
                                                     1e-6, None, OP.max)
                ve(mkcl(), [BK[pvb]], ["rec2"], f"{tg}cl{p}_{h}")
                def mkrl(pvb=pvb):
                    return lambda s: s.activation(rec2[64:65, 0:512],
                                                  rec2[64:65, 0:512], AF.Ln)
                se(mkrl(), ["rec2"], ["rec2"], f"{tg}rl{p}_{h}")
                def mkrc(pvb=pvb):
                    return lambda s: s.activation(recg[64:65, 0:512],
                                                  rec2[64:65, 0:512], AF.Exp,
                                                  scale=-1.0)
                se(mkrc(), ["rec2"], ["recg"], f"{tg}rc{p}_{h}")
                bkb = 4 + h % 2
                def mkbc(bkb=bkb):
                    return lambda t: t.matmul(bank[bkb][0:64, 0:512],
                                              onesrow[64:65, 0:64],
                                              recg[64:65, 0:512], start=True, stop=True)
                te(mkbc(), ["onesrow", "recg"], [BK[bkb]], f"{tg}bcv{p}_{h}")
                def mkevo(pvb=pvb):
                    return lambda s: s.activation(outSbg.ap(), bank[pvb][0:64, 0:512],
                                                  AF.Identity)
                se(mkevo(), [BK[pvb]], ["outSbg"], f"{tg}evo{p}_{h}")
                def mkno(h=h, bkb=bkb):
                    return lambda v: v.tensor_tensor(outNg[:, h, :], outSbg.ap(),
                                                     bank[bkb][0:64, 0:512], OP.mult)
                ve(mkno(), ["outSbg", BK[bkb]], ["outNg"], f"{tg}no{p}_{h}")
            sdma(zf[:, :, 513:1025], x1H[:, t0:t0 + 512].rearrange("(a q) t -> q a t", q=128),
                 hk("x1H", t0, 512) + ["zf"], ["zf"], f"{tg}xr{p}")
            for dt in range(NDC):
                bky = dt % 2
                wpar = dt % 2
                def mkwod(dt=dt, wpar=wpar):
                    return lambda s: s.dma_start(wao_sl[wpar].ap(), baWo[dt])
                P.add("sq", mkwod(), [("wc", "awo")], [("waog", wpar)], dma=True,
                      tag=f"{tg}wod{p}_{dt}")
                for hh in range(8):
                    def mkwo(hh=hh, bky=bky, wpar=wpar):
                        return lambda t: t.matmul(bank[bky][:, 0:512],
                                                  wao_sl[wpar][:, hh, :], outNg[:, hh, :],
                                                  start=(hh == 0), stop=(hh == 7))
                    te(mkwo(), [("waog", wpar), "outNg"], [BK[bky]],
                       f"{tg}wo{p}_{dt}_{hh}")
                def mkres(dt=dt, bky=bky):
                    return lambda v: v.tensor_tensor(zf[:, dt, 513:1025],
                                                     zf[:, dt, 513:1025],
                                                     bank[bky][:, 0:512], OP.add)
                ve(mkres(), ["zf", BK[bky]], ["zf"], f"{tg}res{p}_{dt}")
            sdma(x2H[:, t0:t0 + 512].rearrange("(a q) t -> q a t", q=128),
                 zf[:, :, 513:1025], ["zf"], hk("x2H", t0, 512), f"{tg}xw{p}")
        scB.close()
        sc.close()
        P.barrier()


    if stage >= 2:
        nlay = DEPTH if stage >= 4 else 1
        for li in range(nlay):
            attn_sublayer(li, f"A{li}_")
            if stage >= 3:
                ffn_sublayer(li, f"F{li}_")
    if stage >= 5:
        combine_pass(xT, "xT", x1H, "x1H", "c1_")
    if stage >= 6:
        ln_pass(x1H, "x1H", zH, "zH", "ln1")
        global_attn("G_")
    if stage >= 7:
        ln_pass(x2H, "x2H", zH, "zH", "ln2")
        for li in range(DEPTH):
            attn_sublayer(li, f"B{li}_")
            ffn_sublayer(li, f"C{li}_")
        combine_pass(x2H, "x2H", outT, "outT", "c2_")

    out_keys = []
    if dump in ("x1", "x2"):
        srcH = x1H if dump == "x1" else x2H
        for p in range(SEQ // PH_F):
            gdma(dumpH[:, p * PH_F:(p + 1) * PH_F], srcH[:, p * PH_F:(p + 1) * PH_F],
                 hk(dump + "H", p * PH_F, PH_F), [("dump", p)], f"dump{p}")
            out_keys.append(("dump", p))
    if dump == "z":
        for p in range(SEQ // PH_F):
            gdma(dumpH[:, p * PH_F:(p + 1) * PH_F], zH[:, p * PH_F:(p + 1) * PH_F],
                 hk("zH", p * PH_F, PH_F), [("dump", p)], f"dump{p}")
            out_keys.append(("dump", p))
    if stage >= 7:
        out_keys += [("hbm", "outT", i) for i in range(SEQ // PH_A)]
    else:
        for p in range(SEQ // PH_F):
            gdma(outT[:, p * PH_F:(p + 1) * PH_F], zH[:, p * PH_F:(p + 1) * PH_F],
                 hk("zH", p * PH_F, PH_F), [("out", p)], f"outcp{p}")
            out_keys.append(("out", p))
    P.add("sq", lambda s: None, out_keys, [("fin",)], tag="final_wait")

    P.analyze()
    print("PROG:", P.stats(), flush=True)
    with nc.Block() as block, nc.allow_low_precision(
            reason="bf16 rounding of normalizer rows; fp32 accumulation elsewhere"):
        P.emit_all(block, sems)
    es.close()
    return nc


def prep_inputs(inputs):
    x = np.asarray(inputs["L0_attn_x"], dtype=np.float32)
    Wqkv = np.asarray(inputs["L0_ffn_Wqkv"], dtype=np.float32)
    perm = np.concatenate([h * 64 + PERM64 for h in range(16)])
    fWqk = np.concatenate([Wqkv[:, :, :1024][:, :, perm],
                           Wqkv[:, :, 1024:2048][:, :, perm]], axis=2)
    fWqk = np.ascontiguousarray(
        fWqk.reshape(DEPTH, 8, 128, 16, 128).transpose(0, 3, 2, 1, 4))
    fWv = np.ascontiguousarray(
        Wqkv[:, :, 2048:].reshape(DEPTH, 8, 128, 1024).transpose(0, 2, 1, 3))
    Wo = np.asarray(inputs["L0_ffn_Wo"], dtype=np.float32)
    fWo = np.ascontiguousarray(
        Wo.reshape(DEPTH, 8, 128, 8, 128).transpose(0, 3, 2, 1, 4))
    W1 = np.asarray(inputs["L0_ffn_W1"], dtype=np.float32)
    b1 = np.asarray(inputs["L0_ffn_b1"], dtype=np.float32)
    W2 = np.asarray(inputs["L0_ffn_W2"], dtype=np.float32)
    padc = 128 * KC - 2730
    pad6 = 6 * 512 - 2730
    W1a = np.pad(W1[:, :, :2730], ((0, 0), (0, 0), (0, pad6)))
    W1g = np.pad(W1[:, :, 2730:], ((0, 0), (0, 0), (0, pad6)))
    fW1a = np.ascontiguousarray(
        W1a.reshape(DEPTH, 8, 128, 6, 512).transpose(0, 3, 2, 1, 4))
    fW1g = np.ascontiguousarray(
        W1g.reshape(DEPTH, 8, 128, 6, 512).transpose(0, 3, 2, 1, 4))
    W2p = np.pad(W2, ((0, 0), (0, padc), (0, 0)))
    fW2 = np.ascontiguousarray(
        W2p.reshape(DEPTH, KC, 128, 8, 128).transpose(0, 3, 2, 1, 4))
    aWq_ = np.asarray(inputs["L0_attn_Wq"], dtype=np.float32)
    aWkv_ = np.asarray(inputs["L0_attn_Wkv"], dtype=np.float32)
    aWo_ = np.asarray(inputs["L0_attn_Wo"], dtype=np.float32)
    aWq_t = np.ascontiguousarray(aWq_.reshape(8, 128, 512).transpose(1, 0, 2))
    aWk_t = np.ascontiguousarray(aWkv_[:, 0:512].reshape(8, 128, 512).transpose(1, 0, 2))
    aWv_t = np.ascontiguousarray(aWkv_[:, 512:1024].reshape(8, 128, 512).transpose(1, 0, 2))
    aWo_t = np.ascontiguousarray(aWo_.reshape(8, 64, 8, 128).transpose(2, 1, 0, 3))
    shared = {
        "aWq": aWq_t, "aWk": aWk_t, "aWv": aWv_t, "aWo": aWo_t,
        "fWqk": fWqk, "fWv": fWv, "fWo": fWo,
        "fW1a": fW1a, "fW1g": fW1g, "fW2": fW2,
        "fqs": np.ascontiguousarray(np.asarray(inputs["L0_ffn_q_scale"], np.float32)[:, PERM64]),
        "fks": np.ascontiguousarray(np.asarray(inputs["L0_ffn_k_scale"], np.float32)[:, PERM64]),
        "fg": np.ascontiguousarray(inputs["L0_ffn_ln_g"], dtype=np.float32),
        "fb": np.ascontiguousarray(inputs["L0_ffn_ln_b"], dtype=np.float32),
        "fb1a": np.ascontiguousarray(np.pad(b1[:, :2730], ((0, 0), (0, padc)))),
        "fb1g": np.ascontiguousarray(np.pad(b1[:, 2730:], ((0, 0), (0, padc)))),
        "fb2": np.ascontiguousarray(inputs["L0_ffn_b2"], dtype=np.float32),
    }
    in_maps = []
    for c in range(8):
        m = dict(shared)
        m["xT"] = np.ascontiguousarray(x[c].T)
        in_maps.append(m)
    return in_maps


def kernel(**inputs):
    in_maps = prep_inputs(inputs)
    nc = build_program(stage=99)
    res = run_bass_kernel_spmd(nc, in_maps, core_ids=list(range(8)))
    return np.stack([np.ascontiguousarray(r["outT"].T) for r in res.results])


# version 07
# speedup vs baseline: 1.0375x; 1.0039x over previous
import numpy as np
import ml_dtypes
from contextlib import ExitStack
import concourse.bass as bass
import concourse.mybir as mybir
from concourse.bass_utils import run_bass_kernel_spmd

F32 = mybir.dt.float32
BF16 = mybir.dt.bfloat16
AF = mybir.ActivationFunctionType
OP = mybir.AluOpType

SEQ = 4096
D = 1024
NDC = 8
DEPTH = 6
NEG = -1e30
PH_A = 512
PH_F = 1024
KC = 22
NDMASEM = 8
ATTN_CUT = 99

PERM64 = np.concatenate([np.arange(0, 16), np.arange(32, 48),
                         np.arange(16, 32), np.arange(48, 64)])


class _Op:
    __slots__ = ("idx", "stream", "emit", "reads", "writes", "dma",
                 "deps", "sdeps", "waits", "ticket", "marked", "tag",
                 "needs_drain", "slot")


class Prog:
    STREAMS = ("te", "ve", "se", "gp", "sq")

    def __init__(self):
        self.ops = []
        self.bars = []
        self.preset = set()

    def add(self, stream, emit, reads=(), writes=(), dma=False, tag=""):
        o = _Op()
        o.idx = len(self.ops)
        o.stream = stream
        o.emit = emit
        o.reads = tuple(reads)
        writes = tuple(writes)
        if stream in ("ve", "se"):
            for k in o.reads:
                if isinstance(k, tuple) and len(k) == 2 and k[0] == "bank":
                    writes = writes + (("bankrd", k[1]),)
        o.writes = tuple(writes)
        o.dma = dma
        o.tag = tag
        o.sdeps = []
        o.needs_drain = False
        o.slot = 0
        self.ops.append(o)
        return o

    def barrier(self):
        self.bars.append(len(self.ops))

    def analyze(self):
        last_w = {k: None for k in self.preset}
        readers = {}
        for o in self.ops:
            deps = set()
            for k in o.reads:
                if k not in last_w:
                    raise RuntimeError(f"read-before-write {k} op#{o.idx} {o.tag}")
                w = last_w[k]
                if w is not None:
                    deps.add(w)
                rd = readers.setdefault(k, {})
                if o.dma:
                    rd.setdefault("_dma", []).append(o)
                else:
                    rd[o.stream] = o
            for k in o.writes:
                pseudo = isinstance(k, tuple) and len(k) == 2 and k[0] == "bankrd"
                w = last_w.get(k)
                if w is not None and not (pseudo and w.stream == o.stream):
                    deps.add(w)
                for sk, r in readers.get(k, {}).items():
                    if sk == "_dma":
                        deps.update(r)
                    else:
                        deps.add(r)
                last_w[k] = o
                readers[k] = {}
            deps.discard(o)
            o.deps = [d for d in deps if d.stream != o.stream or d.dma]
            o.sdeps = [d for d in deps if d.stream == o.stream and not d.dma
                       and o.stream in ("ve", "se", "gp")]
        for bidx in self.bars:
            lasts = {}
            for o in self.ops[:bidx]:
                lasts[o.stream] = o
            firsts = {}
            for o in self.ops[bidx:]:
                if o.stream not in firsts:
                    firsts[o.stream] = o
            for s, o in firsts.items():
                for s2, d in lasts.items():
                    if s2 != s:
                        o.deps.append(d)
        for o in self.ops:
            o.marked = o.dma
        for o in self.ops:
            for d in o.deps:
                d.marked = True
            for d in o.sdeps:
                d.marked = True
        cnt = {}
        for o in self.ops:
            o.ticket = None
            if o.marked:
                if o.dma:
                    key = (o.stream, True)
                    n = cnt.get(key, 0)
                    cnt[key] = n + 1
                    o.slot = n % NDMASEM
                    o.ticket = (n // NDMASEM + 1) * 16
                else:
                    key = (o.stream, False)
                    cnt[key] = cnt.get(key, 0) + 1
                    o.slot = 0
                    o.ticket = cnt[key]
        hw = {s: {} for s in self.STREAMS}
        for o in self.ops:
            need = {}
            for d in o.deps:
                sk = (d.stream, d.dma, d.slot)
                if d.ticket > need.get(sk, 0):
                    need[sk] = d.ticket
            for d in o.sdeps:
                sk = (d.stream, d.dma, d.slot)
                if d.ticket > need.get(sk, 0):
                    need[sk] = d.ticket
            o.waits = []
            for sk in sorted(need, key=str):
                c = need[sk]
                if hw[o.stream].get(sk, 0) >= c:
                    continue
                hw[o.stream][sk] = c
                o.waits.append((sk, c))
        for o in self.ops:
            o.needs_drain = False

    def emit_all(self, block, sems):
        per = {s: [o for o in self.ops if o.stream == s] for s in self.STREAMS}
        sections = {"te": block.tensor, "ve": block.vector, "se": block.scalar,
                    "gp": block.gpsimd, "sq": block.sync}

        def mk(stream):
            def body(eng):
                for o in per[stream]:
                    if o.needs_drain:
                        eng.drain()
                    for sk, c in o.waits:
                        eng.wait_ge(sems[sk], c)
                    if o.dma and o.marked and o.ticket > 16:
                        eng.wait_ge(sems[(o.stream, True, o.slot)], o.ticket - 16)
                    instr = o.emit(eng)
                    if o.marked:
                        assert instr is not None, f"op {o.tag} marked, emit->None"
                        instr.then_inc(sems[(o.stream, o.dma, o.slot)],
                                       16 if o.dma else 1)
            return body

        for s in self.STREAMS:
            if per[s]:
                sections[s](mk(s))

    def stats(self):
        by = {}
        for o in self.ops:
            by[o.stream] = by.get(o.stream, 0) + 1
        return (f"ops={len(self.ops)} marked={sum(o.marked for o in self.ops)} "
                f"waits={sum(len(o.waits) for o in self.ops)} by={by}")


def build_program(stage=99, dump=None):
    nc = bass.Bass("TRN2", target_bir_lowering=False)
    P = Prog()
    te = lambda emit, r, w, tag="": P.add("te", emit, r, w, tag=tag)
    ve = lambda emit, r, w, tag="": P.add("ve", emit, r, w, tag=tag)
    se = lambda emit, r, w, tag="": P.add("se", emit, r, w, tag=tag)
    gp = lambda emit, r, w, tag="": P.add("gp", emit, r, w, tag=tag)

    def gdma(dst, src, r, w, tag=""):
        P.add("gp", lambda g: g.dma_start(dst, src), r, w, dma=True, tag=tag)

    def gdma_nc(dst, src, r, w, tag=""):
        P.add("gp", lambda g: g.dma_start(dst, src, allow_slow_non_contiguous=True),
              r, w, dma=True, tag=tag)

    def sdma(dst, src, r, w, tag=""):
        P.add("sq", lambda s: s.dma_start(dst, src), r, w, dma=True, tag=tag)

    dram = lambda name, shape, dt, kind="Internal": nc.dram_tensor(
        name, shape, dt, kind=kind).ap()
    xT = dram("xT", [D, SEQ], F32, "ExternalInput")
    aWq = dram("aWq", [128, NDC, 512], F32, "ExternalInput")
    aWk = dram("aWk", [128, NDC, 512], F32, "ExternalInput")
    aWv = dram("aWv", [128, NDC, 512], F32, "ExternalInput")
    aWo = dram("aWo", [NDC, 64, 8, 128], F32, "ExternalInput")
    fWqk = dram("fWqk", [DEPTH, 16, 128, NDC, 128], F32, "ExternalInput")
    fWv = dram("fWv", [DEPTH, 128, NDC, 1024], F32, "ExternalInput")
    fWo = dram("fWo", [DEPTH, NDC, 128, NDC, 128], F32, "ExternalInput")
    fW1a = dram("fW1a", [DEPTH, 6, 128, NDC, 512], F32, "ExternalInput")
    fW1g = dram("fW1g", [DEPTH, 6, 128, NDC, 512], F32, "ExternalInput")
    fW2 = dram("fW2", [DEPTH, NDC, 128, KC, 128], F32, "ExternalInput")
    fqs = dram("fqs", [DEPTH, 64], F32, "ExternalInput")
    fks = dram("fks", [DEPTH, 64], F32, "ExternalInput")
    fg = dram("fg", [DEPTH, D], F32, "ExternalInput")
    fb = dram("fb", [DEPTH, D], F32, "ExternalInput")
    fb1a = dram("fb1a", [DEPTH, 128 * KC], F32, "ExternalInput")
    fb1g = dram("fb1g", [DEPTH, 128 * KC], F32, "ExternalInput")
    fb2 = dram("fb2", [DEPTH, D], F32, "ExternalInput")
    outT = dram("outT", [D, SEQ], F32, "ExternalOutput")

    zH = dram("zH", [D, SEQ], F32)
    x1H = dram("x1H", [D, SEQ], F32)
    x2H = dram("x2H", [D, SEQ], F32)
    bWqk = dram("bWqk", [DEPTH, 16, 128, NDC, 128], BF16)
    bWv = dram("bWv", [DEPTH, 128, NDC, 1024], BF16)
    bWo = dram("bWo", [DEPTH, NDC, 128, NDC, 128], BF16)
    bW1a = dram("bW1a", [DEPTH, 6, 128, NDC, 512], BF16)
    bW1g = dram("bW1g", [DEPTH, 6, 128, NDC, 512], BF16)
    bW2 = dram("bW2", [DEPTH, NDC, 128, KC, 128], BF16)
    baWq = dram("baWq", [128, NDC, 512], BF16)
    baWk = dram("baWk", [128, NDC, 512], BF16)
    baWv = dram("baWv", [128, NDC, 512], BF16)
    baWo = dram("baWo", [NDC, 64, 8, 128], BF16)

    dumpH = None
    if dump is not None:
        dumpH = dram("dumpH", [D, SEQ], F32, "ExternalOutput")

    NCH = SEQ // PH_A
    P.preset |= {("hbm", "xT", i) for i in range(NCH)}

    def hk(name, t0, n):
        return [("hbm", name, i) for i in range(t0 // PH_A, (t0 + n + PH_A - 1) // PH_A)]

    es = ExitStack()
    A = lambda cm: es.enter_context(cm)
    sb = lambda name, shape, dt=F32: A(nc.sbuf_tensor(name, shape, dt))

    t_csq = sb("t_csq", [128, SEQ], BF16)
    t_snq = sb("t_snq", [128, SEQ], BF16)
    t_csk = sb("t_csk", [128, SEQ], BF16)
    t_snk = sb("t_snk", [128, SEQ], BF16)
    onescol = sb("onescol", [128, 1], BF16)
    onesrow = sb("onesrow", [128, 128], BF16)
    ident = sb("ident", [128, 128], BF16)
    mA1 = sb("mA1", [128, 128], BF16)
    mA2 = sb("mA2", [128, 128], BF16)
    sel16 = sb("sel16", [128, NDC, 16], BF16)
    vneg12 = sb("vneg12", [128, 1], F32)
    vneg16 = sb("vneg16", [128, 1], F32)
    ceps = sb("ceps", [1, 1], F32)
    cepsl2 = sb("cepsl2", [1, 1], F32)
    c16row = sb("c16row", [1, 64], BF16)
    p_qs = sb("p_qs", [128, 1], F32)
    p_ks = sb("p_ks", [128, 1], F32)
    p_g = sb("p_g", [128, NDC], F32)
    p_b = sb("p_b", [128, NDC], F32)
    p_b1a = sb("p_b1a", [128, KC], F32)
    p_b1g = sb("p_b1g", [128, KC], F32)
    p_b2 = sb("p_b2", [128, NDC], F32)
    zf = sb("zf", [128, NDC, PH_F + 1], F32)
    xs_bf = sb("xs_bf", [128, NDC, PH_F + 1], BF16)
    xsq_bf = sb("xsq_bf", [128, NDC, 513], BF16)
    carryb = sb("carryb", [128, NDC, 1], F32)
    r_mu = sb("r_mu", [1, 512], F32)
    r_s2 = sb("r_s2", [1, 512], F32)
    r_var = sb("r_var", [1, 512], F32)
    r_rstd = sb("r_rstd", [1, 512], BF16)
    r_mrs = sb("r_mrs", [1, 512], BF16)

    bank = [A(nc.psum_tensor(f"pb{i}", [128, 512], F32)) for i in range(8)]
    BK = [("bank", i) for i in range(8)]

    sems = {}
    for s in Prog.STREAMS:
        sems[(s, False, 0)] = A(nc.semaphore(f"sem_{s}"))
    for s in ("sq", "gp"):
        for i in range(NDMASEM):
            sems[(s, True, i)] = A(nc.semaphore(f"sem_{s}d{i}"))

    for (dst, src, n) in [(bWqk, fWqk, "wqk"), (bWv, fWv, "wv"), (bWo, fWo, "wo"),
                          (bW1a, fW1a, "w1a"), (bW1g, fW1g, "w1g"),
                          (bW2, fW2, "w2"), (baWq, aWq, "awq"),
                          (baWk, aWk, "awk"), (baWv, aWv, "awv"),
                          (baWo, aWo, "awo")]:
        gdma(dst, src, (), [("wc", n)], tag=f"cast_{n}")

    gp(lambda g: g.memset(onescol.ap(), 1.0), (), ["onescol"], "ones")
    gp(lambda g: g.memset(onesrow.ap(), 1.0), (), ["onesrow"], "onesr")
    gp(lambda g: g.memset(vneg12.ap(), -12.0), (), ["vneg12"], "v12")
    gp(lambda g: g.memset(vneg16.ap(), -16.0), (), ["vneg16"], "v16")
    gp(lambda g: g.memset(ceps.ap(), 1e-5), (), ["ceps"], "eps")
    gp(lambda g: g.memset(c16row.ap(), 16.0), (), ["c16row"], "c16")
    gp(lambda g: g.memset(cepsl2.ap(), 1e-16), (), ["cepsl2"], "eps2")

    tb = ExitStack()
    T = lambda n, s, d=F32: tb.enter_context(nc.sbuf_tensor(n, s, d))
    itr, itc, tmsk = T("itr", [128, 128]), T("itc", [128, 1]), T("tmsk", [128, 128])
    w_i, w_c, w_if, w_lb = T("w_i", [32, 1]), T("w_c", [32, 1]), T("w_if", [32, 1]), T("w_lb", [32, 1])
    w_fr, w_x, w_k = T("w_fr", [32, SEQ]), T("w_x", [32, SEQ]), T("w_k", [32, SEQ])
    w_sin, w_cos = T("w_sin", [32, SEQ]), T("w_cos", [32, SEQ])
    w_o1, w_o2 = T("w_o1", [32, SEQ], BF16), T("w_o2", [32, SEQ], BF16)

    gp(lambda g: g.iota(itr.ap(), [[1, 128]], base=0, channel_multiplier=0,
                        allow_small_or_imprecise_dtypes=True), (), ["itr"], "itr")
    gp(lambda g: g.iota(itc.ap(), [[1, 1]], base=0, channel_multiplier=1,
                        allow_small_or_imprecise_dtypes=True), (), ["itc"], "itc")
    ve(lambda v: v.tensor_scalar(ident.ap(), itr.ap(), itc.ap(), None, OP.is_equal),
       ["itr", "itc"], ["ident"], "ident")
    ve(lambda v: v.tensor_scalar(tmsk.ap(), itr.ap(), itc.ap(), None, OP.is_gt),
       ["itr", "itc"], ["tmsk"], "m1a")
    ve(lambda v: v.tensor_scalar(mA1.ap(), tmsk.ap(), NEG, None, OP.mult),
       ["tmsk"], ["mA1"], "m1b")
    ve(lambda v: v.tensor_scalar(tmsk.ap(), itr.ap(), itc.ap(), None, OP.is_lt),
       ["itr", "itc"], ["tmsk"], "m2a")
    ve(lambda v: v.tensor_scalar(mA2.ap(), tmsk.ap(), NEG, None, OP.mult),
       ["tmsk"], ["mA2"], "m2b")
    hcol = tb.enter_context(nc.sbuf_tensor("hcol", [128, 1], F32))
    rge = tb.enter_context(nc.sbuf_tensor("rge", [128, 1], F32))
    ve(lambda v: v.tensor_scalar(rge.ap(), itc.ap(), 64.0, None, OP.is_ge),
       ["itc"], ["rge"], "rge")
    for dc in range(NDC):
        def mh(dc=dc):
            return lambda v: v.tensor_scalar(hcol.ap(), rge.ap(), 1.0, float(2 * dc),
                                             OP.mult, OP.add)
        ve(mh(), ["rge"], ["hcol"], f"hc{dc}")
        def ms(dc=dc):
            return lambda v: v.tensor_scalar(sel16[:, dc, :], itr[:, 0:16],
                                             hcol.ap(), None, OP.is_equal)
        ve(ms(), ["hcol", "itr"], ["sel16"], f"s16_{dc}")

    gp(lambda g: g.iota(w_i.ap(), [[1, 1]], base=0, channel_multiplier=1,
                        allow_small_or_imprecise_dtypes=True), (), ["w_i"], "w_i")
    gp(lambda g: g.iota(w_fr.ap(), [[1, SEQ]], base=0, channel_multiplier=0,
                        allow_small_or_imprecise_dtypes=True), (), ["w_fr"], "w_t")
    gp(lambda g: g.memset(w_c.ap(), 10000.0), (), ["w_c"], "c4")
    se(lambda s: s.activation(w_c.ap(), w_c.ap(), AF.Ln), ["w_c"], ["w_c"], "ln4")
    ve(lambda v: v.tensor_scalar(w_if.ap(), w_i.ap(), -1.0 / 32.0, None, OP.mult),
       ["w_i"], ["w_if"], "nx")
    se(lambda s: s.activation(w_if.ap(), w_if.ap(), AF.Exp, scale=w_c.ap()),
       ["w_if", "w_c"], ["w_if"], "invf")
    ve(lambda v: v.tensor_scalar(w_fr.ap(), w_fr.ap(), w_if.ap(), None, OP.mult),
       ["w_fr", "w_if"], ["w_fr"], "fr")

    TWO23 = 8388608.0
    INV2PI = float(np.float32(1.0 / (2 * np.pi)))
    c2pi = 2 * np.pi
    C1 = float(np.float32(int(c2pi * 2**10) / 2**10))
    C2 = float(np.float32(c2pi - C1))
    C3 = float(c2pi - C1 - np.float64(C2))
    PI2 = float(np.float32(np.pi / 2))
    PI2L = float(np.pi / 2 - np.float64(PI2))

    def sin_of(out, shift, tg):
        xin, xk = w_fr, "w_fr"
        if shift:
            ve(lambda v: v.tensor_scalar(w_x.ap(), w_fr.ap(), PI2, PI2L, OP.add, OP.add),
               ["w_fr"], ["w_x"], tg + "x")
            xin, xk = w_x, "w_x"
        ve(lambda v: v.tensor_scalar(w_k.ap(), xin.ap(), INV2PI, TWO23, OP.mult, OP.add),
           [xk], ["w_k"], tg + "k1")
        ve(lambda v: v.tensor_scalar(w_k.ap(), w_k.ap(), TWO23, None, OP.subtract),
           ["w_k"], ["w_k"], tg + "k")
        ve(lambda v: v.scalar_tensor_tensor(out.ap(), w_k.ap(), -C1, xin.ap(),
                                            OP.mult, OP.add),
           ["w_k", xk], [tg + "out"], tg + "r1")
        ve(lambda v: v.scalar_tensor_tensor(out.ap(), w_k.ap(), -C2, out.ap(),
                                            OP.mult, OP.add),
           ["w_k", tg + "out"], [tg + "out"], tg + "r2")
        ve(lambda v: v.scalar_tensor_tensor(out.ap(), w_k.ap(), -C3, out.ap(),
                                            OP.mult, OP.add),
           ["w_k", tg + "out"], [tg + "out"], tg + "r3")
        T2 = 3.1415925
        TWOPI = float(2 * np.pi)
        ve(lambda v: v.tensor_scalar(w_k.ap(), out.ap(), T2, -TWOPI,
                                     OP.is_gt, OP.mult),
           [tg + "out", "w_k"], ["w_k"], tg + "wg")
        ve(lambda v: v.tensor_tensor(out.ap(), out.ap(), w_k.ap(), OP.add),
           [tg + "out", "w_k"], [tg + "out"], tg + "wga")
        ve(lambda v: v.tensor_scalar(w_k.ap(), out.ap(), -T2, TWOPI,
                                     OP.is_lt, OP.mult),
           [tg + "out", "w_k"], ["w_k"], tg + "wl")
        ve(lambda v: v.tensor_tensor(out.ap(), out.ap(), w_k.ap(), OP.add),
           [tg + "out", "w_k"], [tg + "out"], tg + "wla")
        ve(lambda v: v.tensor_scalar(out.ap(), out.ap(), -T2, T2,
                                     OP.max, OP.min),
           [tg + "out"], [tg + "out"], tg + "cl")
        se(lambda s: s.activation(out.ap(), out.ap(), AF.Sin),
           [tg + "out"], [tg + "out"], tg + "sin")

    sin_of(w_sin, False, "ws")
    sin_of(w_cos, True, "wc")
    A256 = float(np.float32(25.6))
    A256L = float(25.6 - np.float64(A256))
    B896 = float(np.float32(89.6))
    B896L = float(89.6 - np.float64(B896))
    ve(lambda v: v.tensor_scalar(w_lb.ap(), w_i.ap(), 2.0, A256, OP.mult, OP.add),
       ["w_i"], ["w_lb"], "b1")
    ve(lambda v: v.tensor_scalar(w_lb.ap(), w_lb.ap(), A256L, None, OP.add),
       ["w_lb"], ["w_lb"], "b2")
    gp(lambda g: g.memset(w_c.ap(), B896), ["w_c"], ["w_c"], "d1")
    ve(lambda v: v.tensor_scalar(w_c.ap(), w_c.ap(), B896L, None, OP.add),
       ["w_c"], ["w_c"], "d2")
    ve(lambda v: v.reciprocal(w_c.ap(), w_c.ap()), ["w_c"], ["w_c"], "d3")
    ve(lambda v: v.tensor_scalar(w_lb.ap(), w_lb.ap(), w_c.ap(), None, OP.mult),
       ["w_lb", "w_c"], ["w_lb"], "base")
    se(lambda s: s.activation(w_lb.ap(), w_lb.ap(), AF.Ln), ["w_lb"], ["w_lb"], "lnb")
    gp(lambda g: g.iota(w_x.ap(), [[1, SEQ]], base=0, channel_multiplier=0,
                        allow_small_or_imprecise_dtypes=True), ["w_x"], ["w_x"], "t2")
    ve(lambda v: v.tensor_scalar(w_x.ap(), w_x.ap(), -2048.0, 1.0 / 128.0,
                                 OP.add, OP.mult),
       ["w_x"], ["w_x"], "pw")
    ve(lambda v: v.tensor_scalar(w_fr.ap(), w_x.ap(), w_lb.ap(), None, OP.mult),
       ["w_x", "w_lb", "w_fr"], ["w_fr"], "plnb")
    se(lambda s: s.activation(w_k.ap(), w_fr.ap(), AF.Exp, scale=-1.0),
       ["w_fr", "w_k"], ["w_k"], "isc")
    se(lambda s: s.activation(w_fr.ap(), w_fr.ap(), AF.Exp),
       ["w_fr"], ["w_fr"], "sc")

    def combine(outt, outk, srcA, srcAk, srcB, srcBk, scl, tg):
        ve(lambda v: v.scalar_tensor_tensor(outt.ap(), srcA.ap(), scl, srcB.ap(),
                                            OP.mult, OP.mult),
           [srcAk, srcBk], [outk], tg)

    def replicate_sn(tbl, tblk, kpos, kneg):
        for blk in range(2):
            r0 = blk * 64
            for (dst0, src, sk, s0) in ((0, w_o2, kneg, 0), (16, w_o1, kpos, 0),
                                        (32, w_o2, kneg, 16), (48, w_o1, kpos, 16)):
                def mk(dst0=dst0, src=src, s0=s0, r0=r0, tbl=tbl):
                    return lambda g: g.dma_start(tbl[r0 + dst0:r0 + dst0 + 16, :],
                                                 src[s0:s0 + 16, :])
                P.add("gp", mk(), [sk], [tblk], dma=True, tag=f"rep{tblk}{r0+dst0}")

    def replicate_cs(tbl, tblk, kcs):
        for blk in range(2):
            r0 = blk * 64
            for (dst0, s0) in ((0, 0), (16, 0), (32, 16), (48, 16)):
                def mk(dst0=dst0, s0=s0, r0=r0, tbl=tbl):
                    return lambda g: g.dma_start(tbl[r0 + dst0:r0 + dst0 + 16, :],
                                                 w_o1[s0:s0 + 16, :])
                P.add("gp", mk(), [kcs], [tblk], dma=True, tag=f"rep{tblk}{r0+dst0}")

    combine(w_o1, "w_o1", w_cos, "wcout", w_fr, "w_fr", 8.0, "csq32")
    replicate_cs(t_csq, "t_csq", "w_o1")
    combine(w_o1, "w_o1", w_sin, "wsout", w_fr, "w_fr", 8.0, "snqp")
    combine(w_o2, "w_o2", w_sin, "wsout", w_fr, "w_fr", -8.0, "snqn")
    replicate_sn(t_snq, "t_snq", "w_o1", "w_o2")
    combine(w_o1, "w_o1", w_cos, "wcout", w_k, "w_k", 1.0, "csk32")
    replicate_cs(t_csk, "t_csk", "w_o1")
    combine(w_o1, "w_o1", w_sin, "wsout", w_k, "w_k", 1.0, "snkp")
    combine(w_o2, "w_o2", w_sin, "wsout", w_k, "w_k", -1.0, "snkn")
    replicate_sn(t_snk, "t_snk", "w_o1", "w_o2")

    P.barrier()
    tb.close()

    def ln_stats_chunk(src_bf, sq_bf, ksrc, ksq, coff, coff_sq, n, shifted, tg):
        for st, (buf, kin, bk, cbase) in enumerate(
                ((src_bf, ksrc, 6, coff), (sq_bf, ksq, 7, coff_sq))):
            for dc in range(NDC):
                off = cbase + (0 if (shifted and dc >= 4) else 1)
                def mk(dc=dc, off=off, bk=bk, buf=buf, n=n):
                    return lambda t: t.matmul(bank[bk][0:1, 0:n], onescol.ap(),
                                              buf[:, dc, off:off + n],
                                              start=(dc == 0), stop=(dc == NDC - 1))
                te(mk(), [kin, "onescol"], [BK[bk]], f"{tg}st{st}dc{dc}")
        se(lambda s: s.activation(r_mu[0:1, 0:n], bank[6][0:1, 0:n], AF.Identity,
                                  scale=1.0 / 1024.0),
           [BK[6]], ["r_mu"], f"{tg}mu")
        se(lambda s: s.activation(r_s2[0:1, 0:n], bank[7][0:1, 0:n], AF.Identity,
                                  scale=1.0 / 1024.0),
           [BK[7]], ["r_s2"], f"{tg}s2")
        ve(lambda v: v.tensor_tensor(r_var[0:1, 0:n], r_mu[0:1, 0:n],
                                     r_mu[0:1, 0:n], OP.mult),
           ["r_mu"], ["r_var"], f"{tg}mu2")
        ve(lambda v: v.tensor_tensor(r_var[0:1, 0:n], r_s2[0:1, 0:n],
                                     r_var[0:1, 0:n], OP.subtract),
           ["r_s2", "r_var"], ["r_var"], f"{tg}var")
        se(lambda s: s.activation(r_var[0:1, 0:n], r_var[0:1, 0:n], AF.Ln,
                                  bias=ceps[0:1, :]),
           ["r_var", "ceps"], ["r_var"], f"{tg}ln")
        se(lambda s: s.activation(r_rstd[0:1, 0:n], r_var[0:1, 0:n], AF.Exp,
                                  scale=-0.5),
           ["r_var"], ["r_rstd"], f"{tg}rstd")
        ve(lambda v: v.tensor_tensor(r_mrs[0:1, 0:n], r_mu[0:1, 0:n],
                                     r_rstd[0:1, 0:n], OP.mult),
           ["r_mu", "r_rstd"], ["r_mrs"], f"{tg}mrs")

    def ln_bcast(n, tg):
        te(lambda t: t.matmul(bank[4][:, 0:n], onesrow[0:1, :], r_rstd[0:1, 0:n],
                              start=True, stop=True),
           ["onesrow", "r_rstd"], [BK[4]], f"{tg}bcA")
        te(lambda t: t.matmul(bank[5][:, 0:n], onesrow[0:1, :], r_mrs[0:1, 0:n],
                              start=True, stop=True),
           ["onesrow", "r_mrs"], [BK[5]], f"{tg}bcB")

    def ln_pass(xin, xin_name, zout, zout_name, tg):
        for p in range(SEQ // PH_F):
            t0 = p * PH_F
            sdma(zf[:, :, 1:PH_F + 1],
                 xin[:, t0:t0 + PH_F].rearrange("(a q) t -> q a t", q=128),
                 hk(xin_name, t0, PH_F), ["zf"], f"{tg}in{p}")
            se(lambda s: s.activation(xs_bf[:, :, 1:PH_F + 1], zf[:, :, 1:PH_F + 1],
                                      AF.Identity),
               ["zf"], ["xs_bf"], f"{tg}bf{p}")
            for c in range(PH_F // 512):
                co = c * 512
                def mksq(co=co):
                    return lambda s: s.activation(xsq_bf[:, :, 1:513],
                                                  zf[:, :, co + 1:co + 513], AF.Square)
                se(mksq(), ["zf"], ["xsq_bf"], f"{tg}sq{p}_{c}")
                ln_stats_chunk(xs_bf, xsq_bf, "xs_bf", "xsq_bf", co, 0, 512, False,
                               f"{tg}p{p}c{c}_")
                ln_bcast(512, f"{tg}p{p}c{c}_")
                bcA = bank[4][:, 0:512].unsqueeze(1).broadcast_to([128, NDC, 512])
                bcB = bank[5][:, 0:512].unsqueeze(1).broadcast_to([128, NDC, 512])
                def mk1(co=co, bcA=bcA):
                    return lambda v: v.tensor_tensor(zf[:, :, 1 + co:513 + co],
                                                     zf[:, :, 1 + co:513 + co],
                                                     bcA, OP.mult)
                ve(mk1(), ["zf", BK[4]], ["zf"], f"{tg}ap1_{p}_{c}")
                def mk2(co=co, bcB=bcB):
                    return lambda v: v.tensor_tensor(zf[:, :, 1 + co:513 + co],
                                                     zf[:, :, 1 + co:513 + co],
                                                     bcB, OP.subtract)
                ve(mk2(), ["zf", BK[5]], ["zf"], f"{tg}ap2_{p}_{c}")
            sdma(zout[:, t0:t0 + PH_F].rearrange("(a q) t -> q a t", q=128),
                 zf[:, :, 1:PH_F + 1], ["zf"], hk(zout_name, t0, PH_F), f"{tg}out{p}")

    if stage >= 1:
        ln_pass(xT, "xT", zH, "zH", "ln0")
    def attn_sublayer(li, tg):
        sc = ExitStack()
        S = lambda n, shp, d=BF16: sc.enter_context(nc.sbuf_tensor(tg + n, shp, d))
        wqk_sl = [S(f"wqk{i}", [128, NDC, 128]) for i in range(2)]
        wv = S("wv", [128, NDC, 1024])
        wo_sl = [S(f"wo{i}", [128, NDC, 128]) for i in range(2)]
        q_bf = S("q_bf", [128, NDC, 512])
        k_bf = S("k_bf", [128, NDC, 512])
        qr_bf = S("qr_bf", [128, NDC, 512])
        kr_bf = S("kr_bf", [128, NDC, 3, 256])
        v66 = S("v66", [128, 3, 2, 16, 65])
        qsh = S("qsh", [128, 512])
        qm = S("qm", [128, 512])
        st_sb = S("st_sb", [16, 512], F32)
        r_l2 = S("r_l2", [65, 8 * 512])
        eT = S("eT", [128, 2, 1536])
        outSb = S("outSb", [128, 512])
        rec_sb = S("rec_sb", [128, 512])
        rec_f = S("rec_f", [128, 512], F32)
        outN = S("outN", [128, NDC, 512])

        gp(lambda g: g.memset(v66[:, :, :, :, 64:65], 1.0), (), [("v66", "k")],
           f"{tg}v66o")
        gdma(p_qs[0:64, 0:1], fqs[li, :], (), ["p_qs"], f"{tg}qs0")
        gdma(p_qs[64:128, 0:1], fqs[li, :], ["p_qs"], ["p_qs"], f"{tg}qs1")
        gdma(p_ks[0:64, 0:1], fks[li, :], (), ["p_ks"], f"{tg}ks0")
        gdma(p_ks[64:128, 0:1], fks[li, :], ["p_ks"], ["p_ks"], f"{tg}ks1")
        sdma(wv.ap(), bWv[li], [("wc", "wv")], ["wv"], f"{tg}wv")
        gp(lambda g: g.memset(zf[:, :, 0:1], 0.0), ["zf"], ["zf"], f"{tg}zc")

        NPH = SEQ // PH_A
        for p in range(NPH):
            t0 = p * PH_A
            sdma(zf[:, :, 1:PH_A + 1],
                 zH[:, t0:t0 + PH_A].rearrange("(a q) t -> q a t", q=128),
                 hk("zH", t0, PH_A) + ["zf"], ["zf"], f"{tg}in{p}")
            if p > 0:
                ve(lambda v: v.tensor_copy(zf[:, :, 0:1], carryb.ap()),
                   ["carryb", "zf"], ["zf"], f"{tg}rcar{p}")
            ve(lambda v: v.tensor_copy(carryb.ap(), zf[:, :, PH_A:PH_A + 1]),
               ["zf"], ["carryb"], f"{tg}ccar{p}")
            se(lambda s: s.activation(xs_bf[:, :, 0:PH_A + 1], zf[:, :, 0:PH_A + 1],
                                      AF.Identity),
               ["zf"], ["xs_bf"], f"{tg}bf{p}")

            for grp, (dst, pvec, kpv, tcs, tsn, tcsk, tsnk) in enumerate(
                    ((q_bf, p_qs, "p_qs", t_csq, t_snq, None, None),
                     (k_bf, p_ks, "p_ks", t_csk, t_snk, None, None))):
                cs_t = t_csq if grp == 0 else t_csk
                sn_t = t_snq if grp == 0 else t_snk
                kcs = "t_csq" if grp == 0 else "t_csk"
                ksn = "t_snq" if grp == 0 else "t_snk"
                kdst = "q_bf" if grp == 0 else "k_bf"
                for pt in range(NDC):
                    ft = grp * 8 + pt
                    par = pt % 2
                    bkq = par
                    def mkw(ft=ft, par=par):
                        return lambda s: s.dma_start(wqk_sl[par].ap(), bWqk[li, ft])
                    P.add("sq", mkw(), [("wc", "wqk")],
                          [("wqk_sl", par)], dma=True, tag=f"{tg}w{p}_{ft}")
                    for dc in range(NDC):
                        off = 0 if dc >= 4 else 1
                        def mkm(dc=dc, off=off, bkq=bkq, par=par):
                            return lambda t: t.matmul(
                                bank[bkq][:, 0:512], wqk_sl[par][:, dc, :],
                                xs_bf[:, dc, off:off + 512],
                                start=(dc == 0), stop=(dc == NDC - 1))
                        te(mkm(), [("wqk_sl", par), "xs_bf"], [BK[bkq]],
                           f"{tg}qk{p}_{ft}_{dc}")
                    def mke1(pt=pt, bkq=bkq):
                        return lambda s: s.activation(xsq_bf[:, pt, 0:512],
                                                      bank[bkq][:, 0:512], AF.Square)
                    se(mke1(), [BK[bkq]], ["xsq_bf"], f"{tg}sq{p}_{ft}")
                    def mke2(pt=pt, bkq=bkq, dst=dst, pvec=pvec):
                        return lambda s: s.activation(dst[:, pt, 0:512],
                                                      bank[bkq][:, 0:512], AF.Identity,
                                                      scale=pvec[:, 0:1])
                    se(mke2(), [BK[bkq], kpv], [kdst], f"{tg}ev{p}_{ft}")
                if ATTN_CUT < 2:
                    continue
                for dc in range(NDC):
                    def mks(dc=dc):
                        return lambda t: t.matmul(bank[6][0:16, 0:512],
                                                  sel16[:, dc, :], xsq_bf[:, dc, 0:512],
                                                  start=(dc == 0), stop=(dc == NDC - 1))
                    te(mks(), ["sel16", "xsq_bf"], [BK[6]], f"{tg}l2s{p}_{grp}_{dc}")
                ve(lambda v: v.tensor_scalar(st_sb.ap(), bank[6][0:16, 0:512],
                                             1e-16, None, OP.max),
                   [BK[6]], ["st_sb"], f"{tg}l2m{p}_{grp}")
                se(lambda s: s.activation(st_sb.ap(), st_sb.ap(), AF.Ln),
                   ["st_sb"], ["st_sb"], f"{tg}l2l{p}_{grp}")
                se(lambda s: s.activation(st_sb.ap(), st_sb.ap(), AF.Exp, scale=-0.5),
                   ["st_sb"], ["st_sb"], f"{tg}l2e{p}_{grp}")
                gdma(r_l2[0:1, :].rearrange("o (s t) -> o s t", s=8),
                     st_sb[0:8, :], ["st_sb"], ["r_l2"], f"{tg}l2r{p}_{grp}")
                gdma(r_l2[64:65, :].rearrange("o (s t) -> o s t", s=8),
                     st_sb[8:16, :], ["st_sb"], ["r_l2"], f"{tg}l2s{p}_{grp}")
                for pt in range(NDC):
                    bkb = 4 + pt % 2
                    h0, h1 = 2 * pt, 2 * pt + 1
                    def mkb0(h0=h0, bkb=bkb):
                        r, c = (h0 // 8) * 64, (h0 % 8) * 512
                        return lambda t: t.matmul(bank[bkb][0:64, 0:512],
                                                  onesrow[r:r + 1, 0:64],
                                                  r_l2[r:r + 1, c:c + 512],
                                                  start=True, stop=True)
                    te(mkb0(), ["onesrow", "r_l2"], [BK[bkb]], f"{tg}bc{p}_{grp}_{h0}")
                    def mkb1(h1=h1, bkb=bkb):
                        r, c = (h1 // 8) * 64, (h1 % 8) * 512
                        return lambda t: t.matmul(bank[bkb][64:128, 0:512],
                                                  onesrow[r:r + 1, 0:64],
                                                  r_l2[r:r + 1, c:c + 512],
                                                  start=True, stop=True)
                    te(mkb1(), ["onesrow", "r_l2"], [BK[bkb]], f"{tg}bc{p}_{grp}_{h1}")
                    def mkn(pt=pt, bkb=bkb, dst=dst):
                        return lambda v: v.tensor_tensor(dst[:, pt, :], dst[:, pt, :],
                                                         bank[bkb][:, 0:512], OP.mult)
                    ve(mkn(), [kdst, BK[bkb]], [kdst], f"{tg}n{p}_{grp}_{pt}")
                    SHUF = [(i + 16) % 32 for i in range(32)]
                    def mksh(pt=pt, dst=dst):
                        return lambda v: v.stream_shuffle(qsh.ap(), dst[:, pt, :], SHUF)
                    ve(mksh(), [kdst], ["qsh"], f"{tg}sh{p}_{grp}_{pt}")
                    def mkqm(pt=pt, dst=dst, cs_t=cs_t, t0=t0):
                        return lambda v: v.tensor_tensor(qm.ap(), dst[:, pt, :],
                                                         cs_t[:, t0:t0 + 512], OP.mult)
                    ve(mkqm(), [kdst, kcs], ["qm"], f"{tg}m{p}_{grp}_{pt}")
                    def mkrt(pt=pt, sn_t=sn_t, t0=t0):
                        return lambda v: v.tensor_tensor(qsh.ap(), qsh.ap(),
                                                         sn_t[:, t0:t0 + 512], OP.mult)
                    ve(mkrt(), ["qsh", ksn], ["qsh"], f"{tg}r{p}_{grp}_{pt}")
                    if grp == 0:
                        def mka(pt=pt):
                            return lambda v: v.tensor_tensor(qr_bf[:, pt, :], qm.ap(),
                                                             qsh.ap(), OP.add)
                        ve(mka(), ["qm", "qsh"], ["qr_bf"], f"{tg}a{p}_{grp}_{pt}")
                    else:
                        for half in range(2):
                            seg = (2 * p + half) % 3
                            def mka(pt=pt, half=half, seg=seg):
                                return lambda v: v.tensor_tensor(
                                    kr_bf[:, pt, seg, :],
                                    qm[:, half * 256:(half + 1) * 256],
                                    qsh[:, half * 256:(half + 1) * 256], OP.add)
                            ve(mka(), ["qm", "qsh"], [("kr", seg)],
                               f"{tg}a{p}_{grp}_{pt}_{half}")

            for tt in range(4 if ATTN_CUT >= 3 else 0):
                seg = (2 * p + tt // 2) % 3
                tile = tt % 2
                toff = 0 if True else 0
                for half in range(2):
                    bkv = half
                    for dc in range(NDC):
                        off = 0 if dc >= 4 else 1
                        def mkv(tt=tt, dc=dc, off=off, bkv=bkv, half=half):
                            return lambda t: t.matmul(
                                bank[bkv][:, 0:512],
                                xs_bf[:, dc, off + tt * 128: off + tt * 128 + 128],
                                wv[:, dc, half * 512:(half + 1) * 512],
                                start=(dc == 0), stop=(dc == NDC - 1))
                        te(mkv(), ["xs_bf", "wv"], [BK[bkv]], f"{tg}v{p}_{tt}_{half}_{dc}")
                    def mkve(tt=tt, bkv=bkv, half=half, seg=seg, tile=tile):
                        src = bank[bkv][:, 0:512].rearrange("q (h e) -> q h e", e=64)
                        return lambda s: s.activation(
                            v66[:, seg, tile, half * 8:(half + 1) * 8, 0:64], src,
                            AF.Identity)
                    se(mkve(), [BK[bkv]], [("v66", seg)], f"{tg}ve{p}_{tt}_{half}")

            segP = (2 * p - 1) % 3
            segC0 = (2 * p) % 3
            segC1 = (2 * p + 1) % 3
            def mktail(j, sub, h, even, pvb):
                def emit_tail():
                    if even:
                        vb = lambda sg, tl: v66[:, sg, tl, h, 0:65]
                        ob, oe = 0, 65
                    else:
                        vb = lambda sg, tl: v66[:, sg, tl, h, 0:64]
                        ob, oe = 64, 128
                    def pv(cols, ecols, lhsT, st, rd, tag):
                        def f(t, cols=cols, ecols=ecols, lhsT=lhsT, st=st):
                            return t.matmul(bank[pvb][ob:oe, cols[0]:cols[1]],
                                            lhsT, eT[:, sub, ecols[0]:ecols[1]],
                                            start=st, stop=False,
                                            skip_group_check=True)
                        te(f, rd + [("v66", "k"), ("eT", sub)], [BK[pvb]], tag)
                    pv((0, 384), (512, 896), vb(segC0, 0), True,
                       [("v66", segC0)], f"{tg}p2_{p}{h}")
                    if p > 0:
                        pv((0, 128), (0, 128), vb(segP, 0), False,
                           [("v66", segP)], f"{tg}p0_{p}{h}")
                        pv((0, 256), (128, 384), vb(segP, 1), False,
                           [("v66", segP)], f"{tg}p1_{p}{h}")
                    pv((128, 384), (896, 1152), vb(segC0, 1), False,
                       [("v66", segC0)], f"{tg}p3a_{p}{h}")
                    pv((384, 512), (1152, 1280), vb(segC0, 1), False,
                       [("v66", segC0)], f"{tg}p3b_{p}{h}")
                    pv((256, 384), (1280, 1408), vb(segC1, 0), False,
                       [("v66", segC1)], f"{tg}p4a_{p}{h}")
                    pv((384, 512), (1408, 1536), vb(segC1, 0), False,
                       [("v66", segC1)], f"{tg}p4b_{p}{h}")
                    pv((384, 512), (384, 512), vb(segC1, 1), False,
                       [("v66", segC1)], f"{tg}p5_{p}{h}")
                    if not even:
                        def rs(cols, ecols, st, tag):
                            def f(t, cols=cols, ecols=ecols, st=st):
                                return t.matmul(bank[pvb][32:33, cols[0]:cols[1]],
                                                onescol.ap(),
                                                eT[:, sub, ecols[0]:ecols[1]],
                                                start=st, stop=False,
                                                skip_group_check=True)
                            te(f, ["onescol", ("eT", sub)], [BK[pvb]], tag)
                        rs((0, 384), (512, 896), True, f"{tg}r2_{p}{h}")
                        if p > 0:
                            rs((0, 128), (0, 128), False, f"{tg}r0_{p}{h}")
                            rs((0, 256), (128, 384), False, f"{tg}r1_{p}{h}")
                        rs((128, 384), (896, 1152), False, f"{tg}r3a_{p}{h}")
                        rs((384, 512), (1152, 1280), False, f"{tg}r3b_{p}{h}")
                        rs((256, 384), (1280, 1408), False, f"{tg}r4a_{p}{h}")
                        rs((384, 512), (1408, 1536), False, f"{tg}r4b_{p}{h}")
                        rs((384, 512), (384, 512), False, f"{tg}r5_{p}{h}")
                        def mkr1a():
                            return lambda s: s.activation(rec_f[64:65, 0:512],
                                                          bank[6][64:65, 0:512],
                                                          AF.Ln)
                        se(mkr1a(), [BK[6]], ["rec_f"], f"{tg}rl0_{p}{j}")
                        def mkr1b():
                            return lambda s: s.activation(rec_sb[64:65, 0:512],
                                                          rec_f[64:65, 0:512],
                                                          AF.Exp, scale=-1.0)
                        se(mkr1b(), ["rec_f"], ["rec_sb"], f"{tg}rc0_{p}{j}")
                        def mkr2a():
                            return lambda s: s.activation(rec_f[32:33, 0:512],
                                                          bank[7][32:33, 0:512],
                                                          AF.Ln)
                        se(mkr2a(), [BK[7]], ["rec_f"], f"{tg}rl1_{p}{j}")
                        def mkr2b():
                            return lambda s: s.activation(rec_sb[32:33, 0:512],
                                                          rec_f[32:33, 0:512],
                                                          AF.Exp, scale=-1.0)
                        se(mkr2b(), ["rec_f"], ["rec_sb"], f"{tg}rc1_{p}{j}")
                        def mkbc0():
                            return lambda t: t.matmul(bank[4][0:64, 0:512],
                                                      onesrow[64:65, 0:64],
                                                      rec_sb[64:65, 0:512],
                                                      start=True, stop=True)
                        te(mkbc0(), ["onesrow", "rec_sb"], [BK[4]], f"{tg}bc0_{p}{j}")
                        def mkbc1():
                            return lambda t: t.matmul(bank[4][64:128, 0:512],
                                                      onesrow[32:33, 0:64],
                                                      rec_sb[32:33, 0:512],
                                                      start=True, stop=True)
                        te(mkbc1(), ["onesrow", "rec_sb"], [BK[4]], f"{tg}bc1_{p}{j}")
                        def mkev0():
                            return lambda v: v.tensor_copy(outSb[0:64, 0:512],
                                                           bank[6][0:64, 0:512])
                        ve(mkev0(), [BK[6]], ["outSb"], f"{tg}ev0_{p}{j}")
                        def mkev1():
                            return lambda v: v.tensor_copy(outSb[64:128, 0:512],
                                                           bank[7][64:128, 0:512])
                        ve(mkev1(), [BK[7]], ["outSb"], f"{tg}ev1_{p}{j}")
                        def mkn0(j=j):
                            return lambda v: v.tensor_tensor(outN[0:64, j, 0:512],
                                                             outSb[0:64, 0:512],
                                                             bank[4][0:64, 0:512],
                                                             OP.mult)
                        ve(mkn0(), ["outSb", BK[4]], ["outN"], f"{tg}no0_{p}{j}")
                        def mkn1(j=j):
                            return lambda v: v.tensor_tensor(outN[64:128, j, 0:512],
                                                             outSb[64:128, 0:512],
                                                             bank[4][64:128, 0:512],
                                                             OP.mult)
                        ve(mkn1(), ["outSb", BK[4]], ["outN"], f"{tg}no1_{p}{j}")
                return emit_tail

            pend = []
            for j in range(8):
                for sub, h in ((0, 2 * j), (1, 2 * j + 1)):
                    rb = (h % 2) * 64
                    ptl = h // 2
                    pvb = 6 + sub
                    even = (sub == 0)
                    def mm(bk, cols, lhsT, rhs, st, sp, rd, tag):
                        def f(t, bk=bk, cols=cols, lhsT=lhsT, rhs=rhs, st=st, sp=sp):
                            return t.matmul(bank[bk][:, cols[0]:cols[1]],
                                            lhsT, rhs, start=st, stop=sp)
                        te(f, rd, [BK[bk]], tag)
                    krs = lambda seg, tl: kr_bf[rb:rb + 64, ptl, seg,
                                                tl * 128:tl * 128 + 128]
                    qrs = lambda c0, c1: qr_bf[rb:rb + 64, ptl, c0:c1]
                    if p > 0:
                        mm(0, (0, 128), krs(segP, 0), qrs(0, 128), True, False,
                           [("kr", segP), "qr_bf"], f"{tg}m0_{p}{h}")
                        mm(0, (0, 128), ident.ap(), mA1[:, 0:128], False, False,
                           ["ident", "mA1"], f"{tg}k0_{p}{h}")
                        mm(0, (128, 384), krs(segP, 1), qrs(0, 256), False, False,
                           [("kr", segP), "qr_bf"], f"{tg}m1_{p}{h}")
                        mm(0, (256, 384), ident.ap(), mA1[:, 0:128], False, False,
                           ["ident", "mA1"], f"{tg}k1_{p}{h}")
                    mm(0, (384, 512), krs(segC1, 1), qrs(384, 512), (p == 0), False,
                       [("kr", segC1), "qr_bf"], f"{tg}m5_{p}{h}")
                    mm(0, (384, 512), ident.ap(), mA2[:, 0:128], False, True,
                       ["ident", "mA2"], f"{tg}k5_{p}{h}")
                    mm(1, (0, 384), krs(segC0, 0), qrs(0, 384), True, False,
                       [("kr", segC0), "qr_bf"], f"{tg}m2_{p}{h}")
                    mm(1, (0, 128), ident.ap(), mA2[:, 0:128], False, False,
                       ["ident", "mA2"], f"{tg}k2a_{p}{h}")
                    mm(1, (256, 384), ident.ap(), mA1[:, 0:128], False, True,
                       ["ident", "mA1"], f"{tg}k2b_{p}{h}")
                    mm(2, (0, 384), krs(segC0, 1), qrs(128, 512), True, False,
                       [("kr", segC0), "qr_bf"], f"{tg}m3_{p}{h}")
                    mm(2, (0, 128), ident.ap(), mA2[:, 0:128], False, False,
                       ["ident", "mA2"], f"{tg}k3a_{p}{h}")
                    mm(2, (256, 384), ident.ap(), mA1[:, 0:128], False, True,
                       ["ident", "mA1"], f"{tg}k3b_{p}{h}")
                    mm(3, (0, 256), krs(segC1, 0), qrs(256, 512), True, False,
                       [("kr", segC1), "qr_bf"], f"{tg}m4_{p}{h}")
                    mm(3, (0, 128), ident.ap(), mA2[:, 0:128], False, True,
                       ["ident", "mA2"], f"{tg}k4_{p}{h}")
                    if p > 0:
                        def mkeA(sub=sub):
                            return lambda s: s.activation(eT[:, sub, 0:512],
                                                          bank[0][:, 0:512], AF.Exp,
                                                          bias=vneg12[:, 0:1])
                        se(mkeA(), [BK[0], "vneg12"], [("eT", sub)], f"{tg}eA_{p}{h}")
                    else:
                        def mkeA(sub=sub):
                            return lambda s: s.activation(eT[:, sub, 384:512],
                                                          bank[0][:, 384:512], AF.Exp,
                                                          bias=vneg12[:, 0:1])
                        se(mkeA(), [BK[0], "vneg12"], [("eT", sub)], f"{tg}eA_{p}{h}")
                    def mkeB(sub=sub):
                        return lambda s: s.activation(eT[:, sub, 512:896],
                                                      bank[1][:, 0:384], AF.Exp,
                                                      bias=vneg12[:, 0:1])
                    se(mkeB(), [BK[1], "vneg12"], [("eT", sub)], f"{tg}eB_{p}{h}")
                    def mkeC(sub=sub):
                        return lambda s: s.activation(eT[:, sub, 896:1280],
                                                      bank[2][:, 0:384], AF.Exp,
                                                      bias=vneg12[:, 0:1])
                    se(mkeC(), [BK[2], "vneg12"], [("eT", sub)], f"{tg}eC_{p}{h}")
                    def mkeD(sub=sub):
                        return lambda s: s.activation(eT[:, sub, 1280:1536],
                                                      bank[3][:, 0:256], AF.Exp,
                                                      bias=vneg12[:, 0:1])
                    se(mkeD(), [BK[3], "vneg12"], [("eT", sub)], f"{tg}eD_{p}{h}")
                    pend.append(mktail(j, sub, h, even, pvb))
                    if len(pend) > 1:
                        pend.pop(0)()
            for f_ in pend:
                f_()

            for dt in range(NDC if ATTN_CUT >= 12 else 0):
                bky = dt % 2
                wpar = dt % 2
                def mkwd(dt=dt, wpar=wpar):
                    return lambda s: s.dma_start(wo_sl[wpar].ap(), bWo[li, dt])
                P.add("sq", mkwd(), [("wc", "wo")], [("wo_sl", wpar)], dma=True,
                      tag=f"{tg}wod{p}_{dt}")
                for ic in range(NDC):
                    def mkwo(dt=dt, ic=ic, bky=bky, wpar=wpar):
                        return lambda t: t.matmul(bank[bky][:, 0:512],
                                                  wo_sl[wpar][:, ic, :], outN[:, ic, :],
                                                  start=(ic == 0), stop=(ic == NDC - 1))
                    te(mkwo(), [("wo_sl", wpar), "outN"], [BK[bky]], f"{tg}wo{p}_{dt}_{ic}")
                def mkres(dt=dt, bky=bky):
                    return lambda v: v.tensor_tensor(zf[:, dt, 1:513], zf[:, dt, 1:513],
                                                     bank[bky][:, 0:512], OP.add)
                ve(mkres(), ["zf", BK[bky]], ["zf"], f"{tg}res{p}_{dt}")
            sdma(zH[:, t0:t0 + PH_A].rearrange("(a q) t -> q a t", q=128),
                 zf[:, :, 1:PH_A + 1], ["zf"], hk("zH", t0, PH_A), f"{tg}out{p}")

        sc.close()
        P.barrier()


    def ffn_sublayer(li, tg):
        PHF = 512
        sc = ExitStack()
        S = lambda n, shp, d=BF16: sc.enter_context(nc.sbuf_tensor(tg + n, shp, d))
        w1a_sl = [S(f"w1a{i}", [128, NDC, 512]) for i in range(2)]
        w1g_sl = [S(f"w1g{i}", [128, NDC, 512]) for i in range(2)]
        w2_sl = [S(f"w2s{i}", [128, KC, 128]) for i in range(2)]
        xn_bf = S("xn_bf", [128, NDC, PHF])
        h2T = S("h2T", [128, KC, PHF])
        ge = S("ge", [128, 512])
        a2 = S("a2", [128, 512])
        t1f = S("t1f", [128, NDC, 512], F32)

        gdma_nc(p_g.ap(), fg[li].rearrange("(a q) -> q a", q=128), (), ["p_g"], f"{tg}g")
        gdma_nc(p_b.ap(), fb[li].rearrange("(a q) -> q a", q=128), (), ["p_b"], f"{tg}b")
        gdma_nc(p_b2.ap(), fb2[li].rearrange("(a q) -> q a", q=128), (), ["p_b2"], f"{tg}b2")
        gdma_nc(p_b1a.ap(), fb1a[li].rearrange("(k q) -> q k", q=128), (), ["p_b1a"], f"{tg}b1a")
        gdma_nc(p_b1g.ap(), fb1g[li].rearrange("(k q) -> q k", q=128), (), ["p_b1g"], f"{tg}b1g")
        gp(lambda g: g.memset(zf[:, :, 0:1], 0.0), ["zf"], ["zf"], f"{tg}zc")

        for p in range(SEQ // PHF):
            t0 = p * PHF
            sdma(zf[:, :, 1:PHF + 1],
                 zH[:, t0:t0 + PHF].rearrange("(a q) t -> q a t", q=128),
                 hk("zH", t0, PHF) + ["zf"], ["zf"], f"{tg}in{p}")
            if p > 0:
                ve(lambda v: v.tensor_copy(zf[:, :, 0:1], carryb.ap()),
                   ["carryb", "zf"], ["zf"], f"{tg}rcar{p}")
            ve(lambda v: v.tensor_copy(carryb.ap(), zf[:, :, PHF:PHF + 1]),
               ["zf"], ["carryb"], f"{tg}ccar{p}")
            se(lambda s: s.activation(xs_bf[:, :, 0:PHF + 1], zf[:, :, 0:PHF + 1],
                                      AF.Identity),
               ["zf"], ["xs_bf"], f"{tg}bf{p}")
            se(lambda s: s.activation(xsq_bf[:, :, 0:513], zf[:, :, 0:513], AF.Square),
               ["zf"], ["xsq_bf"], f"{tg}sq{p}")
            ln_stats_chunk(xs_bf, xsq_bf, "xs_bf", "xsq_bf", 0, 0, 512, True,
                           f"{tg}p{p}_")
            ln_bcast(512, f"{tg}p{p}_")
            for half, (d0, off) in enumerate(((0, 1), (4, 0))):
                bcA = bank[4][:, 0:512].unsqueeze(1).broadcast_to([128, 4, 512])
                bcB = bank[5][:, 0:512].unsqueeze(1).broadcast_to([128, 4, 512])
                def mk1(d0=d0, off=off, bcA=bcA):
                    return lambda v: v.tensor_tensor(
                        t1f[:, d0:d0 + 4, :], zf[:, d0:d0 + 4, off:off + 512],
                        bcA, OP.mult)
                ve(mk1(), ["zf", BK[4]], ["t1f"], f"{tg}a1_{p}{half}")
                def mk2(d0=d0, bcB=bcB):
                    return lambda v: v.tensor_tensor(
                        t1f[:, d0:d0 + 4, :], t1f[:, d0:d0 + 4, :], bcB, OP.subtract)
                ve(mk2(), ["t1f", BK[5]], ["t1f"], f"{tg}a2_{p}{half}")
            for dc in range(NDC):
                def mk3(dc=dc):
                    return lambda s: s.activation(xn_bf[:, dc, :], t1f[:, dc, :],
                                                  AF.Identity,
                                                  scale=p_g[:, dc:dc + 1],
                                                  bias=p_b[:, dc:dc + 1])
                se(mk3(), ["t1f", "p_g", "p_b"], ["xn_bf"], f"{tg}a3_{p}{dc}")
            for ft in range(KC):
                gidx = ft // 4
                par = gidx % 2
                if ft % 4 == 0:
                    for (sl, bsrc, kw) in ((w1a_sl, bW1a, "w1asl"),
                                           (w1g_sl, bW1g, "w1gsl")):
                        def mkw(sl=sl, bsrc=bsrc, par=par, gidx=gidx):
                            return lambda s: s.dma_start(sl[par].ap(), bsrc[li, gidx])
                        P.add("sq", mkw(), [("wc", kw[:3])], [(kw, par)], dma=True,
                              tag=f"{tg}w1d{p}_{gidx}_{kw}")
                fo = (ft % 4) * 128
                bka, bkg = ft % 2, 2 + ft % 2
                for dc in range(NDC):
                    def mka(dc=dc, fo=fo, par=par, bka=bka):
                        return lambda t: t.matmul(
                            bank[bka][:, 0:512], w1a_sl[par][:, dc, fo:fo + 128],
                            xn_bf[:, dc, :], start=(dc == 0), stop=(dc == NDC - 1))
                    te(mka(), [("w1asl", par), "xn_bf"], [BK[bka]],
                       f"{tg}mA{p}_{ft}_{dc}")
                    def mkg(dc=dc, fo=fo, par=par, bkg=bkg):
                        return lambda t: t.matmul(
                            bank[bkg][:, 0:512], w1g_sl[par][:, dc, fo:fo + 128],
                            xn_bf[:, dc, :], start=(dc == 0), stop=(dc == NDC - 1))
                    te(mkg(), [("w1gsl", par), "xn_bf"], [BK[bkg]],
                       f"{tg}mG{p}_{ft}_{dc}")
                def mkge(ft=ft, bkg=bkg):
                    return lambda s: s.activation(ge.ap(), bank[bkg][:, 0:512],
                                                  AF.Gelu, bias=p_b1g[:, ft:ft + 1])
                se(mkge(), [BK[bkg], "p_b1g"], ["ge"], f"{tg}ge{p}_{ft}")
                def mka2(ft=ft, bka=bka):
                    return lambda s: s.activation(a2.ap(), bank[bka][:, 0:512],
                                                  AF.Identity, bias=p_b1a[:, ft:ft + 1])
                se(mka2(), [BK[bka], "p_b1a"], ["a2"], f"{tg}ae{p}_{ft}")
                def mkh2(ft=ft):
                    return lambda v: v.tensor_tensor(h2T[:, ft, :], a2.ap(), ge.ap(),
                                                     OP.mult)
                ve(mkh2(), ["a2", "ge"], ["h2T"], f"{tg}h2_{p}_{ft}")
            for dt in range(NDC):
                par = dt % 2
                def mkw2(dt=dt, par=par):
                    return lambda s: s.dma_start(w2_sl[par].ap(), bW2[li, dt])
                P.add("sq", mkw2(), [("wc", "w2")], [("w2sl", par)], dma=True,
                      tag=f"{tg}w2d{p}_{dt}")
                bky = 4 + dt % 2
                for kc in range(KC):
                    def mky(dt=dt, kc=kc, par=par, bky=bky):
                        return lambda t: t.matmul(
                            bank[bky][:, 0:512], w2_sl[par][:, kc, :], h2T[:, kc, :],
                            start=(kc == 0), stop=(kc == KC - 1))
                    te(mky(), [("w2sl", par), "h2T"], [BK[bky]], f"{tg}y{p}_{dt}_{kc}")
                def mkres(dt=dt, bky=bky):
                    return lambda v: v.scalar_tensor_tensor(
                        zf[:, dt, 1:513], bank[bky][:, 0:512], p_b2[:, dt:dt + 1],
                        zf[:, dt, 1:513], OP.add, OP.add)
                ve(mkres(), ["zf", BK[bky], "p_b2"], ["zf"], f"{tg}res{p}_{dt}")
            sdma(zH[:, t0:t0 + PHF].rearrange("(a q) t -> q a t", q=128),
                 zf[:, :, 1:PHF + 1], ["zf"], hk("zH", t0, PHF), f"{tg}out{p}")
        sc.close()
        P.barrier()

    def combine_pass(xin, xin_name, xout, xout_name, tg):
        sc = ExitStack()
        zc2 = sc.enter_context(nc.sbuf_tensor(tg + "zc2", [128, NDC, 512], F32))
        for p in range(SEQ // 512):
            t0 = p * 512
            sdma(zf[:, :, 0:512],
                 xin[:, t0:t0 + 512].rearrange("(a q) t -> q a t", q=128),
                 hk(xin_name, t0, 512) + ["zf"], ["zf"], f"{tg}xi{p}")
            sdma(zc2.ap(), zH[:, t0:t0 + 512].rearrange("(a q) t -> q a t", q=128),
                 hk("zH", t0, 512) + (["zc2"] if p > 0 else []), ["zc2"], f"{tg}zi{p}")
            ve(lambda v: v.scalar_tensor_tensor(zf[:, :, 0:512], zc2.ap(), 0.5,
                                                zf[:, :, 0:512], OP.mult, OP.add),
               ["zf", "zc2"], ["zf"], f"{tg}cmb{p}")
            sdma(xout[:, t0:t0 + 512].rearrange("(a q) t -> q a t", q=128),
                 zf[:, :, 0:512], ["zf"], hk(xout_name, t0, 512), f"{tg}xo{p}")
        sc.close()
        P.barrier()

    def global_attn(tg):
        sc = ExitStack()
        S = lambda n, shp, d=BF16: sc.enter_context(nc.sbuf_tensor(tg + n, shp, d))
        kT = S("kTg", [128, 4, SEQ])
        v66g = S("v66g", [128, 32, 8, 65])
        scA = ExitStack()
        SA = lambda n, shp, d=BF16: scA.enter_context(nc.sbuf_tensor(tg + n, shp, d))
        wk = SA("wkg", [128, NDC, 512])
        wv = SA("wvg", [128, NDC, 512])
        st8 = SA("st8", [8, 512], F32)
        r_l8 = SA("r_l8", [1, 8 * 512])
        gp(lambda g: g.memset(v66g[:, :, :, 64:65], 1.0), (), [("v66g", "k")], f"{tg}vo")
        sdma(wk.ap(), baWk[:], [("wc", "awk")], ["wkg"], f"{tg}wk")
        sdma(wv.ap(), baWv[:], [("wc", "awv")], ["wvg"], f"{tg}wv")
        for p in range(SEQ // 512):
            t0 = p * 512
            sdma(zf[:, :, 1:513],
                 zH[:, t0:t0 + 512].rearrange("(a q) t -> q a t", q=128),
                 hk("zH", t0, 512) + ["zf"], ["zf"], f"{tg}ain{p}")
            se(lambda s: s.activation(xs_bf[:, :, 1:513], zf[:, :, 1:513], AF.Identity),
               ["zf"], ["xs_bf"], f"{tg}abf{p}")
            for pt in range(4):
                bkq = pt % 2
                for dc in range(NDC):
                    def mkk(pt=pt, dc=dc, bkq=bkq):
                        return lambda t: t.matmul(bank[bkq][:, 0:512],
                                                  wk[:, dc, pt * 128:(pt + 1) * 128],
                                                  xs_bf[:, dc, 1:513],
                                                  start=(dc == 0), stop=(dc == NDC - 1))
                    te(mkk(), ["wkg", "xs_bf"], [BK[bkq]], f"{tg}k{p}_{pt}_{dc}")
                def mksq(pt=pt, bkq=bkq):
                    return lambda s: s.activation(xsq_bf[:, pt, 0:512],
                                                  bank[bkq][:, 0:512], AF.Square)
                se(mksq(), [BK[bkq]], ["xsq_bf"], f"{tg}ksq{p}_{pt}")
                def mkev(pt=pt, bkq=bkq, t0=t0):
                    return lambda s: s.activation(kT[:, pt, t0:t0 + 512],
                                                  bank[bkq][:, 0:512], AF.Identity)
                se(mkev(), [BK[bkq]], ["kTg"], f"{tg}kev{p}_{pt}")
            for dc in range(4):
                def mks(dc=dc):
                    return lambda t: t.matmul(bank[6][0:8, 0:512], sel16[:, dc, 0:8],
                                              xsq_bf[:, dc, 0:512],
                                              start=(dc == 0), stop=(dc == 3))
                te(mks(), ["sel16", "xsq_bf"], [BK[6]], f"{tg}l2s{p}_{dc}")
            ve(lambda v: v.tensor_scalar(st8.ap(), bank[6][0:8, 0:512], 1e-16, None,
                                         OP.max),
               [BK[6]], ["st8"], f"{tg}l2m{p}")
            se(lambda s: s.activation(st8.ap(), st8.ap(), AF.Ln), ["st8"], ["st8"],
               f"{tg}l2l{p}")
            se(lambda s: s.activation(st8.ap(), st8.ap(), AF.Exp, scale=-0.5),
               ["st8"], ["st8"], f"{tg}l2e{p}")
            gdma(r_l8[0:1, :].rearrange("o (h t) -> o h t", h=8), st8.ap(),
                 ["st8"], ["r_l8"], f"{tg}l2r{p}")
            for pt in range(4):
                bkb = 4 + pt % 2
                h0, h1 = 2 * pt, 2 * pt + 1
                def mkb0(h0=h0, bkb=bkb):
                    return lambda t: t.matmul(bank[bkb][0:64, 0:512],
                                              onesrow[0:1, 0:64],
                                              r_l8[0:1, h0 * 512:(h0 + 1) * 512],
                                              start=True, stop=True)
                te(mkb0(), ["onesrow", "r_l8"], [BK[bkb]], f"{tg}bc{p}_{h0}")
                def mkb1(h1=h1, bkb=bkb):
                    return lambda t: t.matmul(bank[bkb][64:128, 0:512],
                                              onesrow[0:1, 0:64],
                                              r_l8[0:1, h1 * 512:(h1 + 1) * 512],
                                              start=True, stop=True)
                te(mkb1(), ["onesrow", "r_l8"], [BK[bkb]], f"{tg}bc{p}_{h1}")
                def mkn(pt=pt, bkb=bkb, t0=t0):
                    return lambda v: v.tensor_tensor(kT[:, pt, t0:t0 + 512],
                                                     kT[:, pt, t0:t0 + 512],
                                                     bank[bkb][:, 0:512], OP.mult)
                ve(mkn(), ["kTg", BK[bkb]], ["kTg"], f"{tg}kn{p}_{pt}")
            for tt in range(4):
                bkv = tt % 2
                for dc in range(NDC):
                    def mkv(tt=tt, dc=dc, bkv=bkv):
                        return lambda t: t.matmul(
                            bank[bkv][:, 0:512],
                            xs_bf[:, dc, 1 + tt * 128:1 + tt * 128 + 128],
                            wv[:, dc, 0:512],
                            start=(dc == 0), stop=(dc == NDC - 1))
                    te(mkv(), ["xs_bf", "wvg"], [BK[bkv]], f"{tg}v{p}_{tt}_{dc}")
                def mkve(tt=tt, bkv=bkv, p=p):
                    src2 = bank[bkv][:, 0:512].rearrange("q (h e) -> q h e", e=64)
                    return lambda s: s.activation(
                        v66g[:, p * 4 + tt, 0:8, 0:64], src2, AF.Identity)
                se(mkve(), [BK[bkv]], [("v66g", "d")], f"{tg}ve{p}_{tt}")
        scA.close()
        P.barrier()
        scB = ExitStack()
        SB = lambda n, shp, d=BF16: scB.enter_context(nc.sbuf_tensor(tg + n, shp, d))
        wq = SB("wqg", [128, NDC, 512])
        wao_sl = [SB(f"waog{i}", [64, 8, 128]) for i in range(2)]
        qT = SB("qTg", [128, 4, 512])
        eT2 = SB("eT2", [128, 2, 512])
        outNg = SB("outNg", [64, 8, 512])
        outSbg = SB("outSbg", [64, 512])
        recg = SB("recg", [128, 512])
        rec2 = SB("rec2", [128, 512], F32)
        st8b = SB("st8b", [8, 512], F32)
        r_l8b = SB("r_l8b", [1, 8 * 512])
        sdma(wq.ap(), baWq[:], [("wc", "awq")], ["wqg"], f"{tg}wq")

        for p in range(SEQ // 512):
            t0 = p * 512
            sdma(zf[:, :, 1:513],
                 zH[:, t0:t0 + 512].rearrange("(a q) t -> q a t", q=128),
                 hk("zH", t0, 512) + ["zf"], ["zf"], f"{tg}bin{p}")
            se(lambda s: s.activation(xs_bf[:, :, 1:513], zf[:, :, 1:513], AF.Identity),
               ["zf"], ["xs_bf"], f"{tg}bbf{p}")
            for pt in range(4):
                bkq = pt % 2
                for dc in range(NDC):
                    def mkq(pt=pt, dc=dc, bkq=bkq):
                        return lambda t: t.matmul(bank[bkq][:, 0:512],
                                                  wq[:, dc, pt * 128:(pt + 1) * 128],
                                                  xs_bf[:, dc, 1:513],
                                                  start=(dc == 0), stop=(dc == NDC - 1))
                    te(mkq(), ["wqg", "xs_bf"], [BK[bkq]], f"{tg}q{p}_{pt}_{dc}")
                def mksq(pt=pt, bkq=bkq):
                    return lambda s: s.activation(xsq_bf[:, pt, 0:512],
                                                  bank[bkq][:, 0:512], AF.Square)
                se(mksq(), [BK[bkq]], ["xsq_bf"], f"{tg}qsq{p}_{pt}")
                def mkev(pt=pt, bkq=bkq):
                    return lambda s: s.activation(qT[:, pt, :], bank[bkq][:, 0:512],
                                                  AF.Identity)
                se(mkev(), [BK[bkq]], ["qTg"], f"{tg}qev{p}_{pt}")
            for dc in range(4):
                def mks(dc=dc):
                    return lambda t: t.matmul(bank[6][0:8, 0:512], sel16[:, dc, 0:8],
                                              xsq_bf[:, dc, 0:512],
                                              start=(dc == 0), stop=(dc == 3))
                te(mks(), ["sel16", "xsq_bf"], [BK[6]], f"{tg}ql2{p}_{dc}")
            ve(lambda v: v.tensor_scalar(st8b.ap(), bank[6][0:8, 0:512], 1e-16, None,
                                         OP.max),
               [BK[6]], ["st8b"], f"{tg}qlm{p}")
            se(lambda s: s.activation(st8b.ap(), st8b.ap(), AF.Ln), ["st8b"], ["st8b"],
               f"{tg}qll{p}")
            se(lambda s: s.activation(st8b.ap(), st8b.ap(), AF.Exp, scale=-0.5),
               ["st8b"], ["st8b"], f"{tg}qle{p}")
            gdma(r_l8b[0:1, :].rearrange("o (h t) -> o h t", h=8), st8b.ap(),
                 ["st8b"], ["r_l8b"], f"{tg}qlr{p}")
            for pt in range(4):
                bkb = 4 + pt % 2
                h0, h1 = 2 * pt, 2 * pt + 1
                def mkb0(h0=h0, bkb=bkb):
                    return lambda t: t.matmul(bank[bkb][0:64, 0:512], c16row.ap(),
                                              r_l8b[0:1, h0 * 512:(h0 + 1) * 512],
                                              start=True, stop=True)
                te(mkb0(), ["c16row", "r_l8b"], [BK[bkb]], f"{tg}qbc{p}_{h0}")
                def mkb1(h1=h1, bkb=bkb):
                    return lambda t: t.matmul(bank[bkb][64:128, 0:512], c16row.ap(),
                                              r_l8b[0:1, h1 * 512:(h1 + 1) * 512],
                                              start=True, stop=True)
                te(mkb1(), ["c16row", "r_l8b"], [BK[bkb]], f"{tg}qbc{p}_{h1}")
                def mkn(pt=pt, bkb=bkb):
                    return lambda v: v.tensor_tensor(qT[:, pt, :], qT[:, pt, :],
                                                     bank[bkb][:, 0:512], OP.mult)
                ve(mkn(), ["qTg", BK[bkb]], ["qTg"], f"{tg}qn{p}_{pt}")
            for h in range(8):
                rb = (h % 2) * 64
                ptl = h // 2
                pvb = 6 + (h % 2)
                first = [True]
                for kb in range(8):
                    sb_ = 2 + (kb % 2)
                    for kt in range(4):
                        kc0 = kb * 512 + kt * 128
                        def mksim(kc0=kc0, sb_=sb_, rb=rb, ptl=ptl):
                            return lambda t: t.matmul(
                                bank[sb_][0:128, 0:512],
                                kT[rb:rb + 64, ptl, kc0:kc0 + 128],
                                qT[rb:rb + 64, ptl, 0:512],
                                start=True, stop=True)
                        te(mksim(), ["kTg", "qTg"], [BK[sb_]], f"{tg}s{p}_{h}_{kb}_{kt}")
                        def mke(sb_=sb_, kt=kt):
                            return lambda s: s.activation(eT2[:, sb_ - 2, :],
                                                          bank[sb_][:, 0:512], AF.Exp,
                                                          bias=vneg16[:, 0:1])
                        se(mke(), [BK[sb_], "vneg16"], [("eT2", sb_ - 2)],
                           f"{tg}e{p}_{h}_{kb}_{kt}")
                        def mkpv(kb=kb, kt=kt, h=h, pvb=pvb, sb_=sb_, st=first[0]):
                            return lambda t: t.matmul(
                                bank[pvb][0:65, 0:512],
                                v66g[:, kb * 4 + kt, h, 0:65], eT2[:, sb_ - 2, :],
                                start=st, stop=False, skip_group_check=True)
                        te(mkpv(), [("v66g", "d"), ("v66g", "k"), ("eT2", sb_ - 2)],
                           [BK[pvb]], f"{tg}pv{p}_{h}_{kb}_{kt}")
                        first[0] = False
                def mkcl(pvb=pvb):
                    return lambda v: v.tensor_scalar(rec2[64:65, 0:512],
                                                     bank[pvb][64:65, 0:512],
                                                     1e-6, None, OP.max)
                ve(mkcl(), [BK[pvb]], ["rec2"], f"{tg}cl{p}_{h}")
                def mkrl(pvb=pvb):
                    return lambda s: s.activation(rec2[64:65, 0:512],
                                                  rec2[64:65, 0:512], AF.Ln)
                se(mkrl(), ["rec2"], ["rec2"], f"{tg}rl{p}_{h}")
                def mkrc(pvb=pvb):
                    return lambda s: s.activation(recg[64:65, 0:512],
                                                  rec2[64:65, 0:512], AF.Exp,
                                                  scale=-1.0)
                se(mkrc(), ["rec2"], ["recg"], f"{tg}rc{p}_{h}")
                bkb = 4 + h % 2
                def mkbc(bkb=bkb):
                    return lambda t: t.matmul(bank[bkb][0:64, 0:512],
                                              onesrow[64:65, 0:64],
                                              recg[64:65, 0:512], start=True, stop=True)
                te(mkbc(), ["onesrow", "recg"], [BK[bkb]], f"{tg}bcv{p}_{h}")
                def mkevo(pvb=pvb):
                    return lambda s: s.activation(outSbg.ap(), bank[pvb][0:64, 0:512],
                                                  AF.Identity)
                se(mkevo(), [BK[pvb]], ["outSbg"], f"{tg}evo{p}_{h}")
                def mkno(h=h, bkb=bkb):
                    return lambda v: v.tensor_tensor(outNg[:, h, :], outSbg.ap(),
                                                     bank[bkb][0:64, 0:512], OP.mult)
                ve(mkno(), ["outSbg", BK[bkb]], ["outNg"], f"{tg}no{p}_{h}")
            sdma(zf[:, :, 513:1025], x1H[:, t0:t0 + 512].rearrange("(a q) t -> q a t", q=128),
                 hk("x1H", t0, 512) + ["zf"], ["zf"], f"{tg}xr{p}")
            for dt in range(NDC):
                bky = dt % 2
                wpar = dt % 2
                def mkwod(dt=dt, wpar=wpar):
                    return lambda s: s.dma_start(wao_sl[wpar].ap(), baWo[dt])
                P.add("sq", mkwod(), [("wc", "awo")], [("waog", wpar)], dma=True,
                      tag=f"{tg}wod{p}_{dt}")
                for hh in range(8):
                    def mkwo(hh=hh, bky=bky, wpar=wpar):
                        return lambda t: t.matmul(bank[bky][:, 0:512],
                                                  wao_sl[wpar][:, hh, :], outNg[:, hh, :],
                                                  start=(hh == 0), stop=(hh == 7))
                    te(mkwo(), [("waog", wpar), "outNg"], [BK[bky]],
                       f"{tg}wo{p}_{dt}_{hh}")
                def mkres(dt=dt, bky=bky):
                    return lambda v: v.tensor_tensor(zf[:, dt, 513:1025],
                                                     zf[:, dt, 513:1025],
                                                     bank[bky][:, 0:512], OP.add)
                ve(mkres(), ["zf", BK[bky]], ["zf"], f"{tg}res{p}_{dt}")
            sdma(x2H[:, t0:t0 + 512].rearrange("(a q) t -> q a t", q=128),
                 zf[:, :, 513:1025], ["zf"], hk("x2H", t0, 512), f"{tg}xw{p}")
        scB.close()
        sc.close()
        P.barrier()


    if stage >= 2:
        nlay = DEPTH if stage >= 4 else 1
        for li in range(nlay):
            attn_sublayer(li, f"A{li}_")
            if stage >= 3:
                ffn_sublayer(li, f"F{li}_")
    if stage >= 5:
        combine_pass(xT, "xT", x1H, "x1H", "c1_")
    if stage >= 6:
        ln_pass(x1H, "x1H", zH, "zH", "ln1")
        global_attn("G_")
    if stage >= 7:
        ln_pass(x2H, "x2H", zH, "zH", "ln2")
        for li in range(DEPTH):
            attn_sublayer(li, f"B{li}_")
            ffn_sublayer(li, f"C{li}_")
        combine_pass(x2H, "x2H", outT, "outT", "c2_")

    out_keys = []
    if dump in ("x1", "x2"):
        srcH = x1H if dump == "x1" else x2H
        for p in range(SEQ // PH_F):
            gdma(dumpH[:, p * PH_F:(p + 1) * PH_F], srcH[:, p * PH_F:(p + 1) * PH_F],
                 hk(dump + "H", p * PH_F, PH_F), [("dump", p)], f"dump{p}")
            out_keys.append(("dump", p))
    if dump == "z":
        for p in range(SEQ // PH_F):
            gdma(dumpH[:, p * PH_F:(p + 1) * PH_F], zH[:, p * PH_F:(p + 1) * PH_F],
                 hk("zH", p * PH_F, PH_F), [("dump", p)], f"dump{p}")
            out_keys.append(("dump", p))
    if stage >= 7:
        out_keys += [("hbm", "outT", i) for i in range(SEQ // PH_A)]
    else:
        for p in range(SEQ // PH_F):
            gdma(outT[:, p * PH_F:(p + 1) * PH_F], zH[:, p * PH_F:(p + 1) * PH_F],
                 hk("zH", p * PH_F, PH_F), [("out", p)], f"outcp{p}")
            out_keys.append(("out", p))
    P.add("sq", lambda s: None, out_keys, [("fin",)], tag="final_wait")

    P.analyze()
    print("PROG:", P.stats(), flush=True)
    with nc.Block() as block, nc.allow_low_precision(
            reason="bf16 rounding of normalizer rows; fp32 accumulation elsewhere"):
        P.emit_all(block, sems)
    es.close()
    return nc


def prep_inputs(inputs):
    x = np.asarray(inputs["L0_attn_x"], dtype=np.float32)
    Wqkv = np.asarray(inputs["L0_ffn_Wqkv"], dtype=np.float32)
    perm = np.concatenate([h * 64 + PERM64 for h in range(16)])
    fWqk = np.concatenate([Wqkv[:, :, :1024][:, :, perm],
                           Wqkv[:, :, 1024:2048][:, :, perm]], axis=2)
    fWqk = np.ascontiguousarray(
        fWqk.reshape(DEPTH, 8, 128, 16, 128).transpose(0, 3, 2, 1, 4))
    fWv = np.ascontiguousarray(
        Wqkv[:, :, 2048:].reshape(DEPTH, 8, 128, 1024).transpose(0, 2, 1, 3))
    Wo = np.asarray(inputs["L0_ffn_Wo"], dtype=np.float32)
    fWo = np.ascontiguousarray(
        Wo.reshape(DEPTH, 8, 128, 8, 128).transpose(0, 3, 2, 1, 4))
    W1 = np.asarray(inputs["L0_ffn_W1"], dtype=np.float32)
    b1 = np.asarray(inputs["L0_ffn_b1"], dtype=np.float32)
    W2 = np.asarray(inputs["L0_ffn_W2"], dtype=np.float32)
    padc = 128 * KC - 2730
    pad6 = 6 * 512 - 2730
    W1a = np.pad(W1[:, :, :2730], ((0, 0), (0, 0), (0, pad6)))
    W1g = np.pad(W1[:, :, 2730:], ((0, 0), (0, 0), (0, pad6)))
    fW1a = np.ascontiguousarray(
        W1a.reshape(DEPTH, 8, 128, 6, 512).transpose(0, 3, 2, 1, 4))
    fW1g = np.ascontiguousarray(
        W1g.reshape(DEPTH, 8, 128, 6, 512).transpose(0, 3, 2, 1, 4))
    W2p = np.pad(W2, ((0, 0), (0, padc), (0, 0)))
    fW2 = np.ascontiguousarray(
        W2p.reshape(DEPTH, KC, 128, 8, 128).transpose(0, 3, 2, 1, 4))
    aWq_ = np.asarray(inputs["L0_attn_Wq"], dtype=np.float32)
    aWkv_ = np.asarray(inputs["L0_attn_Wkv"], dtype=np.float32)
    aWo_ = np.asarray(inputs["L0_attn_Wo"], dtype=np.float32)
    aWq_t = np.ascontiguousarray(aWq_.reshape(8, 128, 512).transpose(1, 0, 2))
    aWk_t = np.ascontiguousarray(aWkv_[:, 0:512].reshape(8, 128, 512).transpose(1, 0, 2))
    aWv_t = np.ascontiguousarray(aWkv_[:, 512:1024].reshape(8, 128, 512).transpose(1, 0, 2))
    aWo_t = np.ascontiguousarray(aWo_.reshape(8, 64, 8, 128).transpose(2, 1, 0, 3))
    shared = {
        "aWq": aWq_t, "aWk": aWk_t, "aWv": aWv_t, "aWo": aWo_t,
        "fWqk": fWqk, "fWv": fWv, "fWo": fWo,
        "fW1a": fW1a, "fW1g": fW1g, "fW2": fW2,
        "fqs": np.ascontiguousarray(np.asarray(inputs["L0_ffn_q_scale"], np.float32)[:, PERM64]),
        "fks": np.ascontiguousarray(np.asarray(inputs["L0_ffn_k_scale"], np.float32)[:, PERM64]),
        "fg": np.ascontiguousarray(inputs["L0_ffn_ln_g"], dtype=np.float32),
        "fb": np.ascontiguousarray(inputs["L0_ffn_ln_b"], dtype=np.float32),
        "fb1a": np.ascontiguousarray(np.pad(b1[:, :2730], ((0, 0), (0, padc)))),
        "fb1g": np.ascontiguousarray(np.pad(b1[:, 2730:], ((0, 0), (0, padc)))),
        "fb2": np.ascontiguousarray(inputs["L0_ffn_b2"], dtype=np.float32),
    }
    in_maps = []
    for c in range(8):
        m = dict(shared)
        m["xT"] = np.ascontiguousarray(x[c].T)
        in_maps.append(m)
    return in_maps


def kernel(**inputs):
    in_maps = prep_inputs(inputs)
    nc = build_program(stage=99)
    res = run_bass_kernel_spmd(nc, in_maps, core_ids=list(range(8)))
    return np.stack([np.ascontiguousarray(r["outT"].T) for r in res.results])


# version 08
# speedup vs baseline: 1.0597x; 1.0213x over previous
import numpy as np
import ml_dtypes
from contextlib import ExitStack
import concourse.bass as bass
import concourse.mybir as mybir
from concourse.bass_utils import run_bass_kernel_spmd

F32 = mybir.dt.float32
BF16 = mybir.dt.bfloat16
AF = mybir.ActivationFunctionType
OP = mybir.AluOpType

SEQ = 4096
D = 1024
NDC = 8
DEPTH = 6
NEG = -1e30
PH_A = 512
PH_F = 1024
KC = 22
NDMASEM = 8
ATTN_CUT = 99

PERM64 = np.concatenate([np.arange(0, 16), np.arange(32, 48),
                         np.arange(16, 32), np.arange(48, 64)])


class _Op:
    __slots__ = ("idx", "stream", "emit", "reads", "writes", "dma",
                 "deps", "sdeps", "waits", "ticket", "marked", "tag",
                 "needs_drain", "slot")


class Prog:
    STREAMS = ("te", "ve", "se", "gp", "sq")

    def __init__(self):
        self.ops = []
        self.bars = []
        self.preset = set()

    def add(self, stream, emit, reads=(), writes=(), dma=False, tag=""):
        o = _Op()
        o.idx = len(self.ops)
        o.stream = stream
        o.emit = emit
        o.reads = tuple(reads)
        writes = tuple(writes)
        if stream in ("ve", "se"):
            for k in o.reads:
                if isinstance(k, tuple) and len(k) == 2 and k[0] == "bank":
                    writes = writes + (("bankrd", k[1]),)
        o.writes = tuple(writes)
        o.dma = dma
        o.tag = tag
        o.sdeps = []
        o.needs_drain = False
        o.slot = 0
        self.ops.append(o)
        return o

    def barrier(self):
        self.bars.append(len(self.ops))

    def analyze(self):
        last_w = {k: None for k in self.preset}
        readers = {}
        for o in self.ops:
            deps = set()
            for k in o.reads:
                if k not in last_w:
                    raise RuntimeError(f"read-before-write {k} op#{o.idx} {o.tag}")
                w = last_w[k]
                if w is not None:
                    deps.add(w)
                rd = readers.setdefault(k, {})
                if o.dma:
                    rd.setdefault("_dma", []).append(o)
                else:
                    rd[o.stream] = o
            for k in o.writes:
                pseudo = isinstance(k, tuple) and len(k) == 2 and k[0] == "bankrd"
                w = last_w.get(k)
                if w is not None and not (pseudo and w.stream == o.stream):
                    deps.add(w)
                for sk, r in readers.get(k, {}).items():
                    if sk == "_dma":
                        deps.update(r)
                    else:
                        deps.add(r)
                last_w[k] = o
                readers[k] = {}
            deps.discard(o)
            o.deps = [d for d in deps if d.stream != o.stream or d.dma]
            o.sdeps = [d for d in deps if d.stream == o.stream and not d.dma
                       and o.stream in ("ve", "se", "gp")]
        for bidx in self.bars:
            lasts = {}
            for o in self.ops[:bidx]:
                lasts[o.stream] = o
            firsts = {}
            for o in self.ops[bidx:]:
                if o.stream not in firsts:
                    firsts[o.stream] = o
            for s, o in firsts.items():
                for s2, d in lasts.items():
                    if s2 != s:
                        o.deps.append(d)
        for o in self.ops:
            o.marked = o.dma
        for o in self.ops:
            for d in o.deps:
                d.marked = True
            for d in o.sdeps:
                d.marked = True
        cnt = {}
        for o in self.ops:
            o.ticket = None
            if o.marked:
                if o.dma:
                    key = (o.stream, True)
                    n = cnt.get(key, 0)
                    cnt[key] = n + 1
                    o.slot = n % NDMASEM
                    o.ticket = (n // NDMASEM + 1) * 16
                else:
                    key = (o.stream, False)
                    cnt[key] = cnt.get(key, 0) + 1
                    o.slot = 0
                    o.ticket = cnt[key]
        hw = {s: {} for s in self.STREAMS}
        for o in self.ops:
            need = {}
            for d in o.deps:
                sk = (d.stream, d.dma, d.slot)
                if d.ticket > need.get(sk, 0):
                    need[sk] = d.ticket
            for d in o.sdeps:
                sk = (d.stream, d.dma, d.slot)
                if d.ticket > need.get(sk, 0):
                    need[sk] = d.ticket
            o.waits = []
            for sk in sorted(need, key=str):
                c = need[sk]
                if hw[o.stream].get(sk, 0) >= c:
                    continue
                hw[o.stream][sk] = c
                o.waits.append((sk, c))
        for o in self.ops:
            o.needs_drain = False

    def emit_all(self, block, sems):
        per = {s: [o for o in self.ops if o.stream == s] for s in self.STREAMS}
        sections = {"te": block.tensor, "ve": block.vector, "se": block.scalar,
                    "gp": block.gpsimd, "sq": block.sync}

        def mk(stream):
            def body(eng):
                for o in per[stream]:
                    if o.needs_drain:
                        eng.drain()
                    for sk, c in o.waits:
                        eng.wait_ge(sems[sk], c)
                    if o.dma and o.marked and o.ticket > 16:
                        eng.wait_ge(sems[(o.stream, True, o.slot)], o.ticket - 16)
                    instr = o.emit(eng)
                    if o.marked:
                        assert instr is not None, f"op {o.tag} marked, emit->None"
                        instr.then_inc(sems[(o.stream, o.dma, o.slot)],
                                       16 if o.dma else 1)
            return body

        for s in self.STREAMS:
            if per[s]:
                sections[s](mk(s))

    def stats(self):
        by = {}
        for o in self.ops:
            by[o.stream] = by.get(o.stream, 0) + 1
        return (f"ops={len(self.ops)} marked={sum(o.marked for o in self.ops)} "
                f"waits={sum(len(o.waits) for o in self.ops)} by={by}")


def build_program(stage=99, dump=None):
    nc = bass.Bass("TRN2", target_bir_lowering=False)
    P = Prog()
    te = lambda emit, r, w, tag="": P.add("te", emit, r, w, tag=tag)
    ve = lambda emit, r, w, tag="": P.add("ve", emit, r, w, tag=tag)
    se = lambda emit, r, w, tag="": P.add("se", emit, r, w, tag=tag)
    gp = lambda emit, r, w, tag="": P.add("gp", emit, r, w, tag=tag)

    def gdma(dst, src, r, w, tag=""):
        P.add("gp", lambda g: g.dma_start(dst, src), r, w, dma=True, tag=tag)

    def gdma_nc(dst, src, r, w, tag=""):
        P.add("gp", lambda g: g.dma_start(dst, src, allow_slow_non_contiguous=True),
              r, w, dma=True, tag=tag)

    def sdma(dst, src, r, w, tag=""):
        P.add("sq", lambda s: s.dma_start(dst, src), r, w, dma=True, tag=tag)

    dram = lambda name, shape, dt, kind="Internal": nc.dram_tensor(
        name, shape, dt, kind=kind).ap()
    xT = dram("xT", [D, SEQ], F32, "ExternalInput")
    aWq = dram("aWq", [128, NDC, 512], F32, "ExternalInput")
    aWk = dram("aWk", [128, NDC, 512], F32, "ExternalInput")
    aWv = dram("aWv", [128, NDC, 512], F32, "ExternalInput")
    aWo = dram("aWo", [NDC, 64, 8, 128], F32, "ExternalInput")
    fWqk = dram("fWqk", [DEPTH, 16, 128, NDC, 128], F32, "ExternalInput")
    fWv = dram("fWv", [DEPTH, 128, NDC, 1024], F32, "ExternalInput")
    fWo = dram("fWo", [DEPTH, NDC, 128, NDC, 128], F32, "ExternalInput")
    fW1a = dram("fW1a", [DEPTH, 6, 128, NDC, 512], F32, "ExternalInput")
    fW1g = dram("fW1g", [DEPTH, 6, 128, NDC, 512], F32, "ExternalInput")
    fW2 = dram("fW2", [DEPTH, NDC, 128, KC, 128], F32, "ExternalInput")
    fqs = dram("fqs", [DEPTH, 64], F32, "ExternalInput")
    fks = dram("fks", [DEPTH, 64], F32, "ExternalInput")
    fg = dram("fg", [DEPTH, D], F32, "ExternalInput")
    fb = dram("fb", [DEPTH, D], F32, "ExternalInput")
    fb1a = dram("fb1a", [DEPTH, 128 * KC], F32, "ExternalInput")
    fb1g = dram("fb1g", [DEPTH, 128 * KC], F32, "ExternalInput")
    fb2 = dram("fb2", [DEPTH, D], F32, "ExternalInput")
    outT = dram("outT", [D, SEQ], F32, "ExternalOutput")

    zH = dram("zH", [D, SEQ], F32)
    x1H = dram("x1H", [D, SEQ], F32)
    x2H = dram("x2H", [D, SEQ], F32)
    bWqk = dram("bWqk", [DEPTH, 16, 128, NDC, 128], BF16)
    bWv = dram("bWv", [DEPTH, 128, NDC, 1024], BF16)
    bWo = dram("bWo", [DEPTH, NDC, 128, NDC, 128], BF16)
    bW1a = dram("bW1a", [DEPTH, 6, 128, NDC, 512], BF16)
    bW1g = dram("bW1g", [DEPTH, 6, 128, NDC, 512], BF16)
    bW2 = dram("bW2", [DEPTH, NDC, 128, KC, 128], BF16)
    baWq = dram("baWq", [128, NDC, 512], BF16)
    baWk = dram("baWk", [128, NDC, 512], BF16)
    baWv = dram("baWv", [128, NDC, 512], BF16)
    baWo = dram("baWo", [NDC, 64, 8, 128], BF16)

    dumpH = None
    if dump is not None:
        dumpH = dram("dumpH", [D, SEQ], F32, "ExternalOutput")

    NCH = SEQ // PH_A
    P.preset |= {("hbm", "xT", i) for i in range(NCH)}

    def hk(name, t0, n):
        return [("hbm", name, i) for i in range(t0 // PH_A, (t0 + n + PH_A - 1) // PH_A)]

    es = ExitStack()
    A = lambda cm: es.enter_context(cm)
    sb = lambda name, shape, dt=F32: A(nc.sbuf_tensor(name, shape, dt))

    t_csq = sb("t_csq", [128, SEQ], BF16)
    t_snq = sb("t_snq", [128, SEQ], BF16)
    t_csk = sb("t_csk", [128, SEQ], BF16)
    t_snk = sb("t_snk", [128, SEQ], BF16)
    onescol = sb("onescol", [128, 1], BF16)
    onesrow = sb("onesrow", [128, 128], BF16)
    ident = sb("ident", [128, 128], BF16)
    mA1 = sb("mA1", [128, 128], BF16)
    mA2 = sb("mA2", [128, 128], BF16)
    sel16 = sb("sel16", [128, NDC, 16], BF16)
    vneg12 = sb("vneg12", [128, 1], F32)
    vneg16 = sb("vneg16", [128, 1], F32)
    ceps = sb("ceps", [1, 1], F32)
    cepsl2 = sb("cepsl2", [1, 1], F32)
    c16row = sb("c16row", [1, 64], BF16)
    p_qs = sb("p_qs", [128, 1], F32)
    p_ks = sb("p_ks", [128, 1], F32)
    p_g = sb("p_g", [128, NDC], F32)
    p_b = sb("p_b", [128, NDC], F32)
    p_b1a = sb("p_b1a", [128, KC], F32)
    p_b1g = sb("p_b1g", [128, KC], F32)
    p_b2 = sb("p_b2", [128, NDC], F32)
    zf = sb("zf", [128, NDC, PH_F + 1], F32)
    xs_bf = sb("xs_bf", [128, NDC, PH_F + 1], BF16)
    xsq_bf = sb("xsq_bf", [128, NDC, 513], BF16)
    carryb = sb("carryb", [128, NDC, 1], F32)
    r_mu = sb("r_mu", [1, 512], F32)
    r_s2 = sb("r_s2", [1, 512], F32)
    r_var = sb("r_var", [1, 512], F32)
    r_rstd = sb("r_rstd", [1, 512], BF16)
    r_mrs = sb("r_mrs", [1, 512], BF16)

    bank = [A(nc.psum_tensor(f"pb{i}", [128, 512], F32)) for i in range(8)]
    BK = [("bank", i) for i in range(8)]

    sems = {}
    for s in Prog.STREAMS:
        sems[(s, False, 0)] = A(nc.semaphore(f"sem_{s}"))
    for s in ("sq", "gp"):
        for i in range(NDMASEM):
            sems[(s, True, i)] = A(nc.semaphore(f"sem_{s}d{i}"))

    for (dst, src, n) in [(bWqk, fWqk, "wqk"), (bWv, fWv, "wv"), (bWo, fWo, "wo"),
                          (bW1a, fW1a, "w1a"), (bW1g, fW1g, "w1g"),
                          (bW2, fW2, "w2"), (baWq, aWq, "awq"),
                          (baWk, aWk, "awk"), (baWv, aWv, "awv"),
                          (baWo, aWo, "awo")]:
        gdma(dst, src, (), [("wc", n)], tag=f"cast_{n}")

    gp(lambda g: g.memset(onescol.ap(), 1.0), (), ["onescol"], "ones")
    gp(lambda g: g.memset(onesrow.ap(), 1.0), (), ["onesrow"], "onesr")
    gp(lambda g: g.memset(vneg12.ap(), -12.0), (), ["vneg12"], "v12")
    gp(lambda g: g.memset(vneg16.ap(), -16.0), (), ["vneg16"], "v16")
    gp(lambda g: g.memset(ceps.ap(), 1e-5), (), ["ceps"], "eps")
    gp(lambda g: g.memset(c16row.ap(), 16.0), (), ["c16row"], "c16")
    gp(lambda g: g.memset(cepsl2.ap(), 1e-16), (), ["cepsl2"], "eps2")

    tb = ExitStack()
    T = lambda n, s, d=F32: tb.enter_context(nc.sbuf_tensor(n, s, d))
    itr, itc, tmsk = T("itr", [128, 128]), T("itc", [128, 1]), T("tmsk", [128, 128])
    w_i, w_c, w_if, w_lb = T("w_i", [32, 1]), T("w_c", [32, 1]), T("w_if", [32, 1]), T("w_lb", [32, 1])
    w_fr, w_x, w_k = T("w_fr", [32, SEQ]), T("w_x", [32, SEQ]), T("w_k", [32, SEQ])
    w_sin, w_cos = T("w_sin", [32, SEQ]), T("w_cos", [32, SEQ])
    w_o1, w_o2 = T("w_o1", [32, SEQ], BF16), T("w_o2", [32, SEQ], BF16)

    gp(lambda g: g.iota(itr.ap(), [[1, 128]], base=0, channel_multiplier=0,
                        allow_small_or_imprecise_dtypes=True), (), ["itr"], "itr")
    gp(lambda g: g.iota(itc.ap(), [[1, 1]], base=0, channel_multiplier=1,
                        allow_small_or_imprecise_dtypes=True), (), ["itc"], "itc")
    ve(lambda v: v.tensor_scalar(ident.ap(), itr.ap(), itc.ap(), None, OP.is_equal),
       ["itr", "itc"], ["ident"], "ident")
    ve(lambda v: v.tensor_scalar(tmsk.ap(), itr.ap(), itc.ap(), None, OP.is_gt),
       ["itr", "itc"], ["tmsk"], "m1a")
    ve(lambda v: v.tensor_scalar(mA1.ap(), tmsk.ap(), NEG, None, OP.mult),
       ["tmsk"], ["mA1"], "m1b")
    ve(lambda v: v.tensor_scalar(tmsk.ap(), itr.ap(), itc.ap(), None, OP.is_lt),
       ["itr", "itc"], ["tmsk"], "m2a")
    ve(lambda v: v.tensor_scalar(mA2.ap(), tmsk.ap(), NEG, None, OP.mult),
       ["tmsk"], ["mA2"], "m2b")
    hcol = tb.enter_context(nc.sbuf_tensor("hcol", [128, 1], F32))
    rge = tb.enter_context(nc.sbuf_tensor("rge", [128, 1], F32))
    ve(lambda v: v.tensor_scalar(rge.ap(), itc.ap(), 64.0, None, OP.is_ge),
       ["itc"], ["rge"], "rge")
    for dc in range(NDC):
        def mh(dc=dc):
            return lambda v: v.tensor_scalar(hcol.ap(), rge.ap(), 1.0, float(2 * dc),
                                             OP.mult, OP.add)
        ve(mh(), ["rge"], ["hcol"], f"hc{dc}")
        def ms(dc=dc):
            return lambda v: v.tensor_scalar(sel16[:, dc, :], itr[:, 0:16],
                                             hcol.ap(), None, OP.is_equal)
        ve(ms(), ["hcol", "itr"], ["sel16"], f"s16_{dc}")

    gp(lambda g: g.iota(w_i.ap(), [[1, 1]], base=0, channel_multiplier=1,
                        allow_small_or_imprecise_dtypes=True), (), ["w_i"], "w_i")
    gp(lambda g: g.iota(w_fr.ap(), [[1, SEQ]], base=0, channel_multiplier=0,
                        allow_small_or_imprecise_dtypes=True), (), ["w_fr"], "w_t")
    gp(lambda g: g.memset(w_c.ap(), 10000.0), (), ["w_c"], "c4")
    se(lambda s: s.activation(w_c.ap(), w_c.ap(), AF.Ln), ["w_c"], ["w_c"], "ln4")
    ve(lambda v: v.tensor_scalar(w_if.ap(), w_i.ap(), -1.0 / 32.0, None, OP.mult),
       ["w_i"], ["w_if"], "nx")
    se(lambda s: s.activation(w_if.ap(), w_if.ap(), AF.Exp, scale=w_c.ap()),
       ["w_if", "w_c"], ["w_if"], "invf")
    ve(lambda v: v.tensor_scalar(w_fr.ap(), w_fr.ap(), w_if.ap(), None, OP.mult),
       ["w_fr", "w_if"], ["w_fr"], "fr")

    TWO23 = 8388608.0
    INV2PI = float(np.float32(1.0 / (2 * np.pi)))
    c2pi = 2 * np.pi
    C1 = float(np.float32(int(c2pi * 2**10) / 2**10))
    C2 = float(np.float32(c2pi - C1))
    C3 = float(c2pi - C1 - np.float64(C2))
    PI2 = float(np.float32(np.pi / 2))
    PI2L = float(np.pi / 2 - np.float64(PI2))

    def sin_of(out, shift, tg):
        xin, xk = w_fr, "w_fr"
        if shift:
            ve(lambda v: v.tensor_scalar(w_x.ap(), w_fr.ap(), PI2, PI2L, OP.add, OP.add),
               ["w_fr"], ["w_x"], tg + "x")
            xin, xk = w_x, "w_x"
        ve(lambda v: v.tensor_scalar(w_k.ap(), xin.ap(), INV2PI, TWO23, OP.mult, OP.add),
           [xk], ["w_k"], tg + "k1")
        ve(lambda v: v.tensor_scalar(w_k.ap(), w_k.ap(), TWO23, None, OP.subtract),
           ["w_k"], ["w_k"], tg + "k")
        ve(lambda v: v.scalar_tensor_tensor(out.ap(), w_k.ap(), -C1, xin.ap(),
                                            OP.mult, OP.add),
           ["w_k", xk], [tg + "out"], tg + "r1")
        ve(lambda v: v.scalar_tensor_tensor(out.ap(), w_k.ap(), -C2, out.ap(),
                                            OP.mult, OP.add),
           ["w_k", tg + "out"], [tg + "out"], tg + "r2")
        ve(lambda v: v.scalar_tensor_tensor(out.ap(), w_k.ap(), -C3, out.ap(),
                                            OP.mult, OP.add),
           ["w_k", tg + "out"], [tg + "out"], tg + "r3")
        T2 = 3.1415925
        TWOPI = float(2 * np.pi)
        ve(lambda v: v.tensor_scalar(w_k.ap(), out.ap(), T2, -TWOPI,
                                     OP.is_gt, OP.mult),
           [tg + "out", "w_k"], ["w_k"], tg + "wg")
        ve(lambda v: v.tensor_tensor(out.ap(), out.ap(), w_k.ap(), OP.add),
           [tg + "out", "w_k"], [tg + "out"], tg + "wga")
        ve(lambda v: v.tensor_scalar(w_k.ap(), out.ap(), -T2, TWOPI,
                                     OP.is_lt, OP.mult),
           [tg + "out", "w_k"], ["w_k"], tg + "wl")
        ve(lambda v: v.tensor_tensor(out.ap(), out.ap(), w_k.ap(), OP.add),
           [tg + "out", "w_k"], [tg + "out"], tg + "wla")
        ve(lambda v: v.tensor_scalar(out.ap(), out.ap(), -T2, T2,
                                     OP.max, OP.min),
           [tg + "out"], [tg + "out"], tg + "cl")
        se(lambda s: s.activation(out.ap(), out.ap(), AF.Sin),
           [tg + "out"], [tg + "out"], tg + "sin")

    sin_of(w_sin, False, "ws")
    sin_of(w_cos, True, "wc")
    A256 = float(np.float32(25.6))
    A256L = float(25.6 - np.float64(A256))
    B896 = float(np.float32(89.6))
    B896L = float(89.6 - np.float64(B896))
    ve(lambda v: v.tensor_scalar(w_lb.ap(), w_i.ap(), 2.0, A256, OP.mult, OP.add),
       ["w_i"], ["w_lb"], "b1")
    ve(lambda v: v.tensor_scalar(w_lb.ap(), w_lb.ap(), A256L, None, OP.add),
       ["w_lb"], ["w_lb"], "b2")
    gp(lambda g: g.memset(w_c.ap(), B896), ["w_c"], ["w_c"], "d1")
    ve(lambda v: v.tensor_scalar(w_c.ap(), w_c.ap(), B896L, None, OP.add),
       ["w_c"], ["w_c"], "d2")
    ve(lambda v: v.reciprocal(w_c.ap(), w_c.ap()), ["w_c"], ["w_c"], "d3")
    ve(lambda v: v.tensor_scalar(w_lb.ap(), w_lb.ap(), w_c.ap(), None, OP.mult),
       ["w_lb", "w_c"], ["w_lb"], "base")
    se(lambda s: s.activation(w_lb.ap(), w_lb.ap(), AF.Ln), ["w_lb"], ["w_lb"], "lnb")
    gp(lambda g: g.iota(w_x.ap(), [[1, SEQ]], base=0, channel_multiplier=0,
                        allow_small_or_imprecise_dtypes=True), ["w_x"], ["w_x"], "t2")
    ve(lambda v: v.tensor_scalar(w_x.ap(), w_x.ap(), -2048.0, 1.0 / 128.0,
                                 OP.add, OP.mult),
       ["w_x"], ["w_x"], "pw")
    ve(lambda v: v.tensor_scalar(w_fr.ap(), w_x.ap(), w_lb.ap(), None, OP.mult),
       ["w_x", "w_lb", "w_fr"], ["w_fr"], "plnb")
    se(lambda s: s.activation(w_k.ap(), w_fr.ap(), AF.Exp, scale=-1.0),
       ["w_fr", "w_k"], ["w_k"], "isc")
    se(lambda s: s.activation(w_fr.ap(), w_fr.ap(), AF.Exp),
       ["w_fr"], ["w_fr"], "sc")

    def combine(outt, outk, srcA, srcAk, srcB, srcBk, scl, tg):
        ve(lambda v: v.scalar_tensor_tensor(outt.ap(), srcA.ap(), scl, srcB.ap(),
                                            OP.mult, OP.mult),
           [srcAk, srcBk], [outk], tg)

    def replicate_sn(tbl, tblk, kpos, kneg):
        for blk in range(2):
            r0 = blk * 64
            for (dst0, src, sk, s0) in ((0, w_o2, kneg, 0), (16, w_o1, kpos, 0),
                                        (32, w_o2, kneg, 16), (48, w_o1, kpos, 16)):
                def mk(dst0=dst0, src=src, s0=s0, r0=r0, tbl=tbl):
                    return lambda g: g.dma_start(tbl[r0 + dst0:r0 + dst0 + 16, :],
                                                 src[s0:s0 + 16, :])
                P.add("gp", mk(), [sk], [tblk], dma=True, tag=f"rep{tblk}{r0+dst0}")

    def replicate_cs(tbl, tblk, kcs):
        for blk in range(2):
            r0 = blk * 64
            for (dst0, s0) in ((0, 0), (16, 0), (32, 16), (48, 16)):
                def mk(dst0=dst0, s0=s0, r0=r0, tbl=tbl):
                    return lambda g: g.dma_start(tbl[r0 + dst0:r0 + dst0 + 16, :],
                                                 w_o1[s0:s0 + 16, :])
                P.add("gp", mk(), [kcs], [tblk], dma=True, tag=f"rep{tblk}{r0+dst0}")

    combine(w_o1, "w_o1", w_cos, "wcout", w_fr, "w_fr", 8.0, "csq32")
    replicate_cs(t_csq, "t_csq", "w_o1")
    combine(w_o1, "w_o1", w_sin, "wsout", w_fr, "w_fr", 8.0, "snqp")
    combine(w_o2, "w_o2", w_sin, "wsout", w_fr, "w_fr", -8.0, "snqn")
    replicate_sn(t_snq, "t_snq", "w_o1", "w_o2")
    combine(w_o1, "w_o1", w_cos, "wcout", w_k, "w_k", 1.0, "csk32")
    replicate_cs(t_csk, "t_csk", "w_o1")
    combine(w_o1, "w_o1", w_sin, "wsout", w_k, "w_k", 1.0, "snkp")
    combine(w_o2, "w_o2", w_sin, "wsout", w_k, "w_k", -1.0, "snkn")
    replicate_sn(t_snk, "t_snk", "w_o1", "w_o2")

    P.barrier()
    tb.close()

    def ln_stats_chunk(src_bf, sq_bf, ksrc, ksq, coff, coff_sq, n, shifted, tg):
        for st, (buf, kin, bk, cbase) in enumerate(
                ((src_bf, ksrc, 6, coff), (sq_bf, ksq, 7, coff_sq))):
            for dc in range(NDC):
                off = cbase + (0 if (shifted and dc >= 4) else 1)
                def mk(dc=dc, off=off, bk=bk, buf=buf, n=n):
                    return lambda t: t.matmul(bank[bk][0:1, 0:n], onescol.ap(),
                                              buf[:, dc, off:off + n],
                                              start=(dc == 0), stop=(dc == NDC - 1))
                te(mk(), [kin, "onescol"], [BK[bk]], f"{tg}st{st}dc{dc}")
        se(lambda s: s.activation(r_mu[0:1, 0:n], bank[6][0:1, 0:n], AF.Identity,
                                  scale=1.0 / 1024.0),
           [BK[6]], ["r_mu"], f"{tg}mu")
        se(lambda s: s.activation(r_s2[0:1, 0:n], bank[7][0:1, 0:n], AF.Identity,
                                  scale=1.0 / 1024.0),
           [BK[7]], ["r_s2"], f"{tg}s2")
        ve(lambda v: v.tensor_tensor(r_var[0:1, 0:n], r_mu[0:1, 0:n],
                                     r_mu[0:1, 0:n], OP.mult),
           ["r_mu"], ["r_var"], f"{tg}mu2")
        ve(lambda v: v.tensor_tensor(r_var[0:1, 0:n], r_s2[0:1, 0:n],
                                     r_var[0:1, 0:n], OP.subtract),
           ["r_s2", "r_var"], ["r_var"], f"{tg}var")
        se(lambda s: s.activation(r_var[0:1, 0:n], r_var[0:1, 0:n], AF.Ln,
                                  bias=ceps[0:1, :]),
           ["r_var", "ceps"], ["r_var"], f"{tg}ln")
        se(lambda s: s.activation(r_rstd[0:1, 0:n], r_var[0:1, 0:n], AF.Exp,
                                  scale=-0.5),
           ["r_var"], ["r_rstd"], f"{tg}rstd")
        ve(lambda v: v.tensor_tensor(r_mrs[0:1, 0:n], r_mu[0:1, 0:n],
                                     r_rstd[0:1, 0:n], OP.mult),
           ["r_mu", "r_rstd"], ["r_mrs"], f"{tg}mrs")

    def ln_bcast(n, tg):
        te(lambda t: t.matmul(bank[4][:, 0:n], onesrow[0:1, :], r_rstd[0:1, 0:n],
                              start=True, stop=True),
           ["onesrow", "r_rstd"], [BK[4]], f"{tg}bcA")
        te(lambda t: t.matmul(bank[5][:, 0:n], onesrow[0:1, :], r_mrs[0:1, 0:n],
                              start=True, stop=True),
           ["onesrow", "r_mrs"], [BK[5]], f"{tg}bcB")

    def ln_pass(xin, xin_name, zout, zout_name, tg):
        for p in range(SEQ // PH_F):
            t0 = p * PH_F
            sdma(zf[:, :, 1:PH_F + 1],
                 xin[:, t0:t0 + PH_F].rearrange("(a q) t -> q a t", q=128),
                 hk(xin_name, t0, PH_F), ["zf"], f"{tg}in{p}")
            se(lambda s: s.activation(xs_bf[:, :, 1:PH_F + 1], zf[:, :, 1:PH_F + 1],
                                      AF.Identity),
               ["zf"], ["xs_bf"], f"{tg}bf{p}")
            for c in range(PH_F // 512):
                co = c * 512
                def mksq(co=co):
                    return lambda s: s.activation(xsq_bf[:, :, 1:513],
                                                  zf[:, :, co + 1:co + 513], AF.Square)
                se(mksq(), ["zf"], ["xsq_bf"], f"{tg}sq{p}_{c}")
                ln_stats_chunk(xs_bf, xsq_bf, "xs_bf", "xsq_bf", co, 0, 512, False,
                               f"{tg}p{p}c{c}_")
                ln_bcast(512, f"{tg}p{p}c{c}_")
                bcA = bank[4][:, 0:512].unsqueeze(1).broadcast_to([128, NDC, 512])
                bcB = bank[5][:, 0:512].unsqueeze(1).broadcast_to([128, NDC, 512])
                def mk1(co=co, bcA=bcA):
                    return lambda v: v.tensor_tensor(zf[:, :, 1 + co:513 + co],
                                                     zf[:, :, 1 + co:513 + co],
                                                     bcA, OP.mult)
                ve(mk1(), ["zf", BK[4]], ["zf"], f"{tg}ap1_{p}_{c}")
                def mk2(co=co, bcB=bcB):
                    return lambda v: v.tensor_tensor(zf[:, :, 1 + co:513 + co],
                                                     zf[:, :, 1 + co:513 + co],
                                                     bcB, OP.subtract)
                ve(mk2(), ["zf", BK[5]], ["zf"], f"{tg}ap2_{p}_{c}")
            sdma(zout[:, t0:t0 + PH_F].rearrange("(a q) t -> q a t", q=128),
                 zf[:, :, 1:PH_F + 1], ["zf"], hk(zout_name, t0, PH_F), f"{tg}out{p}")

    if stage >= 1:
        ln_pass(xT, "xT", zH, "zH", "ln0")
    def attn_sublayer(li, tg):
        sc = ExitStack()
        S = lambda n, shp, d=BF16: sc.enter_context(nc.sbuf_tensor(tg + n, shp, d))
        wqk_sl = [S(f"wqk{i}", [128, NDC, 128]) for i in range(2)]
        wv = S("wv", [128, NDC, 1024])
        wo_sl = [S(f"wo{i}", [128, NDC, 128]) for i in range(2)]
        q_bf = S("q_bf", [128, NDC, 512])
        k_bf = S("k_bf", [128, NDC, 512])
        qr_bf = S("qr_bf", [128, NDC, 512])
        kr_bf = S("kr_bf", [128, NDC, 3, 256])
        v66 = S("v66", [128, 3, 2, 16, 65])
        qsh = S("qsh", [128, 512])
        qm = S("qm", [128, 512])
        st_sb = S("st_sb", [16, 512], F32)
        r_l2 = S("r_l2", [65, 8 * 512])
        eT = S("eT", [128, 2, 1536])
        outSb = S("outSb", [128, 512])
        rec_sb = S("rec_sb", [128, 512])
        rec_f = S("rec_f", [128, 512], F32)
        outN = S("outN", [128, NDC, 512])

        gp(lambda g: g.memset(v66[:, :, :, :, 64:65], 1.0), (), [("v66", "k")],
           f"{tg}v66o")
        gdma(p_qs[0:64, 0:1], fqs[li, :], (), ["p_qs"], f"{tg}qs0")
        gdma(p_qs[64:128, 0:1], fqs[li, :], ["p_qs"], ["p_qs"], f"{tg}qs1")
        gdma(p_ks[0:64, 0:1], fks[li, :], (), ["p_ks"], f"{tg}ks0")
        gdma(p_ks[64:128, 0:1], fks[li, :], ["p_ks"], ["p_ks"], f"{tg}ks1")
        sdma(wv.ap(), bWv[li], [("wc", "wv")], ["wv"], f"{tg}wv")
        gp(lambda g: g.memset(zf[:, :, 0:1], 0.0), ["zf"], ["zf"], f"{tg}zc")

        NPH = SEQ // PH_A
        for p in range(NPH):
            t0 = p * PH_A
            sdma(zf[:, :, 1:PH_A + 1],
                 zH[:, t0:t0 + PH_A].rearrange("(a q) t -> q a t", q=128),
                 hk("zH", t0, PH_A) + ["zf"], ["zf"], f"{tg}in{p}")
            if p > 0:
                ve(lambda v: v.tensor_copy(zf[:, :, 0:1], carryb.ap()),
                   ["carryb", "zf"], ["zf"], f"{tg}rcar{p}")
            ve(lambda v: v.tensor_copy(carryb.ap(), zf[:, :, PH_A:PH_A + 1]),
               ["zf"], ["carryb"], f"{tg}ccar{p}")
            se(lambda s: s.activation(xs_bf[:, :, 0:PH_A + 1], zf[:, :, 0:PH_A + 1],
                                      AF.Identity),
               ["zf"], ["xs_bf"], f"{tg}bf{p}")

            for grp, (dst, pvec, kpv, tcs, tsn, tcsk, tsnk) in enumerate(
                    ((q_bf, p_qs, "p_qs", t_csq, t_snq, None, None),
                     (k_bf, p_ks, "p_ks", t_csk, t_snk, None, None))):
                cs_t = t_csq if grp == 0 else t_csk
                sn_t = t_snq if grp == 0 else t_snk
                kcs = "t_csq" if grp == 0 else "t_csk"
                ksn = "t_snq" if grp == 0 else "t_snk"
                kdst = "q_bf" if grp == 0 else "k_bf"
                for pt in range(NDC):
                    ft = grp * 8 + pt
                    par = pt % 2
                    bkq = par
                    def mkw(ft=ft, par=par):
                        return lambda s: s.dma_start(wqk_sl[par].ap(), bWqk[li, ft])
                    P.add("sq", mkw(), [("wc", "wqk")],
                          [("wqk_sl", par)], dma=True, tag=f"{tg}w{p}_{ft}")
                    for dc in range(NDC):
                        off = 0 if dc >= 4 else 1
                        def mkm(dc=dc, off=off, bkq=bkq, par=par):
                            return lambda t: t.matmul(
                                bank[bkq][:, 0:512], wqk_sl[par][:, dc, :],
                                xs_bf[:, dc, off:off + 512],
                                start=(dc == 0), stop=(dc == NDC - 1))
                        te(mkm(), [("wqk_sl", par), "xs_bf"], [BK[bkq]],
                           f"{tg}qk{p}_{ft}_{dc}")
                    def mke1(pt=pt, bkq=bkq):
                        return lambda s: s.activation(xsq_bf[:, pt, 0:512],
                                                      bank[bkq][:, 0:512], AF.Square)
                    se(mke1(), [BK[bkq]], ["xsq_bf"], f"{tg}sq{p}_{ft}")
                    def mke2(pt=pt, bkq=bkq, dst=dst, pvec=pvec):
                        return lambda s: s.activation(dst[:, pt, 0:512],
                                                      bank[bkq][:, 0:512], AF.Identity,
                                                      scale=pvec[:, 0:1])
                    se(mke2(), [BK[bkq], kpv], [kdst], f"{tg}ev{p}_{ft}")
                if ATTN_CUT < 2:
                    continue
                for dc in range(NDC):
                    def mks(dc=dc):
                        return lambda t: t.matmul(bank[6][0:16, 0:512],
                                                  sel16[:, dc, :], xsq_bf[:, dc, 0:512],
                                                  start=(dc == 0), stop=(dc == NDC - 1))
                    te(mks(), ["sel16", "xsq_bf"], [BK[6]], f"{tg}l2s{p}_{grp}_{dc}")
                ve(lambda v: v.tensor_scalar(st_sb.ap(), bank[6][0:16, 0:512],
                                             1e-16, None, OP.max),
                   [BK[6]], ["st_sb"], f"{tg}l2m{p}_{grp}")
                se(lambda s: s.activation(st_sb.ap(), st_sb.ap(), AF.Ln),
                   ["st_sb"], ["st_sb"], f"{tg}l2l{p}_{grp}")
                se(lambda s: s.activation(st_sb.ap(), st_sb.ap(), AF.Exp, scale=-0.5),
                   ["st_sb"], ["st_sb"], f"{tg}l2e{p}_{grp}")
                gdma(r_l2[0:1, :].rearrange("o (s t) -> o s t", s=8),
                     st_sb[0:8, :], ["st_sb"], ["r_l2"], f"{tg}l2r{p}_{grp}")
                gdma(r_l2[64:65, :].rearrange("o (s t) -> o s t", s=8),
                     st_sb[8:16, :], ["st_sb"], ["r_l2"], f"{tg}l2s{p}_{grp}")
                for pt in range(NDC):
                    bkb = 4 + pt % 2
                    h0, h1 = 2 * pt, 2 * pt + 1
                    def mkb0(h0=h0, bkb=bkb):
                        r, c = (h0 // 8) * 64, (h0 % 8) * 512
                        return lambda t: t.matmul(bank[bkb][0:64, 0:512],
                                                  onesrow[r:r + 1, 0:64],
                                                  r_l2[r:r + 1, c:c + 512],
                                                  start=True, stop=True)
                    te(mkb0(), ["onesrow", "r_l2"], [BK[bkb]], f"{tg}bc{p}_{grp}_{h0}")
                    def mkb1(h1=h1, bkb=bkb):
                        r, c = (h1 // 8) * 64, (h1 % 8) * 512
                        return lambda t: t.matmul(bank[bkb][64:128, 0:512],
                                                  onesrow[r:r + 1, 0:64],
                                                  r_l2[r:r + 1, c:c + 512],
                                                  start=True, stop=True)
                    te(mkb1(), ["onesrow", "r_l2"], [BK[bkb]], f"{tg}bc{p}_{grp}_{h1}")
                    def mkn(pt=pt, bkb=bkb, dst=dst):
                        return lambda v: v.tensor_tensor(dst[:, pt, :], dst[:, pt, :],
                                                         bank[bkb][:, 0:512], OP.mult)
                    ve(mkn(), [kdst, BK[bkb]], [kdst], f"{tg}n{p}_{grp}_{pt}")
                    SHUF = [(i + 16) % 32 for i in range(32)]
                    def mksh(pt=pt, dst=dst):
                        return lambda v: v.stream_shuffle(qsh.ap(), dst[:, pt, :], SHUF)
                    ve(mksh(), [kdst], ["qsh"], f"{tg}sh{p}_{grp}_{pt}")
                    def mkqm(pt=pt, dst=dst, cs_t=cs_t, t0=t0):
                        return lambda v: v.tensor_tensor(qm.ap(), dst[:, pt, :],
                                                         cs_t[:, t0:t0 + 512], OP.mult)
                    ve(mkqm(), [kdst, kcs], ["qm"], f"{tg}m{p}_{grp}_{pt}")
                    def mkrt(pt=pt, sn_t=sn_t, t0=t0):
                        return lambda v: v.tensor_tensor(qsh.ap(), qsh.ap(),
                                                         sn_t[:, t0:t0 + 512], OP.mult)
                    ve(mkrt(), ["qsh", ksn], ["qsh"], f"{tg}r{p}_{grp}_{pt}")
                    if grp == 0:
                        def mka(pt=pt):
                            return lambda v: v.tensor_tensor(qr_bf[:, pt, :], qm.ap(),
                                                             qsh.ap(), OP.add)
                        ve(mka(), ["qm", "qsh"], ["qr_bf"], f"{tg}a{p}_{grp}_{pt}")
                    else:
                        for half in range(2):
                            seg = (2 * p + half) % 3
                            def mka(pt=pt, half=half, seg=seg):
                                return lambda v: v.tensor_tensor(
                                    kr_bf[:, pt, seg, :],
                                    qm[:, half * 256:(half + 1) * 256],
                                    qsh[:, half * 256:(half + 1) * 256], OP.add)
                            ve(mka(), ["qm", "qsh"], [("kr", seg)],
                               f"{tg}a{p}_{grp}_{pt}_{half}")

            for tt in range(4 if ATTN_CUT >= 3 else 0):
                seg = (2 * p + tt // 2) % 3
                tile = tt % 2
                toff = 0 if True else 0
                for half in range(2):
                    bkv = half
                    for dc in range(NDC):
                        off = 0 if dc >= 4 else 1
                        def mkv(tt=tt, dc=dc, off=off, bkv=bkv, half=half):
                            return lambda t: t.matmul(
                                bank[bkv][:, 0:512],
                                xs_bf[:, dc, off + tt * 128: off + tt * 128 + 128],
                                wv[:, dc, half * 512:(half + 1) * 512],
                                start=(dc == 0), stop=(dc == NDC - 1))
                        te(mkv(), ["xs_bf", "wv"], [BK[bkv]], f"{tg}v{p}_{tt}_{half}_{dc}")
                    def mkve(tt=tt, bkv=bkv, half=half, seg=seg, tile=tile):
                        src = bank[bkv][:, 0:512].rearrange("q (h e) -> q h e", e=64)
                        return lambda s: s.activation(
                            v66[:, seg, tile, half * 8:(half + 1) * 8, 0:64], src,
                            AF.Identity)
                    se(mkve(), [BK[bkv]], [("v66", seg)], f"{tg}ve{p}_{tt}_{half}")

            segP = (2 * p - 1) % 3
            segC0 = (2 * p) % 3
            segC1 = (2 * p + 1) % 3
            def mktail(j, sub, h, even, pvb):
                def emit_tail():
                    if even:
                        vb = lambda sg, tl: v66[:, sg, tl, h, 0:65]
                        ob, oe = 0, 65
                    else:
                        vb = lambda sg, tl: v66[:, sg, tl, h, 0:64]
                        ob, oe = 64, 128
                    def pv(cols, ecols, lhsT, st, rd, tag):
                        def f(t, cols=cols, ecols=ecols, lhsT=lhsT, st=st):
                            return t.matmul(bank[pvb][ob:oe, cols[0]:cols[1]],
                                            lhsT, eT[:, sub, ecols[0]:ecols[1]],
                                            start=st, stop=False,
                                            skip_group_check=True)
                        te(f, rd + [("v66", "k"), ("eT", sub)], [BK[pvb]], tag)
                    pv((0, 384), (512, 896), vb(segC0, 0), True,
                       [("v66", segC0)], f"{tg}p2_{p}{h}")
                    if p > 0:
                        pv((0, 128), (0, 128), vb(segP, 0), False,
                           [("v66", segP)], f"{tg}p0_{p}{h}")
                        pv((0, 256), (128, 384), vb(segP, 1), False,
                           [("v66", segP)], f"{tg}p1_{p}{h}")
                    pv((128, 512), (896, 1280), vb(segC0, 1), False,
                       [("v66", segC0)], f"{tg}p3_{p}{h}")
                    pv((256, 512), (1280, 1536), vb(segC1, 0), False,
                       [("v66", segC1)], f"{tg}p4_{p}{h}")
                    pv((384, 512), (384, 512), vb(segC1, 1), False,
                       [("v66", segC1)], f"{tg}p5_{p}{h}")
                    if not even:
                        def rs(cols, ecols, st, tag):
                            def f(t, cols=cols, ecols=ecols, st=st):
                                return t.matmul(bank[pvb][32:33, cols[0]:cols[1]],
                                                onescol.ap(),
                                                eT[:, sub, ecols[0]:ecols[1]],
                                                start=st, stop=False,
                                                skip_group_check=True)
                            te(f, ["onescol", ("eT", sub)], [BK[pvb]], tag)
                        rs((0, 384), (512, 896), True, f"{tg}r2_{p}{h}")
                        if p > 0:
                            rs((0, 128), (0, 128), False, f"{tg}r0_{p}{h}")
                            rs((0, 256), (128, 384), False, f"{tg}r1_{p}{h}")
                        rs((128, 512), (896, 1280), False, f"{tg}r3_{p}{h}")
                        rs((256, 512), (1280, 1536), False, f"{tg}r4_{p}{h}")
                        rs((384, 512), (384, 512), False, f"{tg}r5_{p}{h}")
                        def mkr1a():
                            return lambda s: s.activation(rec_f[64:65, 0:512],
                                                          bank[6][64:65, 0:512],
                                                          AF.Ln)
                        se(mkr1a(), [BK[6]], ["rec_f"], f"{tg}rl0_{p}{j}")
                        def mkr1b():
                            return lambda s: s.activation(rec_sb[64:65, 0:512],
                                                          rec_f[64:65, 0:512],
                                                          AF.Exp, scale=-1.0)
                        se(mkr1b(), ["rec_f"], ["rec_sb"], f"{tg}rc0_{p}{j}")
                        def mkr2a():
                            return lambda s: s.activation(rec_f[32:33, 0:512],
                                                          bank[7][32:33, 0:512],
                                                          AF.Ln)
                        se(mkr2a(), [BK[7]], ["rec_f"], f"{tg}rl1_{p}{j}")
                        def mkr2b():
                            return lambda s: s.activation(rec_sb[32:33, 0:512],
                                                          rec_f[32:33, 0:512],
                                                          AF.Exp, scale=-1.0)
                        se(mkr2b(), ["rec_f"], ["rec_sb"], f"{tg}rc1_{p}{j}")
                        def mkbc0():
                            return lambda t: t.matmul(bank[4][0:64, 0:512],
                                                      onesrow[64:65, 0:64],
                                                      rec_sb[64:65, 0:512],
                                                      start=True, stop=True)
                        te(mkbc0(), ["onesrow", "rec_sb"], [BK[4]], f"{tg}bc0_{p}{j}")
                        def mkbc1():
                            return lambda t: t.matmul(bank[4][64:128, 0:512],
                                                      onesrow[32:33, 0:64],
                                                      rec_sb[32:33, 0:512],
                                                      start=True, stop=True)
                        te(mkbc1(), ["onesrow", "rec_sb"], [BK[4]], f"{tg}bc1_{p}{j}")
                        def mkev0():
                            return lambda v: v.tensor_copy(outSb[0:64, 0:512],
                                                           bank[6][0:64, 0:512])
                        ve(mkev0(), [BK[6]], ["outSb"], f"{tg}ev0_{p}{j}")
                        def mkev1():
                            return lambda v: v.tensor_copy(outSb[64:128, 0:512],
                                                           bank[7][64:128, 0:512])
                        ve(mkev1(), [BK[7]], ["outSb"], f"{tg}ev1_{p}{j}")
                        def mkn0(j=j):
                            return lambda v: v.tensor_tensor(outN[0:64, j, 0:512],
                                                             outSb[0:64, 0:512],
                                                             bank[4][0:64, 0:512],
                                                             OP.mult)
                        ve(mkn0(), ["outSb", BK[4]], ["outN"], f"{tg}no0_{p}{j}")
                        def mkn1(j=j):
                            return lambda v: v.tensor_tensor(outN[64:128, j, 0:512],
                                                             outSb[64:128, 0:512],
                                                             bank[4][64:128, 0:512],
                                                             OP.mult)
                        ve(mkn1(), ["outSb", BK[4]], ["outN"], f"{tg}no1_{p}{j}")
                return emit_tail

            pend = []
            for j in range(8):
                for sub, h in ((0, 2 * j), (1, 2 * j + 1)):
                    rb = (h % 2) * 64
                    ptl = h // 2
                    pvb = 6 + sub
                    even = (sub == 0)
                    def mm(bk, cols, lhsT, rhs, st, sp, rd, tag):
                        def f(t, bk=bk, cols=cols, lhsT=lhsT, rhs=rhs, st=st, sp=sp):
                            return t.matmul(bank[bk][:, cols[0]:cols[1]],
                                            lhsT, rhs, start=st, stop=sp)
                        te(f, rd, [BK[bk]], tag)
                    krs = lambda seg, tl: kr_bf[rb:rb + 64, ptl, seg,
                                                tl * 128:tl * 128 + 128]
                    qrs = lambda c0, c1: qr_bf[rb:rb + 64, ptl, c0:c1]
                    if p > 0:
                        mm(0, (0, 128), krs(segP, 0), qrs(0, 128), True, False,
                           [("kr", segP), "qr_bf"], f"{tg}m0_{p}{h}")
                        mm(0, (0, 128), ident.ap(), mA1[:, 0:128], False, False,
                           ["ident", "mA1"], f"{tg}k0_{p}{h}")
                        mm(0, (128, 384), krs(segP, 1), qrs(0, 256), False, False,
                           [("kr", segP), "qr_bf"], f"{tg}m1_{p}{h}")
                        mm(0, (256, 384), ident.ap(), mA1[:, 0:128], False, False,
                           ["ident", "mA1"], f"{tg}k1_{p}{h}")
                    mm(0, (384, 512), krs(segC1, 1), qrs(384, 512), (p == 0), False,
                       [("kr", segC1), "qr_bf"], f"{tg}m5_{p}{h}")
                    mm(0, (384, 512), ident.ap(), mA2[:, 0:128], False, True,
                       ["ident", "mA2"], f"{tg}k5_{p}{h}")
                    mm(1, (0, 384), krs(segC0, 0), qrs(0, 384), True, False,
                       [("kr", segC0), "qr_bf"], f"{tg}m2_{p}{h}")
                    mm(1, (0, 128), ident.ap(), mA2[:, 0:128], False, False,
                       ["ident", "mA2"], f"{tg}k2a_{p}{h}")
                    mm(1, (256, 384), ident.ap(), mA1[:, 0:128], False, True,
                       ["ident", "mA1"], f"{tg}k2b_{p}{h}")
                    mm(2, (0, 384), krs(segC0, 1), qrs(128, 512), True, False,
                       [("kr", segC0), "qr_bf"], f"{tg}m3_{p}{h}")
                    mm(2, (0, 128), ident.ap(), mA2[:, 0:128], False, False,
                       ["ident", "mA2"], f"{tg}k3a_{p}{h}")
                    mm(2, (256, 384), ident.ap(), mA1[:, 0:128], False, True,
                       ["ident", "mA1"], f"{tg}k3b_{p}{h}")
                    mm(3, (0, 256), krs(segC1, 0), qrs(256, 512), True, False,
                       [("kr", segC1), "qr_bf"], f"{tg}m4_{p}{h}")
                    mm(3, (0, 128), ident.ap(), mA2[:, 0:128], False, True,
                       ["ident", "mA2"], f"{tg}k4_{p}{h}")
                    if p > 0:
                        def mkeA(sub=sub):
                            return lambda s: s.activation(eT[:, sub, 0:512],
                                                          bank[0][:, 0:512], AF.Exp,
                                                          bias=vneg12[:, 0:1])
                        se(mkeA(), [BK[0], "vneg12"], [("eT", sub)], f"{tg}eA_{p}{h}")
                    else:
                        def mkeA(sub=sub):
                            return lambda s: s.activation(eT[:, sub, 384:512],
                                                          bank[0][:, 384:512], AF.Exp,
                                                          bias=vneg12[:, 0:1])
                        se(mkeA(), [BK[0], "vneg12"], [("eT", sub)], f"{tg}eA_{p}{h}")
                    def mkeB(sub=sub):
                        return lambda s: s.activation(eT[:, sub, 512:896],
                                                      bank[1][:, 0:384], AF.Exp,
                                                      bias=vneg12[:, 0:1])
                    se(mkeB(), [BK[1], "vneg12"], [("eT", sub)], f"{tg}eB_{p}{h}")
                    def mkeC(sub=sub):
                        return lambda s: s.activation(eT[:, sub, 896:1280],
                                                      bank[2][:, 0:384], AF.Exp,
                                                      bias=vneg12[:, 0:1])
                    se(mkeC(), [BK[2], "vneg12"], [("eT", sub)], f"{tg}eC_{p}{h}")
                    def mkeD(sub=sub):
                        return lambda s: s.activation(eT[:, sub, 1280:1536],
                                                      bank[3][:, 0:256], AF.Exp,
                                                      bias=vneg12[:, 0:1])
                    se(mkeD(), [BK[3], "vneg12"], [("eT", sub)], f"{tg}eD_{p}{h}")
                    pend.append(mktail(j, sub, h, even, pvb))
                    if len(pend) > 1:
                        pend.pop(0)()
            for f_ in pend:
                f_()

            for dt in range(NDC if ATTN_CUT >= 12 else 0):
                bky = dt % 2
                wpar = dt % 2
                def mkwd(dt=dt, wpar=wpar):
                    return lambda s: s.dma_start(wo_sl[wpar].ap(), bWo[li, dt])
                P.add("sq", mkwd(), [("wc", "wo")], [("wo_sl", wpar)], dma=True,
                      tag=f"{tg}wod{p}_{dt}")
                for ic in range(NDC):
                    def mkwo(dt=dt, ic=ic, bky=bky, wpar=wpar):
                        return lambda t: t.matmul(bank[bky][:, 0:512],
                                                  wo_sl[wpar][:, ic, :], outN[:, ic, :],
                                                  start=(ic == 0), stop=(ic == NDC - 1))
                    te(mkwo(), [("wo_sl", wpar), "outN"], [BK[bky]], f"{tg}wo{p}_{dt}_{ic}")
                def mkres(dt=dt, bky=bky):
                    return lambda v: v.tensor_tensor(zf[:, dt, 1:513], zf[:, dt, 1:513],
                                                     bank[bky][:, 0:512], OP.add)
                ve(mkres(), ["zf", BK[bky]], ["zf"], f"{tg}res{p}_{dt}")
            sdma(zH[:, t0:t0 + PH_A].rearrange("(a q) t -> q a t", q=128),
                 zf[:, :, 1:PH_A + 1], ["zf"], hk("zH", t0, PH_A), f"{tg}out{p}")

        sc.close()
        P.barrier()


    def ffn_sublayer(li, tg):
        PHF = 512
        sc = ExitStack()
        S = lambda n, shp, d=BF16: sc.enter_context(nc.sbuf_tensor(tg + n, shp, d))
        w1a_sl = [S(f"w1a{i}", [128, NDC, 512]) for i in range(2)]
        w1g_sl = [S(f"w1g{i}", [128, NDC, 512]) for i in range(2)]
        w2_sl = [S(f"w2s{i}", [128, KC, 128]) for i in range(2)]
        xn_bf = S("xn_bf", [128, NDC, PHF])
        h2T = S("h2T", [128, KC, PHF])
        ge = S("ge", [128, 512])
        a2 = S("a2", [128, 512])
        t1f = S("t1f", [128, NDC, 512], F32)

        gdma_nc(p_g.ap(), fg[li].rearrange("(a q) -> q a", q=128), (), ["p_g"], f"{tg}g")
        gdma_nc(p_b.ap(), fb[li].rearrange("(a q) -> q a", q=128), (), ["p_b"], f"{tg}b")
        gdma_nc(p_b2.ap(), fb2[li].rearrange("(a q) -> q a", q=128), (), ["p_b2"], f"{tg}b2")
        gdma_nc(p_b1a.ap(), fb1a[li].rearrange("(k q) -> q k", q=128), (), ["p_b1a"], f"{tg}b1a")
        gdma_nc(p_b1g.ap(), fb1g[li].rearrange("(k q) -> q k", q=128), (), ["p_b1g"], f"{tg}b1g")
        gp(lambda g: g.memset(zf[:, :, 0:1], 0.0), ["zf"], ["zf"], f"{tg}zc")

        for p in range(SEQ // PHF):
            t0 = p * PHF
            sdma(zf[:, :, 1:PHF + 1],
                 zH[:, t0:t0 + PHF].rearrange("(a q) t -> q a t", q=128),
                 hk("zH", t0, PHF) + ["zf"], ["zf"], f"{tg}in{p}")
            if p > 0:
                ve(lambda v: v.tensor_copy(zf[:, :, 0:1], carryb.ap()),
                   ["carryb", "zf"], ["zf"], f"{tg}rcar{p}")
            ve(lambda v: v.tensor_copy(carryb.ap(), zf[:, :, PHF:PHF + 1]),
               ["zf"], ["carryb"], f"{tg}ccar{p}")
            se(lambda s: s.activation(xs_bf[:, :, 0:PHF + 1], zf[:, :, 0:PHF + 1],
                                      AF.Identity),
               ["zf"], ["xs_bf"], f"{tg}bf{p}")
            se(lambda s: s.activation(xsq_bf[:, :, 0:513], zf[:, :, 0:513], AF.Square),
               ["zf"], ["xsq_bf"], f"{tg}sq{p}")
            ln_stats_chunk(xs_bf, xsq_bf, "xs_bf", "xsq_bf", 0, 0, 512, True,
                           f"{tg}p{p}_")
            ln_bcast(512, f"{tg}p{p}_")
            for half, (d0, off) in enumerate(((0, 1), (4, 0))):
                bcA = bank[4][:, 0:512].unsqueeze(1).broadcast_to([128, 4, 512])
                bcB = bank[5][:, 0:512].unsqueeze(1).broadcast_to([128, 4, 512])
                def mk1(d0=d0, off=off, bcA=bcA):
                    return lambda v: v.tensor_tensor(
                        t1f[:, d0:d0 + 4, :], zf[:, d0:d0 + 4, off:off + 512],
                        bcA, OP.mult)
                ve(mk1(), ["zf", BK[4]], ["t1f"], f"{tg}a1_{p}{half}")
                def mk2(d0=d0, bcB=bcB):
                    return lambda v: v.tensor_tensor(
                        t1f[:, d0:d0 + 4, :], t1f[:, d0:d0 + 4, :], bcB, OP.subtract)
                ve(mk2(), ["t1f", BK[5]], ["t1f"], f"{tg}a2_{p}{half}")
            for dc in range(NDC):
                def mk3(dc=dc):
                    return lambda s: s.activation(xn_bf[:, dc, :], t1f[:, dc, :],
                                                  AF.Identity,
                                                  scale=p_g[:, dc:dc + 1],
                                                  bias=p_b[:, dc:dc + 1])
                se(mk3(), ["t1f", "p_g", "p_b"], ["xn_bf"], f"{tg}a3_{p}{dc}")
            for ft in range(KC):
                gidx = ft // 4
                par = gidx % 2
                if ft % 4 == 0:
                    for (sl, bsrc, kw) in ((w1a_sl, bW1a, "w1asl"),
                                           (w1g_sl, bW1g, "w1gsl")):
                        def mkw(sl=sl, bsrc=bsrc, par=par, gidx=gidx):
                            return lambda s: s.dma_start(sl[par].ap(), bsrc[li, gidx])
                        P.add("sq", mkw(), [("wc", kw[:3])], [(kw, par)], dma=True,
                              tag=f"{tg}w1d{p}_{gidx}_{kw}")
                fo = (ft % 4) * 128
                bka, bkg = ft % 2, 2 + ft % 2
                for dc in range(NDC):
                    def mka(dc=dc, fo=fo, par=par, bka=bka):
                        return lambda t: t.matmul(
                            bank[bka][:, 0:512], w1a_sl[par][:, dc, fo:fo + 128],
                            xn_bf[:, dc, :], start=(dc == 0), stop=(dc == NDC - 1))
                    te(mka(), [("w1asl", par), "xn_bf"], [BK[bka]],
                       f"{tg}mA{p}_{ft}_{dc}")
                    def mkg(dc=dc, fo=fo, par=par, bkg=bkg):
                        return lambda t: t.matmul(
                            bank[bkg][:, 0:512], w1g_sl[par][:, dc, fo:fo + 128],
                            xn_bf[:, dc, :], start=(dc == 0), stop=(dc == NDC - 1))
                    te(mkg(), [("w1gsl", par), "xn_bf"], [BK[bkg]],
                       f"{tg}mG{p}_{ft}_{dc}")
                def mkge(ft=ft, bkg=bkg):
                    return lambda s: s.activation(ge.ap(), bank[bkg][:, 0:512],
                                                  AF.Gelu, bias=p_b1g[:, ft:ft + 1])
                se(mkge(), [BK[bkg], "p_b1g"], ["ge"], f"{tg}ge{p}_{ft}")
                def mka2(ft=ft, bka=bka):
                    return lambda s: s.activation(a2.ap(), bank[bka][:, 0:512],
                                                  AF.Identity, bias=p_b1a[:, ft:ft + 1])
                se(mka2(), [BK[bka], "p_b1a"], ["a2"], f"{tg}ae{p}_{ft}")
                def mkh2(ft=ft):
                    return lambda v: v.tensor_tensor(h2T[:, ft, :], a2.ap(), ge.ap(),
                                                     OP.mult)
                ve(mkh2(), ["a2", "ge"], ["h2T"], f"{tg}h2_{p}_{ft}")
            for dt in range(NDC):
                par = dt % 2
                def mkw2(dt=dt, par=par):
                    return lambda s: s.dma_start(w2_sl[par].ap(), bW2[li, dt])
                P.add("sq", mkw2(), [("wc", "w2")], [("w2sl", par)], dma=True,
                      tag=f"{tg}w2d{p}_{dt}")
                bky = 4 + dt % 2
                for kc in range(KC):
                    def mky(dt=dt, kc=kc, par=par, bky=bky):
                        return lambda t: t.matmul(
                            bank[bky][:, 0:512], w2_sl[par][:, kc, :], h2T[:, kc, :],
                            start=(kc == 0), stop=(kc == KC - 1))
                    te(mky(), [("w2sl", par), "h2T"], [BK[bky]], f"{tg}y{p}_{dt}_{kc}")
                def mkres(dt=dt, bky=bky):
                    return lambda v: v.scalar_tensor_tensor(
                        zf[:, dt, 1:513], bank[bky][:, 0:512], p_b2[:, dt:dt + 1],
                        zf[:, dt, 1:513], OP.add, OP.add)
                ve(mkres(), ["zf", BK[bky], "p_b2"], ["zf"], f"{tg}res{p}_{dt}")
            sdma(zH[:, t0:t0 + PHF].rearrange("(a q) t -> q a t", q=128),
                 zf[:, :, 1:PHF + 1], ["zf"], hk("zH", t0, PHF), f"{tg}out{p}")
        sc.close()
        P.barrier()

    def combine_pass(xin, xin_name, xout, xout_name, tg):
        sc = ExitStack()
        zc2 = sc.enter_context(nc.sbuf_tensor(tg + "zc2", [128, NDC, 512], F32))
        for p in range(SEQ // 512):
            t0 = p * 512
            sdma(zf[:, :, 0:512],
                 xin[:, t0:t0 + 512].rearrange("(a q) t -> q a t", q=128),
                 hk(xin_name, t0, 512) + ["zf"], ["zf"], f"{tg}xi{p}")
            sdma(zc2.ap(), zH[:, t0:t0 + 512].rearrange("(a q) t -> q a t", q=128),
                 hk("zH", t0, 512) + (["zc2"] if p > 0 else []), ["zc2"], f"{tg}zi{p}")
            ve(lambda v: v.scalar_tensor_tensor(zf[:, :, 0:512], zc2.ap(), 0.5,
                                                zf[:, :, 0:512], OP.mult, OP.add),
               ["zf", "zc2"], ["zf"], f"{tg}cmb{p}")
            sdma(xout[:, t0:t0 + 512].rearrange("(a q) t -> q a t", q=128),
                 zf[:, :, 0:512], ["zf"], hk(xout_name, t0, 512), f"{tg}xo{p}")
        sc.close()
        P.barrier()

    def global_attn(tg):
        sc = ExitStack()
        S = lambda n, shp, d=BF16: sc.enter_context(nc.sbuf_tensor(tg + n, shp, d))
        kT = S("kTg", [128, 4, SEQ])
        v66g = S("v66g", [128, 32, 8, 65])
        scA = ExitStack()
        SA = lambda n, shp, d=BF16: scA.enter_context(nc.sbuf_tensor(tg + n, shp, d))
        wk = SA("wkg", [128, NDC, 512])
        wv = SA("wvg", [128, NDC, 512])
        st8 = SA("st8", [8, 512], F32)
        r_l8 = SA("r_l8", [1, 8 * 512])
        gp(lambda g: g.memset(v66g[:, :, :, 64:65], 1.0), (), [("v66g", "k")], f"{tg}vo")
        sdma(wk.ap(), baWk[:], [("wc", "awk")], ["wkg"], f"{tg}wk")
        sdma(wv.ap(), baWv[:], [("wc", "awv")], ["wvg"], f"{tg}wv")
        for p in range(SEQ // 512):
            t0 = p * 512
            sdma(zf[:, :, 1:513],
                 zH[:, t0:t0 + 512].rearrange("(a q) t -> q a t", q=128),
                 hk("zH", t0, 512) + ["zf"], ["zf"], f"{tg}ain{p}")
            se(lambda s: s.activation(xs_bf[:, :, 1:513], zf[:, :, 1:513], AF.Identity),
               ["zf"], ["xs_bf"], f"{tg}abf{p}")
            for pt in range(4):
                bkq = pt % 2
                for dc in range(NDC):
                    def mkk(pt=pt, dc=dc, bkq=bkq):
                        return lambda t: t.matmul(bank[bkq][:, 0:512],
                                                  wk[:, dc, pt * 128:(pt + 1) * 128],
                                                  xs_bf[:, dc, 1:513],
                                                  start=(dc == 0), stop=(dc == NDC - 1))
                    te(mkk(), ["wkg", "xs_bf"], [BK[bkq]], f"{tg}k{p}_{pt}_{dc}")
                def mksq(pt=pt, bkq=bkq):
                    return lambda s: s.activation(xsq_bf[:, pt, 0:512],
                                                  bank[bkq][:, 0:512], AF.Square)
                se(mksq(), [BK[bkq]], ["xsq_bf"], f"{tg}ksq{p}_{pt}")
                def mkev(pt=pt, bkq=bkq, t0=t0):
                    return lambda s: s.activation(kT[:, pt, t0:t0 + 512],
                                                  bank[bkq][:, 0:512], AF.Identity)
                se(mkev(), [BK[bkq]], ["kTg"], f"{tg}kev{p}_{pt}")
            for dc in range(4):
                def mks(dc=dc):
                    return lambda t: t.matmul(bank[6][0:8, 0:512], sel16[:, dc, 0:8],
                                              xsq_bf[:, dc, 0:512],
                                              start=(dc == 0), stop=(dc == 3))
                te(mks(), ["sel16", "xsq_bf"], [BK[6]], f"{tg}l2s{p}_{dc}")
            ve(lambda v: v.tensor_scalar(st8.ap(), bank[6][0:8, 0:512], 1e-16, None,
                                         OP.max),
               [BK[6]], ["st8"], f"{tg}l2m{p}")
            se(lambda s: s.activation(st8.ap(), st8.ap(), AF.Ln), ["st8"], ["st8"],
               f"{tg}l2l{p}")
            se(lambda s: s.activation(st8.ap(), st8.ap(), AF.Exp, scale=-0.5),
               ["st8"], ["st8"], f"{tg}l2e{p}")
            gdma(r_l8[0:1, :].rearrange("o (h t) -> o h t", h=8), st8.ap(),
                 ["st8"], ["r_l8"], f"{tg}l2r{p}")
            for pt in range(4):
                bkb = 4 + pt % 2
                h0, h1 = 2 * pt, 2 * pt + 1
                def mkb0(h0=h0, bkb=bkb):
                    return lambda t: t.matmul(bank[bkb][0:64, 0:512],
                                              onesrow[0:1, 0:64],
                                              r_l8[0:1, h0 * 512:(h0 + 1) * 512],
                                              start=True, stop=True)
                te(mkb0(), ["onesrow", "r_l8"], [BK[bkb]], f"{tg}bc{p}_{h0}")
                def mkb1(h1=h1, bkb=bkb):
                    return lambda t: t.matmul(bank[bkb][64:128, 0:512],
                                              onesrow[0:1, 0:64],
                                              r_l8[0:1, h1 * 512:(h1 + 1) * 512],
                                              start=True, stop=True)
                te(mkb1(), ["onesrow", "r_l8"], [BK[bkb]], f"{tg}bc{p}_{h1}")
                def mkn(pt=pt, bkb=bkb, t0=t0):
                    return lambda v: v.tensor_tensor(kT[:, pt, t0:t0 + 512],
                                                     kT[:, pt, t0:t0 + 512],
                                                     bank[bkb][:, 0:512], OP.mult)
                ve(mkn(), ["kTg", BK[bkb]], ["kTg"], f"{tg}kn{p}_{pt}")
            for tt in range(4):
                bkv = tt % 2
                for dc in range(NDC):
                    def mkv(tt=tt, dc=dc, bkv=bkv):
                        return lambda t: t.matmul(
                            bank[bkv][:, 0:512],
                            xs_bf[:, dc, 1 + tt * 128:1 + tt * 128 + 128],
                            wv[:, dc, 0:512],
                            start=(dc == 0), stop=(dc == NDC - 1))
                    te(mkv(), ["xs_bf", "wvg"], [BK[bkv]], f"{tg}v{p}_{tt}_{dc}")
                def mkve(tt=tt, bkv=bkv, p=p):
                    src2 = bank[bkv][:, 0:512].rearrange("q (h e) -> q h e", e=64)
                    return lambda s: s.activation(
                        v66g[:, p * 4 + tt, 0:8, 0:64], src2, AF.Identity)
                se(mkve(), [BK[bkv]], [("v66g", "d")], f"{tg}ve{p}_{tt}")
        scA.close()
        P.barrier()
        scB = ExitStack()
        SB = lambda n, shp, d=BF16: scB.enter_context(nc.sbuf_tensor(tg + n, shp, d))
        wq = SB("wqg", [128, NDC, 512])
        wao_sl = [SB(f"waog{i}", [64, 8, 128]) for i in range(2)]
        qT = SB("qTg", [128, 4, 512])
        eT2 = SB("eT2", [128, 2, 512])
        outNg = SB("outNg", [64, 8, 512])
        outSbg = SB("outSbg", [64, 512])
        recg = SB("recg", [128, 512])
        rec2 = SB("rec2", [128, 512], F32)
        st8b = SB("st8b", [8, 512], F32)
        r_l8b = SB("r_l8b", [1, 8 * 512])
        sdma(wq.ap(), baWq[:], [("wc", "awq")], ["wqg"], f"{tg}wq")

        for p in range(SEQ // 512):
            t0 = p * 512
            sdma(zf[:, :, 1:513],
                 zH[:, t0:t0 + 512].rearrange("(a q) t -> q a t", q=128),
                 hk("zH", t0, 512) + ["zf"], ["zf"], f"{tg}bin{p}")
            se(lambda s: s.activation(xs_bf[:, :, 1:513], zf[:, :, 1:513], AF.Identity),
               ["zf"], ["xs_bf"], f"{tg}bbf{p}")
            for pt in range(4):
                bkq = pt % 2
                for dc in range(NDC):
                    def mkq(pt=pt, dc=dc, bkq=bkq):
                        return lambda t: t.matmul(bank[bkq][:, 0:512],
                                                  wq[:, dc, pt * 128:(pt + 1) * 128],
                                                  xs_bf[:, dc, 1:513],
                                                  start=(dc == 0), stop=(dc == NDC - 1))
                    te(mkq(), ["wqg", "xs_bf"], [BK[bkq]], f"{tg}q{p}_{pt}_{dc}")
                def mksq(pt=pt, bkq=bkq):
                    return lambda s: s.activation(xsq_bf[:, pt, 0:512],
                                                  bank[bkq][:, 0:512], AF.Square)
                se(mksq(), [BK[bkq]], ["xsq_bf"], f"{tg}qsq{p}_{pt}")
                def mkev(pt=pt, bkq=bkq):
                    return lambda s: s.activation(qT[:, pt, :], bank[bkq][:, 0:512],
                                                  AF.Identity)
                se(mkev(), [BK[bkq]], ["qTg"], f"{tg}qev{p}_{pt}")
            for dc in range(4):
                def mks(dc=dc):
                    return lambda t: t.matmul(bank[6][0:8, 0:512], sel16[:, dc, 0:8],
                                              xsq_bf[:, dc, 0:512],
                                              start=(dc == 0), stop=(dc == 3))
                te(mks(), ["sel16", "xsq_bf"], [BK[6]], f"{tg}ql2{p}_{dc}")
            ve(lambda v: v.tensor_scalar(st8b.ap(), bank[6][0:8, 0:512], 1e-16, None,
                                         OP.max),
               [BK[6]], ["st8b"], f"{tg}qlm{p}")
            se(lambda s: s.activation(st8b.ap(), st8b.ap(), AF.Ln), ["st8b"], ["st8b"],
               f"{tg}qll{p}")
            se(lambda s: s.activation(st8b.ap(), st8b.ap(), AF.Exp, scale=-0.5),
               ["st8b"], ["st8b"], f"{tg}qle{p}")
            gdma(r_l8b[0:1, :].rearrange("o (h t) -> o h t", h=8), st8b.ap(),
                 ["st8b"], ["r_l8b"], f"{tg}qlr{p}")
            for pt in range(4):
                bkb = 4 + pt % 2
                h0, h1 = 2 * pt, 2 * pt + 1
                def mkb0(h0=h0, bkb=bkb):
                    return lambda t: t.matmul(bank[bkb][0:64, 0:512], c16row.ap(),
                                              r_l8b[0:1, h0 * 512:(h0 + 1) * 512],
                                              start=True, stop=True)
                te(mkb0(), ["c16row", "r_l8b"], [BK[bkb]], f"{tg}qbc{p}_{h0}")
                def mkb1(h1=h1, bkb=bkb):
                    return lambda t: t.matmul(bank[bkb][64:128, 0:512], c16row.ap(),
                                              r_l8b[0:1, h1 * 512:(h1 + 1) * 512],
                                              start=True, stop=True)
                te(mkb1(), ["c16row", "r_l8b"], [BK[bkb]], f"{tg}qbc{p}_{h1}")
                def mkn(pt=pt, bkb=bkb):
                    return lambda v: v.tensor_tensor(qT[:, pt, :], qT[:, pt, :],
                                                     bank[bkb][:, 0:512], OP.mult)
                ve(mkn(), ["qTg", BK[bkb]], ["qTg"], f"{tg}qn{p}_{pt}")
            for h in range(8):
                rb = (h % 2) * 64
                ptl = h // 2
                pvb = 6 + (h % 2)
                first = [True]
                for kb in range(8):
                    sb_ = 2 + (kb % 2)
                    for kt in range(4):
                        kc0 = kb * 512 + kt * 128
                        def mksim(kc0=kc0, sb_=sb_, rb=rb, ptl=ptl):
                            return lambda t: t.matmul(
                                bank[sb_][0:128, 0:512],
                                kT[rb:rb + 64, ptl, kc0:kc0 + 128],
                                qT[rb:rb + 64, ptl, 0:512],
                                start=True, stop=True)
                        te(mksim(), ["kTg", "qTg"], [BK[sb_]], f"{tg}s{p}_{h}_{kb}_{kt}")
                        def mke(sb_=sb_, kt=kt):
                            return lambda s: s.activation(eT2[:, sb_ - 2, :],
                                                          bank[sb_][:, 0:512], AF.Exp,
                                                          bias=vneg16[:, 0:1])
                        se(mke(), [BK[sb_], "vneg16"], [("eT2", sb_ - 2)],
                           f"{tg}e{p}_{h}_{kb}_{kt}")
                        def mkpv(kb=kb, kt=kt, h=h, pvb=pvb, sb_=sb_, st=first[0]):
                            return lambda t: t.matmul(
                                bank[pvb][0:65, 0:512],
                                v66g[:, kb * 4 + kt, h, 0:65], eT2[:, sb_ - 2, :],
                                start=st, stop=False, skip_group_check=True)
                        te(mkpv(), [("v66g", "d"), ("v66g", "k"), ("eT2", sb_ - 2)],
                           [BK[pvb]], f"{tg}pv{p}_{h}_{kb}_{kt}")
                        first[0] = False
                def mkcl(pvb=pvb):
                    return lambda v: v.tensor_scalar(rec2[64:65, 0:512],
                                                     bank[pvb][64:65, 0:512],
                                                     1e-6, None, OP.max)
                ve(mkcl(), [BK[pvb]], ["rec2"], f"{tg}cl{p}_{h}")
                def mkrl(pvb=pvb):
                    return lambda s: s.activation(rec2[64:65, 0:512],
                                                  rec2[64:65, 0:512], AF.Ln)
                se(mkrl(), ["rec2"], ["rec2"], f"{tg}rl{p}_{h}")
                def mkrc(pvb=pvb):
                    return lambda s: s.activation(recg[64:65, 0:512],
                                                  rec2[64:65, 0:512], AF.Exp,
                                                  scale=-1.0)
                se(mkrc(), ["rec2"], ["recg"], f"{tg}rc{p}_{h}")
                bkb = 4 + h % 2
                def mkbc(bkb=bkb):
                    return lambda t: t.matmul(bank[bkb][0:64, 0:512],
                                              onesrow[64:65, 0:64],
                                              recg[64:65, 0:512], start=True, stop=True)
                te(mkbc(), ["onesrow", "recg"], [BK[bkb]], f"{tg}bcv{p}_{h}")
                def mkevo(pvb=pvb):
                    return lambda s: s.activation(outSbg.ap(), bank[pvb][0:64, 0:512],
                                                  AF.Identity)
                se(mkevo(), [BK[pvb]], ["outSbg"], f"{tg}evo{p}_{h}")
                def mkno(h=h, bkb=bkb):
                    return lambda v: v.tensor_tensor(outNg[:, h, :], outSbg.ap(),
                                                     bank[bkb][0:64, 0:512], OP.mult)
                ve(mkno(), ["outSbg", BK[bkb]], ["outNg"], f"{tg}no{p}_{h}")
            sdma(zf[:, :, 513:1025], x1H[:, t0:t0 + 512].rearrange("(a q) t -> q a t", q=128),
                 hk("x1H", t0, 512) + ["zf"], ["zf"], f"{tg}xr{p}")
            for dt in range(NDC):
                bky = dt % 2
                wpar = dt % 2
                def mkwod(dt=dt, wpar=wpar):
                    return lambda s: s.dma_start(wao_sl[wpar].ap(), baWo[dt])
                P.add("sq", mkwod(), [("wc", "awo")], [("waog", wpar)], dma=True,
                      tag=f"{tg}wod{p}_{dt}")
                for hh in range(8):
                    def mkwo(hh=hh, bky=bky, wpar=wpar):
                        return lambda t: t.matmul(bank[bky][:, 0:512],
                                                  wao_sl[wpar][:, hh, :], outNg[:, hh, :],
                                                  start=(hh == 0), stop=(hh == 7))
                    te(mkwo(), [("waog", wpar), "outNg"], [BK[bky]],
                       f"{tg}wo{p}_{dt}_{hh}")
                def mkres(dt=dt, bky=bky):
                    return lambda v: v.tensor_tensor(zf[:, dt, 513:1025],
                                                     zf[:, dt, 513:1025],
                                                     bank[bky][:, 0:512], OP.add)
                ve(mkres(), ["zf", BK[bky]], ["zf"], f"{tg}res{p}_{dt}")
            sdma(x2H[:, t0:t0 + 512].rearrange("(a q) t -> q a t", q=128),
                 zf[:, :, 513:1025], ["zf"], hk("x2H", t0, 512), f"{tg}xw{p}")
        scB.close()
        sc.close()
        P.barrier()


    if stage >= 2:
        nlay = DEPTH if stage >= 4 else 1
        for li in range(nlay):
            attn_sublayer(li, f"A{li}_")
            if stage >= 3:
                ffn_sublayer(li, f"F{li}_")
    if stage >= 5:
        combine_pass(xT, "xT", x1H, "x1H", "c1_")
    if stage >= 6:
        ln_pass(x1H, "x1H", zH, "zH", "ln1")
        global_attn("G_")
    if stage >= 7:
        ln_pass(x2H, "x2H", zH, "zH", "ln2")
        for li in range(DEPTH):
            attn_sublayer(li, f"B{li}_")
            ffn_sublayer(li, f"C{li}_")
        combine_pass(x2H, "x2H", outT, "outT", "c2_")

    out_keys = []
    if dump in ("x1", "x2"):
        srcH = x1H if dump == "x1" else x2H
        for p in range(SEQ // PH_F):
            gdma(dumpH[:, p * PH_F:(p + 1) * PH_F], srcH[:, p * PH_F:(p + 1) * PH_F],
                 hk(dump + "H", p * PH_F, PH_F), [("dump", p)], f"dump{p}")
            out_keys.append(("dump", p))
    if dump == "z":
        for p in range(SEQ // PH_F):
            gdma(dumpH[:, p * PH_F:(p + 1) * PH_F], zH[:, p * PH_F:(p + 1) * PH_F],
                 hk("zH", p * PH_F, PH_F), [("dump", p)], f"dump{p}")
            out_keys.append(("dump", p))
    if stage >= 7:
        out_keys += [("hbm", "outT", i) for i in range(SEQ // PH_A)]
    else:
        for p in range(SEQ // PH_F):
            gdma(outT[:, p * PH_F:(p + 1) * PH_F], zH[:, p * PH_F:(p + 1) * PH_F],
                 hk("zH", p * PH_F, PH_F), [("out", p)], f"outcp{p}")
            out_keys.append(("out", p))
    P.add("sq", lambda s: None, out_keys, [("fin",)], tag="final_wait")

    P.analyze()
    print("PROG:", P.stats(), flush=True)
    with nc.Block() as block, nc.allow_low_precision(
            reason="bf16 rounding of normalizer rows; fp32 accumulation elsewhere"):
        P.emit_all(block, sems)
    es.close()
    return nc


def prep_inputs(inputs):
    x = np.asarray(inputs["L0_attn_x"], dtype=np.float32)
    Wqkv = np.asarray(inputs["L0_ffn_Wqkv"], dtype=np.float32)
    perm = np.concatenate([h * 64 + PERM64 for h in range(16)])
    fWqk = np.concatenate([Wqkv[:, :, :1024][:, :, perm],
                           Wqkv[:, :, 1024:2048][:, :, perm]], axis=2)
    fWqk = np.ascontiguousarray(
        fWqk.reshape(DEPTH, 8, 128, 16, 128).transpose(0, 3, 2, 1, 4))
    fWv = np.ascontiguousarray(
        Wqkv[:, :, 2048:].reshape(DEPTH, 8, 128, 1024).transpose(0, 2, 1, 3))
    Wo = np.asarray(inputs["L0_ffn_Wo"], dtype=np.float32)
    fWo = np.ascontiguousarray(
        Wo.reshape(DEPTH, 8, 128, 8, 128).transpose(0, 3, 2, 1, 4))
    W1 = np.asarray(inputs["L0_ffn_W1"], dtype=np.float32)
    b1 = np.asarray(inputs["L0_ffn_b1"], dtype=np.float32)
    W2 = np.asarray(inputs["L0_ffn_W2"], dtype=np.float32)
    padc = 128 * KC - 2730
    pad6 = 6 * 512 - 2730
    W1a = np.pad(W1[:, :, :2730], ((0, 0), (0, 0), (0, pad6)))
    W1g = np.pad(W1[:, :, 2730:], ((0, 0), (0, 0), (0, pad6)))
    fW1a = np.ascontiguousarray(
        W1a.reshape(DEPTH, 8, 128, 6, 512).transpose(0, 3, 2, 1, 4))
    fW1g = np.ascontiguousarray(
        W1g.reshape(DEPTH, 8, 128, 6, 512).transpose(0, 3, 2, 1, 4))
    W2p = np.pad(W2, ((0, 0), (0, padc), (0, 0)))
    fW2 = np.ascontiguousarray(
        W2p.reshape(DEPTH, KC, 128, 8, 128).transpose(0, 3, 2, 1, 4))
    aWq_ = np.asarray(inputs["L0_attn_Wq"], dtype=np.float32)
    aWkv_ = np.asarray(inputs["L0_attn_Wkv"], dtype=np.float32)
    aWo_ = np.asarray(inputs["L0_attn_Wo"], dtype=np.float32)
    aWq_t = np.ascontiguousarray(aWq_.reshape(8, 128, 512).transpose(1, 0, 2))
    aWk_t = np.ascontiguousarray(aWkv_[:, 0:512].reshape(8, 128, 512).transpose(1, 0, 2))
    aWv_t = np.ascontiguousarray(aWkv_[:, 512:1024].reshape(8, 128, 512).transpose(1, 0, 2))
    aWo_t = np.ascontiguousarray(aWo_.reshape(8, 64, 8, 128).transpose(2, 1, 0, 3))
    shared = {
        "aWq": aWq_t, "aWk": aWk_t, "aWv": aWv_t, "aWo": aWo_t,
        "fWqk": fWqk, "fWv": fWv, "fWo": fWo,
        "fW1a": fW1a, "fW1g": fW1g, "fW2": fW2,
        "fqs": np.ascontiguousarray(np.asarray(inputs["L0_ffn_q_scale"], np.float32)[:, PERM64]),
        "fks": np.ascontiguousarray(np.asarray(inputs["L0_ffn_k_scale"], np.float32)[:, PERM64]),
        "fg": np.ascontiguousarray(inputs["L0_ffn_ln_g"], dtype=np.float32),
        "fb": np.ascontiguousarray(inputs["L0_ffn_ln_b"], dtype=np.float32),
        "fb1a": np.ascontiguousarray(np.pad(b1[:, :2730], ((0, 0), (0, padc)))),
        "fb1g": np.ascontiguousarray(np.pad(b1[:, 2730:], ((0, 0), (0, padc)))),
        "fb2": np.ascontiguousarray(inputs["L0_ffn_b2"], dtype=np.float32),
    }
    in_maps = []
    for c in range(8):
        m = dict(shared)
        m["xT"] = np.ascontiguousarray(x[c].T)
        in_maps.append(m)
    return in_maps


def kernel(**inputs):
    in_maps = prep_inputs(inputs)
    nc = build_program(stage=99)
    res = run_bass_kernel_spmd(nc, in_maps, core_ids=list(range(8)))
    return np.stack([np.ascontiguousarray(r["outT"].T) for r in res.results])
